# version 05
# baseline (speedup 1.0000x reference)
import os
import numpy as np
import ml_dtypes
import concourse.bass as bass
import concourse.mybir as mybir
from concourse.bass_utils import run_bass_kernel_spmd
from contextlib import ExitStack

F32 = mybir.dt.float32
BF16 = mybir.dt.bfloat16
AF = mybir.ActivationFunctionType
ALU = mybir.AluOpType
AX = mybir.AxisListType

T, C, H, DH = 2048, 1024, 16, 64
NT, NCH = 16, 8
DFF, NDF, NEXP = 2048, 16, 8
SCALE = 0.125
EPS_LN, EPS_RMS = 1e-5, 1e-6
NEG_BIG = -1.0e30
STAGES = int(os.environ.get("MK_STAGES", "4"))
SUB = int(os.environ.get("MK_SUB", "9"))
FSUB = float(os.environ.get("MK_FSUB", "9"))
BF = ml_dtypes.bfloat16


def host_consts():
    iden = np.eye(128, dtype=BF)
    ones128 = np.ones((1, 128), dtype=BF)
    ones512 = np.ones((1, 512), dtype=BF)
    sel0 = np.zeros((1, 128), dtype=BF); sel0[0, :64] = 1
    sel1 = np.zeros((1, 128), dtype=BF); sel1[0, 64:] = 1
    ind2 = np.zeros((128, 2), dtype=BF); ind2[:64, 0] = 1; ind2[64:, 1] = 1
    kk, qq = np.meshgrid(np.arange(128), np.arange(128), indexing="ij")
    diag = np.where(qq >= kk, 0.0, NEG_BIG).astype(np.float32)
    onescol = np.ones((128, 1), dtype=BF)
    return {"c_iden": iden, "c_ones128": ones128, "c_ones512": ones512,
            "c_sel0": sel0, "c_sel1": sel1, "c_ind2": ind2, "c_diag": diag,
            "c_onescol": onescol}


class Planner:

    QUEUES = ("sync", "gp")
    ENGINES = ("pe", "act", "dve", "pool") + QUEUES

    def __init__(self, mksem):
        self.ops = {e: [] for e in self.ENGINES}
        self.cnt = {e: 0 for e in self.ENGINES}
        self.last_wait = {e: {} for e in self.ENGINES}
        self.sems = {}
        self.mksem = mksem

    def sem(self, name):
        if name not in self.sems:
            self.sems[name] = self.mksem(name)
            self.cnt.setdefault(name, 0)
        return self.sems[name]

    def emit(self, eng, fn, inc=False, waits=(), inc_sem=None, step=1):
        lw = self.last_wait[eng]
        eff = []
        for s, v in waits:
            if v <= 0 or lw.get(s, -1) >= v:
                continue
            lw[s] = v
            eff.append((s, v))
            self.sem(s)
        tgt = None
        if inc:
            sname = inc_sem or eng
            self.sem(sname)
            self.cnt[sname] = self.cnt.get(sname, 0) + step
            tgt = self.cnt[sname]
        sems = self.sems

        def run(e, eff=tuple(eff), fn=fn, inc=inc, step=step,
                sname=(inc_sem or eng)):
            for s, v in eff:
                e.wait_ge(sems[s], v)
            ins = fn(e)
            if inc:
                ins.then_inc(sems[sname], step)
        self.ops[eng].append(run)
        return tgt if inc else self.cnt.get(eng, 0)

    def wait_only(self, eng, waits):
        self.emit(eng, lambda e: None, waits=waits)

    def now(self):
        return [(s, c) for s, c in self.cnt.items() if c > 0]


class Trk:
    def __init__(self):
        self.ready = []
        self.free = []

    def write(self, ready):
        self.ready = list(ready)
        self.free = []

    def read(self, free):
        for w in free:
            if w not in self.free:
                self.free.append(w)


def merge(*wls):
    out = []
    for wl in wls:
        for w in wl:
            if w not in out:
                out.append(w)
    return out


def build_program():
    nc = bass.Bass("TRN2", target_bir_lowering=False,
                   detect_race_conditions=(
                       os.environ.get("MK_NORACE", "0") != "1"))
    P = None
    dram = {}

    def din(name, shape, dt=F32):
        dram[name] = nc.dram_tensor(name, shape, dt, kind="ExternalInput").ap()

    din("x", [T, C])
    for nm, shp in [("aa_wq", [C, C]), ("aa_wk", [C, C]), ("aa_wv", [C, C]),
                    ("aa_wo", [C, C]), ("ab_wb", [64, C]), ("ab_wv", [C, C]),
                    ("ab_wo", [C, C]), ("f_wq", [C, C]), ("f_wk", [C, C]),
                    ("f_wv", [C, C]), ("f_wo", [C, C]), ("f_rw", [C, NEXP]),
                    ("f_sg", [C, DFF]), ("f_su", [C, DFF]),
                    ("f_sd", [DFF, C]), ("f_eg", [NEXP, C, DFF]),
                    ("f_eu", [NEXP, C, DFF]), ("f_ed", [NEXP, DFF, C])]:
        din(nm, shp)
    for nm in ["aa_qb", "aa_kb", "aa_vb", "aa_ob", "ab_vb", "ab_ob",
               "f_n1", "f_n2"]:
        din(nm, [C])
    din("c_iden", [128, 128], BF16)
    din("c_ones128", [1, 128], BF16)
    din("c_ones512", [1, 512], BF16)
    din("c_sel0", [1, 128], BF16)
    din("c_sel1", [1, 128], BF16)
    din("c_ind2", [128, 2], BF16)
    din("c_diag", [128, 128], F32)
    din("c_onescol", [128, 1], BF16)
    out_d = nc.dram_tensor("out", [T, C], F32, kind="ExternalOutput").ap()
    dbg_d = nc.dram_tensor("dbg", [T, C], F32, kind="ExternalOutput").ap()
    xsp_d = nc.dram_tensor("xspill", [T, C], F32, kind="Internal").ap()

    bufs = {}

    def trk(name):
        if name not in bufs:
            bufs[name] = Trk()
        return bufs[name]

    def barrier():
        g = P.now()
        for t in bufs.values():
            t.read(g)

    ctx = ExitStack()
    with ctx:
        AR_BYTES = 208128
        AR = ctx.enter_context(nc.sbuf_tensor("AR", [128, AR_BYTES // 4],
                                              F32))
        R_X, R_A, R_B, R_Z = 0, 65536, 131072, 206080

        def view(off_b, elems, dt=F32):
            assert off_b % 4 == 0, off_b
            n32 = elems if dt == F32 else (elems + 1) // 2
            assert off_b // 4 + n32 <= AR_BYTES // 4, (off_b, elems)
            a = AR[:, off_b // 4: off_b // 4 + n32]
            if dt != F32:
                a = a.bitcast(dt)[:, 0:elems]
            return a

        V = {}

        def reg(name, off_b, elems, dt=F32):
            V[name] = view(off_b, elems, dt)
            trk(name)
            return V[name]

        reg("x", R_X, 16384, F32)
        reg("x2", R_A, 16384, F32)
        reg("su", R_A, 32768, BF16)
        reg("h2", R_B, 16384, BF16)
        reg("down", R_B + 32768, 16384, BF16)
        GR = R_B + 65536
        for i in range(2):
            reg(f"gg{i}", GR + i * 2048, 1024, BF16)
            reg(f"ug{i}", GR + 4096 + i * 2048, 1024, BF16)
        reg("m_c", R_Z, 16)
        reg("rstd_c", R_Z + 64, 16)
        reg("sig_c", R_Z + 128, 16)
        reg("s_c", R_Z + 192, 16)
        reg("q_c", R_Z + 256, 16)
        reg("var_c", R_Z + 320, 16)
        reg("tiny", R_Z + 384, 16)
        reg("tiny2", R_Z + 448, 16)
        reg("combine", R_Z + 512, 128)
        reg("mneg", R_Z + 1024, 4)
        reg("nm_b", R_Z + 1040, 8, BF16)
        reg("mask01", R_Z + 1056, 16)
        reg("iden", R_Z + 1120, 128, BF16)
        reg("ind2", R_Z + 1376, 2, BF16)
        reg("onescol", R_Z + 1380, 1, BF16)
        reg("diag", R_Z + 1384, 128, F32)
        reg("qbc", R_Z + 1896, 8, F32)
        reg("kbc", R_Z + 1928, 8, F32)
        reg("tcol", R_Z + 1960, 16, F32)
        reg("eps_c", R_Z + 2024, 2, F32)
        assert R_Z + 2032 <= AR_BYTES

        ROWS = ctx.enter_context(nc.sbuf_tensor("ROWS", [1, 2048], BF16))
        trk("rows")
        trk("rowscr")

        def row(off, n):
            return ROWS[0:1, off:off + n]

        pb = [ctx.enter_context(nc.psum_tensor(f"pb{i}", [128, 512], F32))
              for i in range(8)]
        for i in range(8):
            trk(f"pb{i}")

        def bk(i, lo=0, n=512, dt=F32):
            if dt == F32:
                return pb[i][:, lo:lo + n]
            return pb[i][:].bitcast(dt)[:, lo:lo + n]

        P = Planner(lambda name: ctx.enter_context(
            nc.semaphore(f"sm_{name}")))
        for s in ("pe", "act", "dve", "pool"):
            P.sem(s)
        emit = P.emit

        def dma(queue, name, dst, src, waits=(), sem=None):
            t = trk(name)
            sname = sem or f"q_{name}"
            v = emit(queue, lambda e, d=dst, s=src: e.dma_start(out=d, in_=s),
                     inc=True, waits=merge(t.free, waits),
                     inc_sem=sname, step=16)
            t.write([(sname, v)])
            return v

        def rdy(*names):
            return merge(*[trk(n).ready for n in names])

        def mm_group(bnum, bank_ap, parts, extra_waits=(), inc=True,
                     reads=()):
            t = trk(f"pb{bnum}")
            n = len(parts)
            v = None
            for i, (l, r) in enumerate(parts):
                w = merge(t.free, extra_waits) if i == 0 else ()
                v = emit("pe", lambda e, l=l, r=r, i=i, n=n, b=bank_ap:
                         e.matmul(b, l, r, start=(i == 0), stop=(i == n - 1)),
                         inc=(inc and i == n - 1), waits=w)
            t.write([("pe", v)])
            for nm in reads:
                trk(nm).read([("pe", v)])
            return v

        dma("sync", "x", V["x"].rearrange("p (t c) -> p t c", c=C),
            dram["x"].rearrange("(t p) c -> p t c", p=128))
        dma("sync", "iden", V["iden"], dram["c_iden"])
        dma("sync", "ind2", V["ind2"], dram["c_ind2"])
        dma("sync", "diag", V["diag"], dram["c_diag"])
        dma("sync", "onescol", V["onescol"], dram["c_onescol"])
        for dst_, srcn_ in ((row(0, 128), "c_ones128"),
                            (row(128, 512), "c_ones512"),
                            (row(640, 128), "c_sel0"),
                            (row(768, 128), "c_sel1")):
            v0 = emit("sync", lambda e, d=dst_, s=srcn_: e.dma_start(
                out=d, in_=dram[s]), inc=True, inc_sem="q_rows", step=16)
        trk("rows").write([("q_rows", v0)])
        ev_ = emit("dve", lambda e: e.memset(V["eps_c"][:, 0:1], EPS_LN),
                   inc=True)
        ev_ = emit("dve", lambda e: e.memset(V["eps_c"][:, 1:2], EPS_RMS),
                   inc=True)
        trk("eps_c").write([("dve", ev_)])

        def col(name, tt):
            return V[name][:, tt:tt + 1]

        def xt(src, tt):
            return V[src][:, tt * C:(tt + 1) * C]

        def emit_ln_stats(src, scr):
            rd = rdy(src)
            st = trk("lnscr")
            sv, qv = {}, {}
            for tt in range(NT):
                sv[tt] = emit("dve", lambda e, tt=tt: e.reduce_sum(
                    col("s_c", tt), xt(src, tt), axis=AX.X), inc=True,
                    waits=rd)
            for tt in range(NT):
                qv[tt] = emit("act", lambda e, tt=tt, s=scr: e.activation(
                    s, xt(src, tt), AF.Square, accum_out=col("q_c", tt)),
                    inc=True, waits=merge(rd, st.free if tt == 0 else ()))
            st.write([("act", qv[NT - 1])])
            st.read([("act", qv[NT - 1])])
            for tt in range(NT):
                emit("dve", lambda e, tt=tt: e.tensor_scalar(
                    col("m_c", tt), col("s_c", tt), 1.0 / C, None, ALU.mult),
                    inc=True, waits=[("dve", sv[tt])])
            mv = P.cnt["dve"]
            mm = {}
            for tt in range(NT):
                mm[tt] = emit("dve", lambda e, tt=tt: e.tensor_tensor(
                    col("tiny", tt), col("m_c", tt), col("m_c", tt),
                    ALU.mult), inc=True, waits=[("dve", mv)])
            for tt in range(NT):
                emit("dve", lambda e, tt=tt: e.scalar_tensor_tensor(
                    col("var_c", tt), col("q_c", tt), 1.0 / C,
                    col("tiny", tt), ALU.mult, ALU.subtract), inc=True,
                    waits=[("dve", mm[tt]), ("act", qv[tt])])
            vv = P.cnt["dve"]
            for tt in range(NT):
                emit("act", lambda e, tt=tt: e.activation(
                    col("tiny2", tt), col("var_c", tt), AF.Ln,
                    bias=V["eps_c"][:, 0:1]), inc=True,
                    waits=merge([("dve", vv)], rdy("eps_c")))
            lv = P.cnt["act"]
            rv = None
            for tt in range(NT):
                rv = emit("act", lambda e, tt=tt: e.activation(
                    col("rstd_c", tt), col("tiny2", tt), AF.Exp, scale=-0.5),
                    inc=True, waits=[("act", lv)])
            trk("stats").write([("act", rv), ("dve", mv)])

        def emit_rms_stats(src, scr, eps, dstcol="sig_c", tile_deps=None):
            rd = rdy(src) if tile_deps is None else ()
            st = trk("lnscr")
            qv = {}
            for tt in range(NT):
                w = merge(rd, [("dve", tile_deps[tt])] if tile_deps else (),
                          st.free if tt == 0 else ())
                qv[tt] = emit("act", lambda e, tt=tt, s=scr, d=dstcol:
                              e.activation(s, xt(src, tt), AF.Square,
                                           accum_out=col(d, tt)),
                              inc=True, waits=w)
            st.write([("act", qv[NT - 1])])
            st.read([("act", qv[NT - 1])])
            for tt in range(NT):
                emit("dve", lambda e, tt=tt, d=dstcol: e.tensor_scalar(
                    col("tiny", tt), col(d, tt), 1.0 / C, None, ALU.mult),
                    inc=True, waits=[("act", qv[tt])])
            sv = P.cnt["dve"]
            for tt in range(NT):
                emit("act", lambda e, tt=tt: e.activation(
                    col("tiny2", tt), col("tiny", tt), AF.Ln,
                    bias=V["eps_c"][:, 1:2]), inc=True,
                    waits=merge([("dve", sv)], rdy("eps_c")))
            lv = P.cnt["act"]
            rv = None
            for tt in range(NT):
                rv = emit("act", lambda e, tt=tt, d=dstcol: e.activation(
                    col(d, tt), col("tiny2", tt), AF.Exp, scale=-0.5),
                    inc=True, waits=[("act", lv)])
            trk("stats").write([("act", rv)])

        ffn_x2a_cell = [None]
        def emit_dense_attn(kind):
            is_aa = kind == "aa"
            src = tgt = "x" if is_aa else "x2"
            CA = R_A if is_aa else R_X
            O_K, O_Q, O_V = CA, CA + 16384, CA + 32768
            O_HF, O_AT, O_HT = CA + 49408, CA + 57600, CA + 61696
            O_W3, O_YF = R_B, R_B + 24576
            O_STG, O_LN, O_W1 = R_B + 57344, R_B + 61440, GR
            kq = reg("K", O_K, 8192, BF16)
            qq = reg("Q", O_Q, 8192, BF16)
            vh = reg("V", O_V, 8320, BF16)
            hfm = reg("hfm", O_HF, 4096, BF16)
            att = [reg(f"att{i}", O_AT + i * 1024, 512, BF16)
                   for i in range(4)]
            htm = reg("htm", O_HT, 1024, BF16)
            htm2 = reg("htm2", O_AT, 1024, BF16)
            w3 = [reg(n, O_W3 + i * 8192, 4096, BF16)
                  for i, n in enumerate(("wq3", "wk3", "wv3"))]
            wof = reg("wof", O_W3, 8192, BF16)
            yf = reg("yfull", O_YF, 16384, BF16)
            stg = reg("stg", O_STG, 1024, BF16)
            stg2 = reg("stg2", O_STG + 2048, 1024, BF16)
            qsq = reg("qsq", O_AT, 2048, BF16)
            lnscr = reg("lnscr", O_LN, 1024, F32)
            w1b = reg("w1b", O_W1, 1024, F32)
            yfT = reg("yfT", O_AT, 1024, BF16)
            trk("w1b").free = merge(trk("w1b").free, trk("gg0").free,
                                    trk("gg1").free)
            if not is_aa and FSUB <= 0.2:
                return
            if is_aa:
                emit_ln_stats("x", lnscr)
                rd = rdy("x")
                for tt in range(NT):
                    a = emit("act", lambda e, tt=tt: e.activation(
                        col("tiny", tt), V["x"][:, tt * C: tt * C + 1],
                        AF.Sign), inc=True, waits=rd)
                    emit("act", lambda e, tt=tt: e.activation(
                        col("mask01", tt), col("tiny", tt), AF.Square),
                        inc=True, waits=[("act", a)])
                trk("mask01").write([("act", P.cnt["act"])])
                for bn_, dn_ in (("qbc", "aa_qb"), ("kbc", "aa_kb")):
                    t_ = trk(bn_)
                    v_ = emit("sync", lambda e, bn_=bn_, dn_=dn_:
                              e.dma_start(
                                  out=V[bn_],
                                  in_=dram[dn_].rearrange(
                                      "(c p) -> p c", p=128),
                                  allow_slow_non_contiguous=True),
                              inc=True, waits=t_.free,
                              inc_sem=f"q_{bn_}", step=16)
                    t_.write([(f"q_{bn_}", v_)])
            else:
                emit_rms_stats("x2", lnscr, EPS_RMS,
                               tile_deps=ffn_x2a_cell[0])
                dma("sync", "w1b", w1b,
                    dram["f_n1"].rearrange("(a c) -> a c", a=1)
                    .broadcast_to([128, C]))

            wnm = (("aa_wq", "aa_wk", "aa_wv", "aa_wo") if is_aa else
                   ("f_wq", "f_wk", "f_wv", "f_wo"))

            def stage_transpose(wsrc_rows, dst_name, dst3, gc0):
                dt_ = trk(dst_name)
                done = []
                for ci in range(4):
                    sname = "stg" if ci % 2 == 0 else "stg2"
                    sb = V[sname]
                    dma("gp", sname, sb,
                        dram[wsrc_rows][gc0 + ci * 128:
                                        gc0 + (ci + 1) * 128, :])
                    bt = trk("pb0")
                    v = None
                    for j in range(NCH):
                        w = merge(bt.free, rdy(sname), rdy("iden")) \
                            if j == 0 else ()
                        v = emit("pe", lambda e, j=j, sb=sb: e.transpose(
                            bk(0, j * 128, 128, BF16),
                            sb[:, j * 128:(j + 1) * 128], V["iden"]),
                            inc=(j == NCH - 1), waits=w)
                    bt.write([("pe", v)])
                    trk(sname).read([("pe", v)])
                    ev = emit("act", lambda e, ci=ci, dst3=dst3:
                              e.activation(dst3(ci),
                                           bk(0, 0, 1024, BF16), AF.Copy),
                              inc=True, waits=merge(bt.ready, dt_.free))
                    bt.read([("act", ev)])
                    done.append(("act", ev))
                dt_.write(done)

            def stage_grp(g):
                gc0 = g * 512
                if is_aa:
                    for wi, dn in enumerate(("wq3", "wk3", "wv3")):
                        wv3 = w3[wi]

                        def mk(wv3=wv3):
                            return lambda ci: wv3.rearrange(
                                "p (j n) -> p j n", n=512)[
                                    :, :, ci * 128:(ci + 1) * 128]
                        stage_transpose(wnm[wi], dn, mk(), gc0)
                    dma("gp", "rowscr", row(896, 512),
                        dram["aa_vb"].rearrange("(a c) -> a c", a=1)[
                            0:1, gc0:gc0 + 512])
                else:
                    for wi, dn in enumerate(("wq3", "wk3", "wv3")):
                        dma("gp", dn,
                            w3[wi].rearrange("p (c n) -> p c n", n=512),
                            dram[wnm[wi]][:, gc0:gc0 + 512].rearrange(
                                "(c p) n -> p c n", p=128))

            def build_wof():
                trk("wof").free = merge(trk("wof").free, trk("wq3").free,
                                        trk("wk3").free, trk("wv3").free)
                if is_aa:
                    for hlf in range(2):
                        def mk(hlf=hlf):
                            return lambda ci: wof.rearrange(
                                "p (c n) -> p c n", n=1024)[
                                    :, :, hlf * 512 + ci * 128:
                                    hlf * 512 + (ci + 1) * 128]
                        stage_transpose(wnm[3], "wof", mk(), hlf * 512)
                    dma("gp", "rowscr", row(896, 1024),
                        dram["aa_ob"].rearrange("(a c) -> a c", a=1))
                else:
                    dma("gp", "wof",
                        wof.rearrange("p (c n) -> p c n", n=C),
                        dram[wnm[3]].rearrange("(c p) n -> p c n", p=128))

            def phase_a(g):
                gc0 = g * 512
                stats_w = rdy("stats")
                trk("htm2").read(merge(trk("att0").free, trk("att1").free,
                                       trk("qsq").free, trk("yfT").free))
                for tg in range(4):
                    for j in range(4):
                        tt = tg * 4 + j
                        hb = (htm, htm2)[j % 2]
                        hn = ("htm", "htm2")[j % 2]
                        ht = trk(hn)
                        if is_aa:
                            hv = emit("dve", lambda e, tt=tt, hb=hb:
                                      e.tensor_scalar(
                                          hb, xt("x", tt), col("m_c", tt),
                                          col("rstd_c", tt), ALU.subtract,
                                          ALU.mult), inc=True,
                                      waits=merge(stats_w, rdy("x"),
                                                  ht.free))
                        else:
                            hv = emit("dve", lambda e, tt=tt, hb=hb,
                                      w1b=w1b: e.scalar_tensor_tensor(
                                          hb, xt("x2", tt),
                                          col("sig_c", tt), w1b, ALU.mult,
                                          ALU.mult), inc=True,
                                      waits=merge(stats_w,
                                                  rdy("x2", "w1b"),
                                                  ht.free))
                            trk("w1b").read([("dve", hv)])
                        ht.write([("dve", hv)])
                        tb = (1, 0)[j % 2]
                        bt = trk(f"pb{tb}")
                        v = None
                        for ch in range(NCH):
                            w = merge(bt.free, [("dve", hv)], rdy("iden")) \
                                if ch == 0 else ()
                            v = emit("pe", lambda e, ch=ch, hb=hb, tb=tb:
                                     e.transpose(
                                         bk(tb, ch * 128, 128, BF16),
                                         hb[:, ch * 128:(ch + 1) * 128],
                                         V["iden"]),
                                     inc=(ch == NCH - 1), waits=w)
                        bt.write([("pe", v)])
                        ht.read([("pe", v)])
                        hf3 = hfm.rearrange("p (c n) -> p c n", n=512)
                        ev = emit("dve", lambda e, j=j, hf3=hf3, tb=tb:
                                  e.tensor_copy(
                                      hf3[:, :, j * 128:(j + 1) * 128],
                                      bk(tb, 0, 1024, BF16).rearrange(
                                          "p (c n) -> p c n", n=128)),
                                  inc=True,
                                  waits=merge(bt.ready, trk("hfm").free
                                              if j == 0 else ()))
                        bt.read([("dve", ev)])
                    hrd = [("dve", P.cnt["dve"])]
                    trk("hfm").write(hrd)
                    for which, dst, wv3, wname, bco in (
                            ("Q", qq, w3[0], "wq3", "qbc"),
                            ("K", kq, w3[1], "wk3", "kbc")):
                        for dch in range(4):
                            bn = 2 + (dch % 2)
                            parts = [(wv3[:, ci * 512 + dch * 128:
                                          ci * 512 + (dch + 1) * 128],
                                      hfm[:, ci * 512:(ci + 1) * 512])
                                     for ci in range(NCH)]
                            mm_group(bn, bk(bn), parts,
                                     extra_waits=merge(hrd, rdy(wname)),
                                     reads=("hfm", wname))
                            bt2 = trk(f"pb{bn}")
                            dsl = dst[:, dch * 2048 + tg * 512:
                                      dch * 2048 + (tg + 1) * 512]
                            if is_aa:
                                bc = col(bco, g * 4 + dch)
                                ev = emit("dve", lambda e, dsl=dsl, bn=bn,
                                          bc=bc: e.tensor_scalar(
                                              dsl, bk(bn), bc, None,
                                              ALU.add), inc=True,
                                          waits=merge(bt2.ready, rdy(bco),
                                                      trk(which).free))
                                bt2.read([("dve", ev)])
                                trk(which).write([("dve", ev)])
                            else:
                                ev = emit("dve", lambda e, dsl=dsl, bn=bn:
                                          e.tensor_copy(dsl, bk(bn)),
                                          inc=True,
                                          waits=merge(bt2.ready,
                                                      trk(which).free))
                                bt2.read([("dve", ev)])
                                trk(which).write([("dve", ev)])
                    for j in range(4):
                        tt = tg * 4 + j
                        bn = 4 + (j % 2)
                        parts = [(hfm[:, ci * 512 + j * 128:
                                      ci * 512 + (j + 1) * 128],
                                  w3[2][:, ci * 512:(ci + 1) * 512])
                                 for ci in range(NCH)]
                        if is_aa:
                            parts.append((row(0, 128), row(896, 512)))
                        mm_group(bn, bk(bn), parts,
                                 extra_waits=merge(
                                     hrd, rdy("wv3"), rdy("rows"),
                                     rdy("rowscr") if is_aa else ()),
                                 reads=(("hfm", "wv3", "rowscr") if is_aa
                                        else ("hfm", "wv3")))
                        bt2 = trk(f"pb{bn}")
                        vdst = vh[:, tt * 520: tt * 520 + 520].rearrange(
                            "p (h g) -> p h g", g=65)
                        bsrc = bk(bn).rearrange("p (h d) -> p h d", d=64)
                        if is_aa:
                            ev = emit("dve", lambda e, vdst=vdst,
                                      bsrc=bsrc, tt=tt: e.tensor_scalar(
                                          vdst[:, :, 0:64], bsrc,
                                          col("mask01", tt), None,
                                          ALU.mult), inc=True,
                                      waits=merge(bt2.ready,
                                                  rdy("mask01"),
                                                  trk("V").free))
                        else:
                            ev = emit("dve", lambda e, vdst=vdst,
                                      bsrc=bsrc: e.tensor_copy(
                                          vdst[:, :, 0:64], bsrc),
                                      inc=True,
                                      waits=merge(bt2.ready,
                                                  trk("V").free))
                        bt2.read([("dve", ev)])
                        scol = (col("mask01", tt) if is_aa
                                else V["onescol"])
                        dv = emit("dve", lambda e, vdst=vdst, s=scol:
                                  e.tensor_copy(
                                      vdst[:, :, 64:65].rearrange(
                                          "p h o -> p (h o)"),
                                      s.broadcast_to([128, 8])), inc=True,
                                  waits=merge([("dve", ev)],
                                              rdy("mask01") if is_aa
                                              else rdy("onescol")))
                        trk("V").write([("dve", dv)])
                trk("qsq").read(merge(trk("att0").free, trk("att1").free,
                                      trk("htm2").free, trk("yfT").free))
                vlast = None
                for side, srcb, sname in (("q", qq, "Q"), ("k", kq, "K")):
                    for dch in range(4):
                        a = emit("act", lambda e, srcb=srcb, dch=dch,
                                 qsq=qsq: e.activation(
                                     qsq, srcb[:, dch * 2048:
                                               (dch + 1) * 2048],
                                     AF.Square), inc=True,
                                 waits=merge(rdy(sname),
                                             trk("qsq").free))
                        trk("qsq").write([("act", a)])
                        for tt in range(NT):
                            ci = (0 if side == "q" else 128) + \
                                (dch * 16 + tt) * 2
                            w = merge(trk("pb6").free, [("act", a)],
                                      rdy("ind2")) if tt == 0 else ()
                            vlast = emit("pe", lambda e, tt=tt, ci=ci,
                                         qsq=qsq: e.matmul(
                                             bk(6, ci, 2),
                                             qsq[:, tt * 128:
                                                 (tt + 1) * 128],
                                             V["ind2"], start=True,
                                             stop=True),
                                         inc=(tt == NT - 1), waits=w)
                        trk("qsq").read([("pe", vlast)])
                trk("pb6").write([("pe", vlast)])
                _emit_bound_finalize(256, g)

            def phase_b(g):
                for i in range(4):
                    trk(f"att{i}").read(merge(trk("qsq").free,
                                              trk("htm2").free,
                                              trk("yfT").free))
                tcol_used = {}
                STA = (0, 1, 2)
                STB = (3, 4, 7)
                for qc in range(4):
                    for hp in range(4):
                        heads = (2 * hp, 2 * hp + 1)
                        attv_last = []

                        def emit_st(kt):
                            out = []
                            for hi, hh in enumerate(heads):
                                base = (hh % 2) * 64
                                dch = hh // 2
                                bn = (STA, STB)[hi][kt % 3]
                                bt = trk(f"pb{bn}")
                                stv = emit("pe", lambda e, dch=dch, bn=bn,
                                           base=base, kt=kt, qc=qc:
                                           e.matmul(
                                               bk(bn),
                                               kq[base:base + 64,
                                                  dch * 2048 + kt * 128:
                                                  dch * 2048 + (kt + 1)
                                                  * 128],
                                               qq[base:base + 64,
                                                  dch * 2048 + qc * 512:
                                                  dch * 2048 + (qc + 1)
                                                  * 512],
                                               start=True, stop=True),
                                           inc=True,
                                           waits=merge(bt.free,
                                                       rdy("K", "Q")))
                                bt.write([("pe", stv)])
                                out.append(bn)
                            return out

                        st_bn = emit_st(0)
                        for kt in range(NT):
                            evs = []
                            for hi, hh in enumerate(heads):
                                bn = st_bn[hi]
                                bt = trk(f"pb{bn}")
                                an = 2 * hi + (kt % 2)
                                abt = trk(f"att{an}")
                                ev = emit("act", lambda e, an=an, bn=bn,
                                          g=g: e.activation(
                                              att[an], bk(bn), AF.Exp,
                                              bias=col("mneg", g),
                                              scale=SCALE), inc=True,
                                          waits=merge(bt.ready, abt.free,
                                                      rdy("mneg")))
                                bt.read([("act", ev)])
                                abt.write([("act", ev)])
                                evs.append(an)
                            if kt + 1 < NT:
                                st_bn = emit_st(kt + 1)
                            for hi, hh in enumerate(heads):
                                an = evs[hi]
                                abt = trk(f"att{an}")
                                for qt in range(4):
                                    yb = trk(f"pb{5 + hi}")
                                    w = merge(abt.ready, rdy("V"),
                                              yb.free if (kt == 0 and
                                                          qt == 0) else ())
                                    av = emit("pe", lambda e, an=an, kt=kt,
                                              qt=qt, hh=hh, hi=hi:
                                              e.matmul(
                                                  bk(5 + hi, qt * 65, 65),
                                                  att[an][:, qt * 128:
                                                          (qt + 1) * 128],
                                                  vh[:, kt * 520 + hh * 65:
                                                     kt * 520 + (hh + 1)
                                                     * 65],
                                                  start=(kt == 0 and
                                                         qt == 0),
                                                  stop=(kt == NT - 1),
                                                  skip_group_check=True),
                                              inc=(kt == NT - 1), waits=w)
                                    if kt == NT - 1:
                                        attv_last.append(av)
                                    abt.read([("pe", av)])
                        for hi in range(2):
                            trk(f"pb{5 + hi}").write(
                                [("pe", attv_last[-1])])
                        for hi, hh in enumerate(heads):
                            yb = trk(f"pb{5 + hi}")
                            for qt in range(4):
                                tc = 8 + hi * 4 + qt
                                wt = merge(yb.ready,
                                           [("dve", tcol_used[tc])]
                                           if tc in tcol_used else ())
                                rv_ = emit("dve", lambda e, qt=qt, hi=hi,
                                           tc=tc: e.reciprocal(
                                               V["tcol"][:, tc:tc + 1],
                                               bk(5 + hi, qt * 65 + 64,
                                                  1)), inc=True, waits=wt)
                                yb.read([("dve", rv_)])
                                hgl = g * 8 + hh
                                gt = qc * 4 + qt
                                ev = emit("dve", lambda e, qt=qt, hi=hi,
                                          hgl=hgl, gt=gt, tc=tc:
                                          e.tensor_scalar(
                                              yf[:, gt * 1024 + hgl * 64:
                                                 gt * 1024 + (hgl + 1)
                                                 * 64],
                                              bk(5 + hi, qt * 65, 64),
                                              V["tcol"][:, tc:tc + 1],
                                              None, ALU.mult), inc=True,
                                          waits=merge([("dve", rv_)],
                                                      trk("yfull").free))
                                yb.read([("dve", ev)])
                                tcol_used[tc] = ev
                        trk("yfull").ready = [("dve", P.cnt["dve"])]
                pev = [("pe", P.cnt["pe"])]
                for nm in ("Q", "K", "V", "wq3", "wk3", "wv3"):
                    trk(nm).read(pev)

            stage_grp(0)
            phase_a(0)
            stage_grp(1)
            if not is_aa and FSUB <= 1:
                return
            phase_b(0)
            phase_a(1)
            build_wof()
            if not is_aa and FSUB <= 2:
                return
            phase_b(1)
            if not is_aa and FSUB <= 3:
                return
            trk("yfT").read(merge(trk("att0").free, trk("att1").free,
                                  trk("att2").free, trk("att3").free,
                                  trk("qsq").free, trk("htm2").free))
            xadds = []
            for tt in range(NT):
                tb = (1, 0)[tt % 2]
                bt = trk(f"pb{tb}")
                v = None
                for ch in range(NCH):
                    w = merge(bt.free, trk("yfull").ready, rdy("iden")) \
                        if ch == 0 else ()
                    v = emit("pe", lambda e, tt=tt, ch=ch, tb=tb:
                             e.transpose(
                                 bk(tb, ch * 128, 128, BF16),
                                 yf[:, tt * 1024 + ch * 128:
                                    tt * 1024 + (ch + 1) * 128],
                                 V["iden"]), inc=(ch == NCH - 1), waits=w)
                bt.write([("pe", v)])
                trk("yfull").read([("pe", v)])
                ev = emit("act", lambda e, yfT=yfT, tb=tb: e.activation(
                    yfT, bk(tb, 0, 1024, BF16), AF.Copy), inc=True,
                    waits=merge(bt.ready, trk("yfT").free))
                bt.read([("act", ev)])
                trk("yfT").write([("act", ev)])
                for chh in range(2):
                    bn = 2 + chh
                    parts = [(yfT[:, cch * 128:(cch + 1) * 128],
                              wof[:, cch * 1024 + chh * 512:
                                  cch * 1024 + (chh + 1) * 512])
                             for cch in range(NCH)]
                    if is_aa:
                        parts.append((row(0, 128),
                                      row(896 + chh * 512, 512)))
                    mm_group(bn, bk(bn), parts,
                             extra_waits=merge(trk("yfT").ready,
                                               rdy("wof"), rdy("rows"),
                                               rdy("rowscr") if is_aa
                                               else ()),
                             reads=(("yfT", "wof", "rowscr") if is_aa
                                    else ("yfT", "wof")))
                    bt2 = trk(f"pb{bn}")
                    xv = emit("dve", lambda e, tt=tt, chh=chh, bn=bn:
                              e.tensor_tensor(
                                  V[tgt][:, tt * C + chh * 512:
                                         tt * C + (chh + 1) * 512],
                                  bk(bn),
                                  V[tgt][:, tt * C + chh * 512:
                                         tt * C + (chh + 1) * 512],
                                  ALU.add), inc=True,
                              waits=merge(bt2.ready, rdy(tgt)))
                    bt2.read([("dve", xv)])
                    xadds.append(xv)
                trk("yfT").read([("pe", P.cnt["pe"])])
            trk(tgt).write([("dve", xadds[-1])])

        def emit_expert(wg, wu, wd, e, cw_col):
            def esl(nm, df=None):
                a = dram[nm]
                if e is not None:
                    a = a[e]
                return a

            dwr = esl(wd).rearrange("(c p) n -> p c n", p=128)
            dview = V["down"].rearrange("p (c n) -> p c n", n=C)
            down_prev_free = list(trk("down").free)
            for df in range(NDF):
                gN, uN = f"gg{df % 2}", f"ug{df % 2}"
                for nm, wsrc in ((gN, wg), (uN, wu)):
                    src = esl(wsrc)[:, df * 128:(df + 1) * 128].rearrange(
                        "(c p) n -> p c n", p=128)
                    dma("gp", nm, V[nm].rearrange("p (c n) -> p c n", n=128),
                        src)
                if 2 <= df < 6:
                    q = df - 2
                    dma("gp", "down", dview[:, q * 4:(q + 1) * 4, :],
                        dwr[:, q * 4:(q + 1) * 4, :],
                        sem="q_down", waits=down_prev_free)
                for tc in range(4):
                    gb, ub = (tc % 2), 2 + (tc % 2)
                    parts = [(V[gN][:, ci * 128:(ci + 1) * 128],
                              V["h2"][:, ci * 2048 + tc * 512:
                                      ci * 2048 + (tc + 1) * 512])
                             for ci in range(NCH)]
                    mm_group(gb, bk(gb), parts,
                             extra_waits=merge(rdy(gN), rdy("h2")),
                             reads=(gN,))
                    parts = [(V[uN][:, ci * 128:(ci + 1) * 128],
                              V["h2"][:, ci * 2048 + tc * 512:
                                      ci * 2048 + (tc + 1) * 512])
                             for ci in range(NCH)]
                    mm_group(ub, bk(ub), parts,
                             extra_waits=merge(rdy(uN), rdy("h2")),
                             reads=(uN,))
                    ssl = V["su"][:, df * 2048 + tc * 512:
                                  df * 2048 + (tc + 1) * 512]
                    gt, ut = trk(f"pb{gb}"), trk(f"pb{ub}")
                    sv = emit("act", lambda e_, ssl=ssl, gb=gb: e_.activation(
                        ssl, bk(gb), AF.Silu), inc=True,
                        waits=merge(gt.ready, trk("su").free))
                    gt.read([("act", sv)])
                    mv = emit("dve", lambda e_, ssl=ssl, ub=ub:
                              e_.tensor_tensor(ssl, bk(ub), ssl, ALU.mult),
                              inc=True, waits=merge(ut.ready, [("act", sv)]))
                    ut.read([("dve", mv)])
                    trk("su").write([("dve", mv)])
            surd = [("dve", P.cnt["dve"])]
            for tt in range(NT):
                for ch in range(2):
                    bn = 4 + (2 * tt + ch) % 4
                    parts = [(V["su"][:, df * 2048 + tt * 128:
                                      df * 2048 + (tt + 1) * 128],
                              V["down"][:, df * 1024 + ch * 512:
                                        df * 1024 + (ch + 1) * 512])
                             for df in range(NDF)]
                    mm_group(bn, bk(bn), parts,
                             extra_waits=merge(surd, rdy("down")),
                             reads=("su", "down"))
                    bt2 = trk(f"pb{bn}")
                    xsl = V["x"][:, tt * C + ch * 512: tt * C + (ch + 1)
                                 * 512]
                    cw = cw_col(tt)
                    if cw is None:
                        xv = emit("dve", lambda e_, xsl=xsl, bn=bn:
                                  e_.tensor_tensor(xsl, bk(bn), xsl,
                                                   ALU.add), inc=True,
                                  waits=merge(bt2.ready, rdy("x")))
                    else:
                        xv = emit("dve", lambda e_, xsl=xsl, bn=bn, cw=cw:
                                  e_.scalar_tensor_tensor(
                                      xsl, bk(bn), cw, xsl, ALU.mult,
                                      ALU.add), inc=True,
                                  waits=merge(bt2.ready, rdy("x"),
                                              rdy("combine")))
                    bt2.read([("dve", xv)])
            trk("x").write([("dve", P.cnt["dve"])])

        def emit_ffn():
            lnscr2 = reg("lnscr2", R_B + 61440, 1024, F32)
            emit_ln_stats("x", lnscr2)
            x2a = {}
            for tt in range(NT):
                x2a[tt] = emit("dve", lambda e, tt=tt: e.tensor_scalar(
                    xt("x2", tt), xt("x", tt), col("m_c", tt),
                    col("rstd_c", tt), ALU.subtract, ALU.mult), inc=True,
                    waits=merge(rdy("stats"), rdy("x"), trk("x2").free))
            trk("x2").write([("dve", x2a[NT - 1])])
            ffn_x2a_cell[0] = dict(x2a)
            if SUB <= 1:
                return
            if os.environ.get("MK_NOSPILL", "0") != "1":
                spv = emit("sync", lambda e: e.dma_start(
                    out=xsp_d.rearrange("(t p) c -> p t c", p=128),
                    in_=V["x"].rearrange("p (t c) -> p t c", c=C)),
                    inc=True, waits=rdy("x"), inc_sem="q_xsp", step=16)
                trk("xsp").write([("q_xsp", spv)])
            else:
                spv = None
            xfree = merge([("q_xsp", spv)] if spv else [],
                          [("dve", x2a[NT - 1])])
            for n in ("K", "Q", "V", "hfm", "htm", "htm2", "att0", "att1",
                      "att2", "att3", "qsq", "yfT"):
                trk(n).read(xfree)
            emit_dense_attn("fa")
            if SUB <= 2:
                return
            trk("h2").read(merge(trk("wq3").free, trk("wk3").free,
                                 trk("wv3").free, trk("wof").free,
                                 trk("yfull").free, trk("stg").free,
                                 trk("lnscr").free))
            emit_rms_stats("x2", lnscr2, EPS_RMS)
            w2b_ld = view(GR, 1024, F32)
            dma("sync", "w1b", w2b_ld,
                dram["f_n2"].rearrange("(a c) -> a c", a=1)
                .broadcast_to([128, C]))
            h2ev = None
            htm2 = view(R_X + 61696, 1024, BF16)
            w2b = view(GR, 1024, F32)
            for tt in range(NT):
                ht = trk("htm")
                hv = emit("dve", lambda e, tt=tt: e.scalar_tensor_tensor(
                    htm2, xt("x2", tt), col("sig_c", tt), w2b,
                    ALU.mult, ALU.mult), inc=True,
                    waits=merge(rdy("stats"), rdy("x2"), rdy("w1b"),
                                ht.free))
                trk("w1b").read([("dve", hv)])
                ht.write([("dve", hv)])
                bt = trk("pb1")
                v = None
                for ch in range(NCH):
                    w = merge(bt.free, [("dve", hv)], rdy("iden")) \
                        if ch == 0 else ()
                    v = emit("pe", lambda e, ch=ch: e.transpose(
                        bk(1, ch * 128, 128, BF16),
                        htm2[:, ch * 128:(ch + 1) * 128],
                        V["iden"]), inc=(ch == NCH - 1), waits=w)
                bt.write([("pe", v)])
                ht.read([("pe", v)])
                h23 = V["h2"].rearrange("p (c n) -> p c n", n=2048)
                h2ev = emit("dve", lambda e, tt=tt, h23=h23:
                            e.tensor_copy(
                                h23[:, :, tt * 128:(tt + 1) * 128],
                                bk(1, 0, 1024, BF16).rearrange(
                                    "p (c n) -> p c n", n=128)), inc=True,
                            waits=merge(bt.ready, trk("h2").free if tt == 0
                                        else ()))
                bt.read([("dve", h2ev)])
            trk("h2").write([("dve", h2ev)])
            rscr = reg("rscr", R_X, 512, F32)
            trk("rscr").read(merge(trk("K").free, trk("Q").free,
                                   trk("hfm").free))
            dma("gp", "rw", reg("rw", R_X + 2048, 64, BF16)
                .rearrange("p (c n) -> p c n", n=8),
                dram["f_rw"].rearrange("(c p) n -> p c n", p=128))
            trk("rw").read([])
            bt = trk("pb0")
            lg = None
            for tt in range(NT):
                for ci in range(NCH):
                    w = merge(bt.free, rdy("h2"), rdy("rw")) \
                        if (tt == 0 and ci == 0) else ()
                    lg = emit("pe", lambda e, tt=tt, ci=ci: e.matmul(
                        bk(0, tt * 8, 8),
                        V["h2"][:, ci * 2048 + tt * 128:
                                ci * 2048 + (tt + 1) * 128],
                        V["rw"][:, ci * 8:(ci + 1) * 8],
                        start=(tt == 0 and ci == 0), stop=(ci == NCH - 1),
                        skip_group_check=True),
                        inc=(ci == NCH - 1), waits=w)
            bt.write([("pe", lg)])
            PR, PM, M1B, M2B = 0, 128, 256, 384
            def rsl(base, tt, n=8):
                return V["rscr"][:, base + tt * 8: base + tt * 8 + n]
            lmv, prv, smv, rcv, nmv = {}, {}, {}, {}, {}
            for tt in range(NT):
                lmv[tt] = emit("dve", lambda e, tt=tt: e.reduce_max(
                    col("tcol", tt), bk(0, tt * 8, 8), axis=AX.X),
                    inc=True, waits=bt.ready)
            bt.read([("dve", lmv[NT - 1])])
            for tt in range(NT):
                nmv[tt] = emit("dve", lambda e, tt=tt: e.tensor_scalar(
                    col("var_c", tt), col("tcol", tt), -1.0, None,
                    ALU.mult), inc=True, waits=[("dve", lmv[tt])])
            exv = {}
            for tt in range(NT):
                exv[tt] = emit("act", lambda e, tt=tt: e.activation(
                    rsl(PR, tt), bk(0, tt * 8, 8), AF.Exp,
                    bias=col("var_c", tt)), inc=True,
                    waits=merge([("dve", nmv[tt])], trk("rscr").free,
                                [("dve", lmv[NT - 1])]))
            bt.read([("act", exv[NT - 1])])
            trk("rscr").write([("act", exv[NT - 1])])
            for tt in range(NT):
                smv[tt] = emit("dve", lambda e, tt=tt: e.reduce_sum(
                    col("s_c", tt), rsl(PR, tt), axis=AX.X), inc=True,
                    waits=[("act", exv[tt])])
            for tt in range(NT):
                rcv[tt] = emit("dve", lambda e, tt=tt: e.reciprocal(
                    col("q_c", tt), col("s_c", tt)), inc=True,
                    waits=[("dve", smv[tt])])
            pnv = {}
            for tt in range(NT):
                pnv[tt] = emit("dve", lambda e, tt=tt: e.tensor_scalar(
                    rsl(PR, tt), rsl(PR, tt), col("q_c", tt), None,
                    ALU.mult), inc=True, waits=[("dve", rcv[tt])])
            m1v, k1v, pmv, m2v, k2v = {}, {}, {}, {}, {}
            for tt in range(NT):
                m1v[tt] = emit("dve", lambda e, tt=tt: e.reduce_max(
                    col("m_c", tt), rsl(PR, tt), axis=AX.X), inc=True,
                    waits=[("dve", pnv[tt])])
            for tt in range(NT):
                k1v[tt] = emit("dve", lambda e, tt=tt: e.tensor_scalar(
                    rsl(M1B, tt), rsl(PR, tt), col("m_c", tt), None,
                    ALU.is_equal), inc=True, waits=[("dve", m1v[tt])])
            for tt in range(NT):
                pmv[tt] = emit("dve", lambda e, tt=tt:
                               e.scalar_tensor_tensor(
                                   rsl(PM, tt), rsl(M1B, tt), -10.0,
                                   rsl(PR, tt), ALU.mult, ALU.add),
                               inc=True, waits=[("dve", k1v[tt])])
            for tt in range(NT):
                m2v[tt] = emit("dve", lambda e, tt=tt: e.reduce_max(
                    col("rstd_c", tt), rsl(PM, tt), axis=AX.X), inc=True,
                    waits=[("dve", pmv[tt])])
            for tt in range(NT):
                k2v[tt] = emit("dve", lambda e, tt=tt: e.tensor_scalar(
                    rsl(M2B, tt), rsl(PM, tt), col("rstd_c", tt), None,
                    ALU.is_equal), inc=True, waits=[("dve", m2v[tt])])
            dv_, ev_, o1v, w1v, w2v = {}, {}, {}, {}, {}
            for tt in range(NT):
                dv_[tt] = emit("dve", lambda e, tt=tt: e.tensor_sub(
                    col("sig_c", tt), col("rstd_c", tt), col("m_c", tt)),
                    inc=True, waits=[("dve", m2v[tt])])
            for tt in range(NT):
                ev_[tt] = emit("act", lambda e, tt=tt: e.activation(
                    col("tiny", tt), col("sig_c", tt), AF.Exp),
                    inc=True, waits=[("dve", dv_[tt])])
            for tt in range(NT):
                o1v[tt] = emit("dve", lambda e, tt=tt: e.tensor_scalar(
                    col("tiny2", tt), col("tiny", tt), 1.0, None, ALU.add),
                    inc=True, waits=[("act", ev_[tt])])
            for tt in range(NT):
                w1v[tt] = emit("dve", lambda e, tt=tt: e.reciprocal(
                    col("m_c", tt), col("tiny2", tt)), inc=True,
                    waits=[("dve", o1v[tt])])
            for tt in range(NT):
                w2v[tt] = emit("dve", lambda e, tt=tt: e.tensor_scalar(
                    col("rstd_c", tt), col("m_c", tt), -1.0, 1.0,
                    ALU.mult, ALU.add), inc=True, waits=[("dve", w1v[tt])])
            c1v, c2v = {}, {}
            for tt in range(NT):
                c1v[tt] = emit("dve", lambda e, tt=tt: e.tensor_scalar(
                    V["combine"][:, tt * 8:(tt + 1) * 8], rsl(M1B, tt),
                    col("m_c", tt), None, ALU.mult), inc=True,
                    waits=merge([("dve", w1v[tt]), ("dve", k1v[tt])],
                                trk("combine").free))
            for tt in range(NT):
                c2v[tt] = emit("dve", lambda e, tt=tt:
                               e.scalar_tensor_tensor(
                                   V["combine"][:, tt * 8:(tt + 1) * 8],
                                   rsl(M2B, tt), col("rstd_c", tt),
                                   V["combine"][:, tt * 8:(tt + 1) * 8],
                                   ALU.mult, ALU.add), inc=True,
                               waits=[("dve", w2v[tt]), ("dve", c1v[tt]),
                                      ("dve", k2v[tt])])
            trk("combine").write([("dve", c2v[NT - 1])])
            trk("rscr").read([("dve", c2v[NT - 1])])
            if SUB <= 3:
                return
            for n in ("rscr", "rw", "V", "K", "Q", "hfm", "htm", "htm2",
                      "att0", "att1", "att2", "att3", "qsq", "yfT",
                      "yfull"):
                trk("x").free = merge(trk("x").free, trk(n).free)
            dma("sync", "x", V["x"].rearrange("p (t c) -> p t c", c=C),
                xsp_d.rearrange("(t p) c -> p t c", p=128), waits=rdy("xsp"))
            xadds = []
            for tt in range(NT):
                xv = emit("dve", lambda e, tt=tt: e.tensor_tensor(
                    xt("x", tt), xt("x2", tt), xt("x", tt), ALU.add),
                    inc=True, waits=merge(rdy("x"), rdy("x2"), rdy("h2"),
                                          rdy("combine")))
                xadds.append(xv)
            trk("x").write([("dve", xadds[-1])])
            trk("x2").read([("dve", xadds[-1])])
            trk("su").read(merge(trk("x2").free))
            trk("down").read(merge(trk("yfull").free, trk("stg").free,
                                   trk("stg2").free, trk("lnscr").free))
            for gname in ("gg0", "gg1", "ug0", "ug1"):
                trk(gname).read(trk("w1b").free)
            if SUB <= 4:
                return
            emit_expert("f_sg", "f_su", "f_sd", None, lambda tt: None)
            if SUB <= 5:
                return
            for e in range(NEXP):
                emit_expert("f_eg", "f_eu", "f_ed", e,
                            lambda tt, e=e: V["combine"][:, tt * 8 + e:
                                                         tt * 8 + e + 1])

        def _emit_bound_finalize(ncols, g):
            r1 = emit("dve", lambda e, ncols=ncols: e.reduce_max(
                col("tiny", 0), bk(6, 0, ncols), axis=AX.X), inc=True,
                waits=trk("pb6").ready)
            trk("pb6").read([("dve", r1)])
            r2 = emit("dve", lambda e: e.tensor_copy(
                V["nm_b"][:, 0:1], col("tiny", 0)), inc=True,
                waits=[("dve", r1)])
            tv = emit("pe", lambda e: e.transpose(
                bk(7, 0, 256, BF16)[0:1, 0:128], V["nm_b"][:, 0:1],
                V["iden"]), inc=True,
                waits=merge([("dve", r2)], trk("pb7").free, rdy("iden")))
            trk("pb7").write([("pe", tv)])
            r3 = emit("dve", lambda e: e.reduce_max(
                col("tiny", 1)[0:1, :], bk(7, 0, 128, BF16)[0:1, :],
                axis=AX.X), inc=True, waits=trk("pb7").ready)
            trk("pb7").read([("dve", r3)])
            r4 = emit("dve", lambda e: e.tensor_scalar(
                col("tiny2", 1)[0:1, :], col("tiny", 1)[0:1, :],
                -SCALE, None, ALU.mult), inc=True, waits=[("dve", r3)])
            r5 = emit("dve", lambda e: e.tensor_copy(
                V["nm_b"][0:1, 1:2], col("tiny2", 1)[0:1, :]), inc=True,
                waits=[("dve", r4)])
            bv_ = emit("pe", lambda e: e.matmul(
                bk(7, 128, 1), row(0, 128), V["nm_b"][0:1, 1:2],
                start=True, stop=True), inc=True,
                waits=merge([("dve", r5)], trk("pb7").free, rdy("rows")))
            trk("pb7").write([("pe", bv_)])
            mv_ = emit("dve", lambda e, g=g: e.tensor_copy(
                col("mneg", g), bk(7, 128, 1)), inc=True,
                waits=trk("pb7").ready)
            trk("pb7").read([("dve", mv_)])
            trk("mneg").write([("dve", mv_)])

        def emit_attn_b():
            blat = reg("blat", R_A, 2048, BF16)
            bV = reg("bV", R_A + 4096, 16384, BF16)
            bWv = reg("bWv", R_A + 36864, 8192, BF16)
            bWbT = reg("bWbT", R_A + 53248, 1024, BF16)
            blatsq = reg("blatsq", R_A + 55296, 2048, BF16)
            bhtm = reg("bhtm", R_A + 59392, 1024, BF16)
            bytm = reg("bytm", R_A + 61440, 1024, BF16)
            byfT = reg("byfT", R_A + 63488, 1024, BF16)
            bWo = reg("bWo", R_B, 8192, BF16)
            batts = reg("batts", R_B + 16384, 8192, BF16)
            bhfm = reg("bhfm", R_B + 32768, 4096, BF16)
            bstg = reg("stg", R_B + 40960, 1024, BF16)
            bstg2 = reg("stg2", R_B + 43008, 1024, BF16)
            blnscr = reg("lnscr", R_B + 45056, 1024, F32)

            emit_ln_stats("x", blnscr)

            def stage_T(wsrc, dst_name, dst_slice, ci0):
                dt_ = trk(dst_name)
                done = []
                for cl in range(4):
                    ci = ci0 + cl
                    sname = "stg" if cl % 2 == 0 else "stg2"
                    sb = V[sname]
                    dma("gp", sname, sb,
                        dram[wsrc][ci * 128:(ci + 1) * 128, :])
                    bt = trk("pb0")
                    v = None
                    for j in range(NCH):
                        w = merge(bt.free, rdy(sname), rdy("iden")) \
                            if j == 0 else ()
                        v = emit("pe", lambda e, j=j, sb=sb: e.transpose(
                            bk(0, j * 128, 128, BF16),
                            sb[:, j * 128:(j + 1) * 128], V["iden"]),
                            inc=(j == NCH - 1), waits=w)
                    bt.write([("pe", v)])
                    trk(sname).read([("pe", v)])
                    ev = emit("act", lambda e, ci=ci, dst_slice=dst_slice:
                              e.activation(
                                  dst_slice(ci),
                                  bk(0, 0, 1024, BF16).rearrange(
                                      "p (j n) -> p j n", n=128),
                                  AF.Copy), inc=True,
                              waits=merge(bt.ready, dt_.free))
                    bt.read([("act", ev)])
                    done.append(("act", ev))
                dt_.write(merge(dt_.ready, done))

            dma("gp", "stg", bstg[0:64, :], dram["ab_wb"][:, :])
            bt = trk("pb0")
            v = None
            for ci in range(NCH):
                w = merge(bt.free, rdy("stg"), rdy("iden")) if ci == 0 \
                    else ()
                v = emit("pe", lambda e, ci=ci: e.transpose(
                    bk(0, ci * 64, 64, BF16),
                    bstg[0:64, ci * 128:(ci + 1) * 128],
                    V["iden"][0:64, 0:64]), inc=(ci == NCH - 1), waits=w)
            bt.write([("pe", v)])
            trk("stg").read([("pe", v)])
            dv = None
            for half in range(2):
                dv = emit("act", lambda e, half=half, bWbT=bWbT:
                          e.activation(
                              bWbT.rearrange("p (c n) -> p c n", n=128)[
                                  :, :, half * 64:(half + 1) * 64],
                              bk(0, 0, 512, BF16).rearrange(
                                  "p (c n) -> p c n", n=64),
                              AF.Copy), inc=True,
                          waits=merge(bt.ready, trk("bWbT").free))
                bt.read([("act", dv)])
            trk("bWbT").write([("act", dv)])

            def wvsl(ci, bWv=bWv):
                return bWv.rearrange("p (j n) -> p j n", n=1024)[
                    :, :, ci * 128:(ci + 1) * 128]
            stage_T("ab_wv", "bWv", wvsl, 0)
            stage_T("ab_wv", "bWv", wvsl, 4)
            dma("gp", "rowscr", row(896, 1024),
                dram["ab_vb"].rearrange("(a c) -> a c", a=1))

            stats_w = rdy("stats")
            for tg in range(4):
                for j in range(4):
                    tt = tg * 4 + j
                    ht = trk("bhtm")
                    hv = emit("dve", lambda e, tt=tt, bhtm=bhtm:
                              e.tensor_scalar(
                                  bhtm, xt("x", tt), col("m_c", tt),
                                  col("rstd_c", tt), ALU.subtract,
                                  ALU.mult), inc=True,
                              waits=merge(stats_w, rdy("x"), ht.free))
                    ht.write([("dve", hv)])
                    bt = trk("pb1")
                    v = None
                    for ch in range(NCH):
                        w = merge(bt.free, [("dve", hv)], rdy("iden")) \
                            if ch == 0 else ()
                        v = emit("pe", lambda e, ch=ch, bhtm=bhtm:
                                 e.transpose(
                                     bk(1, ch * 128, 128, BF16),
                                     bhtm[:, ch * 128:(ch + 1) * 128],
                                     V["iden"]), inc=(ch == NCH - 1),
                                 waits=w)
                    bt.write([("pe", v)])
                    ht.read([("pe", v)])
                    hf3 = bhfm.rearrange("p (c n) -> p c n", n=512)
                    ev = emit("act", lambda e, j=j, hf3=hf3:
                              e.activation(
                                  hf3[:, :, j * 128:(j + 1) * 128],
                                  bk(1, 0, 1024, BF16).rearrange(
                                      "p (c n) -> p c n", n=128),
                                  AF.Copy), inc=True,
                              waits=merge(bt.ready, trk("bhfm").free
                                          if j == 0 else ()))
                    bt.read([("act", ev)])
                hrd = [("act", P.cnt["act"])]
                trk("bhfm").write(hrd)
                parts = [(bWbT[:, ci * 128:(ci + 1) * 128],
                          bhfm[:, ci * 512:(ci + 1) * 512])
                         for ci in range(NCH)]
                mm_group(2, bk(2), parts,
                         extra_waits=merge(hrd, rdy("bWbT")),
                         reads=("bhfm", "bWbT"))
                bt2 = trk("pb2")
                lev = emit("act", lambda e, tg=tg, blat=blat: e.activation(
                    blat[:, tg * 512:(tg + 1) * 512], bk(2), AF.Copy),
                    inc=True, waits=merge(bt2.ready, trk("blat").free))
                bt2.read([("act", lev)])
                trk("blat").ready = [("act", lev)]
                for j in range(4):
                    tt = tg * 4 + j
                    for ch in range(2):
                        bn = 4 + ((2 * j + ch) % 2)
                        parts = [(bhfm[:, ci * 512 + j * 128:
                                       ci * 512 + (j + 1) * 128],
                                  bWv[:, ci * 1024 + ch * 512:
                                      ci * 1024 + (ch + 1) * 512])
                                 for ci in range(NCH)]
                        parts.append((row(0, 128),
                                      row(896 + ch * 512, 512)))
                        mm_group(bn, bk(bn), parts,
                                 extra_waits=merge(hrd, rdy("bWv"),
                                                   rdy("rows"),
                                                   rdy("rowscr")),
                                 reads=("bhfm", "bWv", "rowscr"))
                        bt3 = trk(f"pb{bn}")
                        ev = emit("act", lambda e, tt=tt, ch=ch, bn=bn,
                                  bV=bV: e.activation(
                                      bV[:, tt * 1024 + ch * 512:
                                         tt * 1024 + (ch + 1) * 512],
                                      bk(bn), AF.Copy), inc=True,
                                  waits=merge(bt3.ready, trk("bV").free))
                        bt3.read([("act", ev)])
                        trk("bV").ready = [("act", ev)]
            aq = emit("act", lambda e, blatsq=blatsq, blat=blat:
                      e.activation(blatsq[0:64, :], blat[0:64, :],
                                   AF.Square), inc=True,
                      waits=merge(trk("blat").ready, trk("blatsq").free))
            trk("blatsq").write([("act", aq)])
            vlast = None
            for tt in range(NT):
                w = merge(trk("pb6").free, [("act", aq)],
                          rdy("onescol")) if tt == 0 else ()
                vlast = emit("pe", lambda e, tt=tt, blatsq=blatsq:
                             e.matmul(
                                 bk(6, tt, 1),
                                 blatsq[0:64, tt * 128:(tt + 1) * 128],
                                 V["onescol"][0:64, 0:1], start=True,
                                 stop=True),
                             inc=(tt == NT - 1), waits=w)
            trk("blatsq").read([("pe", vlast)])
            trk("pb6").write([("pe", vlast)])
            _emit_bound_finalize(16, 0)

            def wosl(ci, bWo=bWo):
                return bWo.rearrange("p (j n) -> p j n", n=1024)[
                    :, :, ci * 128:(ci + 1) * 128]
            stage_T("ab_wo", "bWo", wosl, 0)
            stage_T("ab_wo", "bWo", wosl, 4)
            dma("gp", "rowscr2", ROWS[0:1, 640:768] if False else
                row(896, 1024), dram["ab_ob"].rearrange("(a c) -> a c",
                                                        a=1),
                waits=trk("rowscr").free)

            xadds = []
            for qc in range(4):
                nkt = 4 * qc + 4
                den_first = True
                for kt in range(nkt):
                    base = 64 * (kt % 2)
                    bn = 1 + (kt % 2)
                    bt = trk(f"pb{bn}")
                    stv = emit("pe", lambda e, kt=kt, qc=qc, base=base,
                               bn=bn, blat=blat: e.matmul(
                                   bk(bn),
                                   blat[base:base + 64,
                                        kt * 128:(kt + 1) * 128],
                                   blat[base:base + 64,
                                        qc * 512:(qc + 1) * 512],
                                   start=True, stop=True), inc=True,
                               waits=merge(bt.free, trk("blat").ready))
                    bt.write([("pe", stv)])
                    if qc * 4 <= kt:
                        ql = kt - qc * 4
                        dvv = emit("dve", lambda e, ql=ql, bn=bn:
                                   e.tensor_tensor(
                                       bk(bn, ql * 128, 128),
                                       bk(bn, ql * 128, 128),
                                       V["diag"], ALU.add), inc=True,
                                   waits=merge(bt.ready, rdy("diag")))
                        bt.write([("pe", stv), ("dve", dvv)])
                    ev = emit("act", lambda e, kt=kt, bn=bn, batts=batts:
                              e.activation(
                                  batts[:, kt * 512:(kt + 1) * 512],
                                  bk(bn), AF.Exp, bias=col("mneg", 0),
                                  scale=SCALE), inc=True,
                              waits=merge(bt.ready, rdy("mneg"),
                                          trk("batts").free
                                          if kt == 0 else ()))
                    bt.read([("act", ev)])
                trk("batts").write([("act", P.cnt["act"])])
                for qlo in range(4):
                    qt = qc * 4 + qlo
                    dlast = None
                    for kt in range(qt + 1):
                        w = merge(trk("pb7").free, trk("batts").ready,
                                  rdy("onescol")) if kt == 0 else ()
                        dlast = emit("pe", lambda e, kt=kt, qlo=qlo,
                                     qt=qt, batts=batts: e.matmul(
                                         bk(7, qt, 1),
                                         batts[:, kt * 512 + qlo * 128:
                                               kt * 512 + (qlo + 1)
                                               * 128],
                                         V["onescol"], start=(kt == 0 and
                                                              den_first),
                                         stop=(kt == qt),
                                         skip_group_check=True),
                                     inc=(kt == qt), waits=w)
                        den_first = False
                    trk("pb7").write([("pe", dlast)])
                    for half in range(2):
                        bn = 3 + half
                        parts = [(batts[:, kt * 512 + qlo * 128:
                                        kt * 512 + (qlo + 1) * 128],
                                  bV[:, kt * 1024 + half * 512:
                                     kt * 1024 + (half + 1) * 512])
                                 for kt in range(qt + 1)]
                        mm_group(bn, bk(bn), parts,
                                 extra_waits=merge(trk("batts").ready,
                                                   rdy("bV")),
                                 reads=("batts", "bV"))
                    rv_ = emit("dve", lambda e, qt=qt: e.reciprocal(
                        col("tiny", 2), bk(7, qt, 1)), inc=True,
                        waits=merge(trk("pb7").ready,
                                    [("act", P.cnt["act"])]))
                    trk("pb7").read([("dve", rv_)])
                    evs = []
                    for half in range(2):
                        bn = 3 + half
                        bt3 = trk(f"pb{bn}")
                        ev = emit("act", lambda e, half=half, bn=bn,
                                  bytm=bytm: e.activation(
                                      bytm[:, half * 512:(half + 1)
                                           * 512],
                                      bk(bn), AF.Copy,
                                      scale=col("tiny", 2)), inc=True,
                                  waits=merge(bt3.ready, [("dve", rv_)],
                                              trk("bytm").free))
                        bt3.read([("act", ev)])
                        evs.append(("act", ev))
                    trk("bytm").write(evs)
                    bt = trk("pb1")
                    v = None
                    for ch in range(NCH):
                        w = merge(bt.free, trk("bytm").ready,
                                  rdy("iden")) if ch == 0 else ()
                        v = emit("pe", lambda e, ch=ch, bytm=bytm:
                                 e.transpose(
                                     bk(1, ch * 128, 128, BF16),
                                     bytm[:, ch * 128:(ch + 1) * 128],
                                     V["iden"]), inc=(ch == NCH - 1),
                                 waits=w)
                    bt.write([("pe", v)])
                    trk("bytm").read([("pe", v)])
                    ev = emit("act", lambda e, byfT=byfT: e.activation(
                        byfT, bk(1, 0, 1024, BF16), AF.Copy), inc=True,
                        waits=merge(bt.ready, trk("byfT").free))
                    bt.read([("act", ev)])
                    trk("byfT").write([("act", ev)])
                    for chh in range(2):
                        bn = 5
                        parts = [(byfT[:, cch * 128:(cch + 1) * 128],
                                  bWo[:, cch * 1024 + chh * 512:
                                      cch * 1024 + (chh + 1) * 512])
                                 for cch in range(NCH)]
                        parts.append((row(0, 128),
                                      row(896 + chh * 512, 512)))
                        mm_group(bn, bk(bn), parts,
                                 extra_waits=merge(trk("byfT").ready,
                                                   rdy("bWo"), rdy("rows"),
                                                   rdy("rowscr2")),
                                 reads=("byfT", "bWo", "rowscr2"))
                        bt2 = trk(f"pb{bn}")
                        xv = emit("dve", lambda e, qt=qt, chh=chh, bn=bn:
                                  e.tensor_tensor(
                                      V["x"][:, qt * C + chh * 512:
                                             qt * C + (chh + 1) * 512],
                                      bk(bn),
                                      V["x"][:, qt * C + chh * 512:
                                             qt * C + (chh + 1) * 512],
                                      ALU.add), inc=True,
                                  waits=merge(bt2.ready, rdy("x")))
                        bt2.read([("dve", xv)])
                        xadds.append(xv)
            trk("x").write([("dve", xadds[-1])])

        if STAGES >= 1:
            emit_dense_attn("aa")
            barrier()
        if STAGES >= 2:
            emit_ffn()
            barrier()
        if STAGES >= 3:
            emit_attn_b()
            barrier()
        if STAGES >= 4:
            emit_ffn()
            barrier()

        if STAGES >= 2 and SUB < 9:
            vdbg = emit("sync", lambda e: e.dma_start(
                out=dbg_d.rearrange("(t p) c -> p t c", p=128),
                in_=V["x2"].rearrange("p (t c) -> p t c", c=C)),
                inc=True, waits=rdy("x2"), inc_sem="q_dbg", step=16)
            P.wait_only("sync", [("q_dbg", vdbg)])
        vst = emit("sync", lambda e: e.dma_start(
            out=out_d.rearrange("(t p) c -> p t c", p=128),
            in_=V["x"].rearrange("p (t c) -> p t c", c=C)),
            inc=True, waits=rdy("x"), inc_sem="q_out", step=16)
        P.wait_only("sync", [(s, c) for s, c in P.cnt.items()
                             if s.startswith("q_") and c > 0])

        with nc.Block() as block:
            @block.tensor
            def _(t):
                for f in P.ops["pe"]:
                    f(t)

            @block.scalar
            def _(s):
                for f in P.ops["act"]:
                    f(s)

            @block.vector
            def _(v):
                for f in P.ops["dve"]:
                    f(v)

            @block.gpsimd
            def _(g):
                for f in P.ops["gp"]:
                    f(g)

            @block.sync
            def _(s):
                for f in P.ops["sync"]:
                    f(s)
    return nc


_NC_CACHE = {}


def kernel(**inputs):
    if "prog" not in _NC_CACHE:
        _NC_CACHE["prog"] = build_program()
    nc = _NC_CACHE["prog"]
    cst = host_consts()
    x = np.ascontiguousarray(inputs["L0_attn_a_x"])
    names = {
        "aa_wq": "L0_attn_a_wq_w", "aa_qb": "L0_attn_a_wq_b",
        "aa_wk": "L0_attn_a_wk_w", "aa_kb": "L0_attn_a_wk_b",
        "aa_wv": "L0_attn_a_wv_w", "aa_vb": "L0_attn_a_wv_b",
        "aa_wo": "L0_attn_a_wo_w", "aa_ob": "L0_attn_a_wo_b",
        "ab_wb": "L0_attn_b_W_basis", "ab_wv": "L0_attn_b_W_v",
        "ab_vb": "L0_attn_b_b_v", "ab_wo": "L0_attn_b_W_o",
        "ab_ob": "L0_attn_b_b_o", "f_n1": "L0_ffn_norm1_w",
        "f_n2": "L0_ffn_norm2_w", "f_wq": "L0_ffn_wq", "f_wk": "L0_ffn_wk",
        "f_wv": "L0_ffn_wv", "f_wo": "L0_ffn_wo",
        "f_rw": "L0_ffn_router_w", "f_sg": "L0_ffn_sh_gate",
        "f_su": "L0_ffn_sh_up", "f_sd": "L0_ffn_sh_down",
        "f_eg": "L0_ffn_e_gate", "f_eu": "L0_ffn_e_up",
        "f_ed": "L0_ffn_e_down",
    }
    base = {k: np.ascontiguousarray(inputs[v]) for k, v in names.items()}
    base.update(cst)
    in_maps = []
    for i in range(8):
        m = dict(base)
        m["x"] = np.ascontiguousarray(x[i])
        in_maps.append(m)
    res = run_bass_kernel_spmd(nc, in_maps, core_ids=list(range(8)))
    return np.stack([np.asarray(r["out"], np.float32)
                     for r in res.results], axis=0)


# version 06
# speedup vs baseline: 1.0008x; 1.0008x over previous
import os
import numpy as np
import ml_dtypes
import concourse.bass as bass
import concourse.mybir as mybir
from concourse.bass_utils import run_bass_kernel_spmd
from contextlib import ExitStack

F32 = mybir.dt.float32
BF16 = mybir.dt.bfloat16
AF = mybir.ActivationFunctionType
ALU = mybir.AluOpType
AX = mybir.AxisListType

T, C, H, DH = 2048, 1024, 16, 64
NT, NCH = 16, 8
DFF, NDF, NEXP = 2048, 16, 8
SCALE = 0.125
EPS_LN, EPS_RMS = 1e-5, 1e-6
NEG_BIG = -1.0e30
STAGES = int(os.environ.get("MK_STAGES", "4"))
SUB = int(os.environ.get("MK_SUB", "9"))
FSUB = float(os.environ.get("MK_FSUB", "9"))
BF = ml_dtypes.bfloat16


def host_consts():
    iden = np.eye(128, dtype=BF)
    ones128 = np.ones((1, 128), dtype=BF)
    ones512 = np.ones((1, 512), dtype=BF)
    sel0 = np.zeros((1, 128), dtype=BF); sel0[0, :64] = 1
    sel1 = np.zeros((1, 128), dtype=BF); sel1[0, 64:] = 1
    ind2 = np.zeros((128, 2), dtype=BF); ind2[:64, 0] = 1; ind2[64:, 1] = 1
    kk, qq = np.meshgrid(np.arange(128), np.arange(128), indexing="ij")
    diag = np.where(qq >= kk, 0.0, NEG_BIG).astype(np.float32)
    onescol = np.ones((128, 1), dtype=BF)
    return {"c_iden": iden, "c_ones128": ones128, "c_ones512": ones512,
            "c_sel0": sel0, "c_sel1": sel1, "c_ind2": ind2, "c_diag": diag,
            "c_onescol": onescol}


class Planner:

    QUEUES = ("sync", "gp")
    ENGINES = ("pe", "act", "dve", "pool") + QUEUES

    def __init__(self, mksem):
        self.ops = {e: [] for e in self.ENGINES}
        self.cnt = {e: 0 for e in self.ENGINES}
        self.last_wait = {e: {} for e in self.ENGINES}
        self.sems = {}
        self.mksem = mksem

    def sem(self, name):
        if name not in self.sems:
            self.sems[name] = self.mksem(name)
            self.cnt.setdefault(name, 0)
        return self.sems[name]

    def emit(self, eng, fn, inc=False, waits=(), inc_sem=None, step=1):
        lw = self.last_wait[eng]
        eff = []
        for s, v in waits:
            if v <= 0 or lw.get(s, -1) >= v:
                continue
            lw[s] = v
            eff.append((s, v))
            self.sem(s)
        tgt = None
        if inc:
            sname = inc_sem or eng
            self.sem(sname)
            self.cnt[sname] = self.cnt.get(sname, 0) + step
            tgt = self.cnt[sname]
        sems = self.sems

        def run(e, eff=tuple(eff), fn=fn, inc=inc, step=step,
                sname=(inc_sem or eng)):
            for s, v in eff:
                e.wait_ge(sems[s], v)
            ins = fn(e)
            if inc:
                ins.then_inc(sems[sname], step)
        self.ops[eng].append(run)
        return tgt if inc else self.cnt.get(eng, 0)

    def wait_only(self, eng, waits):
        self.emit(eng, lambda e: None, waits=waits)

    def now(self):
        return [(s, c) for s, c in self.cnt.items() if c > 0]


class Trk:
    def __init__(self):
        self.ready = []
        self.free = []

    def write(self, ready):
        self.ready = list(ready)
        self.free = []

    def read(self, free):
        for w in free:
            if w not in self.free:
                self.free.append(w)


def merge(*wls):
    out = []
    for wl in wls:
        for w in wl:
            if w not in out:
                out.append(w)
    return out


def build_program():
    nc = bass.Bass("TRN2", target_bir_lowering=False,
                   detect_race_conditions=(
                       os.environ.get("MK_NORACE", "0") != "1"))
    P = None
    dram = {}

    def din(name, shape, dt=F32):
        dram[name] = nc.dram_tensor(name, shape, dt, kind="ExternalInput").ap()

    din("x", [T, C])
    for nm, shp in [("aa_wq", [C, C]), ("aa_wk", [C, C]), ("aa_wv", [C, C]),
                    ("aa_wo", [C, C]), ("ab_wb", [64, C]), ("ab_wv", [C, C]),
                    ("ab_wo", [C, C]), ("f_wq", [C, C]), ("f_wk", [C, C]),
                    ("f_wv", [C, C]), ("f_wo", [C, C]), ("f_rw", [C, NEXP]),
                    ("f_sg", [C, DFF]), ("f_su", [C, DFF]),
                    ("f_sd", [DFF, C]), ("f_eg", [NEXP, C, DFF]),
                    ("f_eu", [NEXP, C, DFF]), ("f_ed", [NEXP, DFF, C])]:
        din(nm, shp)
    for nm in ["aa_qb", "aa_kb", "aa_vb", "aa_ob", "ab_vb", "ab_ob",
               "f_n1", "f_n2"]:
        din(nm, [C])
    din("c_iden", [128, 128], BF16)
    din("c_ones128", [1, 128], BF16)
    din("c_ones512", [1, 512], BF16)
    din("c_sel0", [1, 128], BF16)
    din("c_sel1", [1, 128], BF16)
    din("c_ind2", [128, 2], BF16)
    din("c_diag", [128, 128], F32)
    din("c_onescol", [128, 1], BF16)
    out_d = nc.dram_tensor("out", [T, C], F32, kind="ExternalOutput").ap()
    dbg_d = nc.dram_tensor("dbg", [T, C], F32, kind="ExternalOutput").ap()
    xsp_d = nc.dram_tensor("xspill", [T, C], F32, kind="Internal").ap()

    bufs = {}

    def trk(name):
        if name not in bufs:
            bufs[name] = Trk()
        return bufs[name]

    def barrier(exclude=()):
        g = P.now()
        for nm, t in bufs.items():
            if nm in exclude:
                continue
            t.read(g)

    ctx = ExitStack()
    with ctx:
        AR_BYTES = 208128
        AR = ctx.enter_context(nc.sbuf_tensor("AR", [128, AR_BYTES // 4],
                                              F32))
        R_X, R_A, R_B, R_Z = 0, 65536, 131072, 206080

        def view(off_b, elems, dt=F32):
            assert off_b % 4 == 0, off_b
            n32 = elems if dt == F32 else (elems + 1) // 2
            assert off_b // 4 + n32 <= AR_BYTES // 4, (off_b, elems)
            a = AR[:, off_b // 4: off_b // 4 + n32]
            if dt != F32:
                a = a.bitcast(dt)[:, 0:elems]
            return a

        V = {}

        def reg(name, off_b, elems, dt=F32):
            V[name] = view(off_b, elems, dt)
            trk(name)
            return V[name]

        reg("x", R_X, 16384, F32)
        reg("x2", R_A, 16384, F32)
        reg("su", R_A, 32768, BF16)
        reg("h2", R_B, 16384, BF16)
        reg("down", R_B + 32768, 16384, BF16)
        GR = R_B + 65536
        for i in range(2):
            reg(f"gg{i}", GR + i * 2048, 1024, BF16)
            reg(f"ug{i}", GR + 4096 + i * 2048, 1024, BF16)
        reg("m_c", R_Z, 16)
        reg("rstd_c", R_Z + 64, 16)
        reg("sig_c", R_Z + 128, 16)
        reg("s_c", R_Z + 192, 16)
        reg("q_c", R_Z + 256, 16)
        reg("var_c", R_Z + 320, 16)
        reg("tiny", R_Z + 384, 16)
        reg("tiny2", R_Z + 448, 16)
        reg("combine", R_Z + 512, 128)
        reg("mneg", R_Z + 1024, 4)
        reg("nm_b", R_Z + 1040, 8, BF16)
        reg("mask01", R_Z + 1056, 16)
        reg("iden", R_Z + 1120, 128, BF16)
        reg("ind2", R_Z + 1376, 2, BF16)
        reg("onescol", R_Z + 1380, 1, BF16)
        reg("diag", R_Z + 1384, 128, F32)
        reg("qbc", R_Z + 1896, 8, F32)
        reg("kbc", R_Z + 1928, 8, F32)
        reg("tcol", R_Z + 1960, 16, F32)
        reg("eps_c", R_Z + 2024, 2, F32)
        assert R_Z + 2032 <= AR_BYTES

        ROWS = ctx.enter_context(nc.sbuf_tensor("ROWS", [1, 2048], BF16))
        trk("rows")
        trk("rowscr")

        def row(off, n):
            return ROWS[0:1, off:off + n]

        pb = [ctx.enter_context(nc.psum_tensor(f"pb{i}", [128, 512], F32))
              for i in range(8)]
        for i in range(8):
            trk(f"pb{i}")

        def bk(i, lo=0, n=512, dt=F32):
            if dt == F32:
                return pb[i][:, lo:lo + n]
            return pb[i][:].bitcast(dt)[:, lo:lo + n]

        P = Planner(lambda name: ctx.enter_context(
            nc.semaphore(f"sm_{name}")))
        for s in ("pe", "act", "dve", "pool"):
            P.sem(s)
        emit = P.emit

        def dma(queue, name, dst, src, waits=(), sem=None):
            t = trk(name)
            sname = sem or f"q_{name}"
            v = emit(queue, lambda e, d=dst, s=src: e.dma_start(out=d, in_=s),
                     inc=True, waits=merge(t.free, waits),
                     inc_sem=sname, step=16)
            t.write([(sname, v)])
            return v

        def rdy(*names):
            return merge(*[trk(n).ready for n in names])

        def mm_group(bnum, bank_ap, parts, extra_waits=(), inc=True,
                     reads=()):
            t = trk(f"pb{bnum}")
            n = len(parts)
            v = None
            for i, (l, r) in enumerate(parts):
                w = merge(t.free, extra_waits) if i == 0 else ()
                v = emit("pe", lambda e, l=l, r=r, i=i, n=n, b=bank_ap:
                         e.matmul(b, l, r, start=(i == 0), stop=(i == n - 1)),
                         inc=(inc and i == n - 1), waits=w)
            t.write([("pe", v)])
            for nm in reads:
                trk(nm).read([("pe", v)])
            return v

        xsb4 = V["x"].rearrange("p (t c) -> p t c", c=C)
        xhb4 = dram["x"].rearrange("(t p) c -> p t c", p=128)
        for ch_ in range(4):
            xv_ = emit("sync", lambda e, c=ch_: e.dma_start(
                out=xsb4[:, c * 4:(c + 1) * 4], in_=xhb4[:, c * 4:(c + 1) * 4]),
                inc=True, inc_sem="q_x", step=16)
        trk("x").write([("q_x", xv_)])
        x_chunk_deps = {tt: [("q_x", 16 * (tt // 4 + 1))] for tt in range(16)}
        dma("sync", "iden", V["iden"], dram["c_iden"])
        dma("sync", "ind2", V["ind2"], dram["c_ind2"])
        dma("sync", "diag", V["diag"], dram["c_diag"])
        dma("sync", "onescol", V["onescol"], dram["c_onescol"])
        for dst_, srcn_ in ((row(0, 128), "c_ones128"),
                            (row(128, 512), "c_ones512"),
                            (row(640, 128), "c_sel0"),
                            (row(768, 128), "c_sel1")):
            v0 = emit("sync", lambda e, d=dst_, s=srcn_: e.dma_start(
                out=d, in_=dram[s]), inc=True, inc_sem="q_rows", step=16)
        trk("rows").write([("q_rows", v0)])
        ev_ = emit("dve", lambda e: e.memset(V["eps_c"][:, 0:1], EPS_LN),
                   inc=True)
        ev_ = emit("dve", lambda e: e.memset(V["eps_c"][:, 1:2], EPS_RMS),
                   inc=True)
        trk("eps_c").write([("dve", ev_)])

        def col(name, tt):
            return V[name][:, tt:tt + 1]

        def xt(src, tt):
            return V[src][:, tt * C:(tt + 1) * C]

        def emit_ln_stats(src, scr, tile_deps=None):
            rd = rdy(src) if tile_deps is None else ()
            st = trk("lnscr")
            sv, qv = {}, {}
            for tt in range(NT):
                sv[tt] = emit("dve", lambda e, tt=tt: e.reduce_sum(
                    col("s_c", tt), xt(src, tt), axis=AX.X), inc=True,
                    waits=merge(rd, tile_deps[tt] if tile_deps else ()))
            for tt in range(NT):
                qv[tt] = emit("act", lambda e, tt=tt, s=scr: e.activation(
                    s, xt(src, tt), AF.Square, accum_out=col("q_c", tt)),
                    inc=True, waits=merge(rd, tile_deps[tt]
                                          if tile_deps else (),
                                          st.free if tt == 0 else ()))
            st.write([("act", qv[NT - 1])])
            st.read([("act", qv[NT - 1])])
            for tt in range(NT):
                emit("dve", lambda e, tt=tt: e.tensor_scalar(
                    col("m_c", tt), col("s_c", tt), 1.0 / C, None, ALU.mult),
                    inc=True, waits=[("dve", sv[tt])])
            mv = P.cnt["dve"]
            mm = {}
            for tt in range(NT):
                mm[tt] = emit("dve", lambda e, tt=tt: e.tensor_tensor(
                    col("tiny", tt), col("m_c", tt), col("m_c", tt),
                    ALU.mult), inc=True, waits=[("dve", mv)])
            for tt in range(NT):
                emit("dve", lambda e, tt=tt: e.scalar_tensor_tensor(
                    col("var_c", tt), col("q_c", tt), 1.0 / C,
                    col("tiny", tt), ALU.mult, ALU.subtract), inc=True,
                    waits=[("dve", mm[tt]), ("act", qv[tt])])
            vv = P.cnt["dve"]
            for tt in range(NT):
                emit("act", lambda e, tt=tt: e.activation(
                    col("tiny2", tt), col("var_c", tt), AF.Ln,
                    bias=V["eps_c"][:, 0:1]), inc=True,
                    waits=merge([("dve", vv)], rdy("eps_c")))
            lv = P.cnt["act"]
            rv = None
            for tt in range(NT):
                rv = emit("act", lambda e, tt=tt: e.activation(
                    col("rstd_c", tt), col("tiny2", tt), AF.Exp, scale=-0.5),
                    inc=True, waits=[("act", lv)])
            trk("stats").write([("act", rv), ("dve", mv)])

        def emit_rms_stats(src, scr, eps, dstcol="sig_c", tile_deps=None):
            rd = rdy(src) if tile_deps is None else ()
            st = trk("lnscr")
            qv = {}
            for tt in range(NT):
                w = merge(rd, tile_deps[tt] if tile_deps else (),
                          st.free if tt == 0 else ())
                qv[tt] = emit("act", lambda e, tt=tt, s=scr, d=dstcol:
                              e.activation(s, xt(src, tt), AF.Square,
                                           accum_out=col(d, tt)),
                              inc=True, waits=w)
            st.write([("act", qv[NT - 1])])
            st.read([("act", qv[NT - 1])])
            for tt in range(NT):
                emit("dve", lambda e, tt=tt, d=dstcol: e.tensor_scalar(
                    col("tiny", tt), col(d, tt), 1.0 / C, None, ALU.mult),
                    inc=True, waits=[("act", qv[tt])])
            sv = P.cnt["dve"]
            for tt in range(NT):
                emit("act", lambda e, tt=tt: e.activation(
                    col("tiny2", tt), col("tiny", tt), AF.Ln,
                    bias=V["eps_c"][:, 1:2]), inc=True,
                    waits=merge([("dve", sv)], rdy("eps_c")))
            lv = P.cnt["act"]
            rv = None
            for tt in range(NT):
                rv = emit("act", lambda e, tt=tt, d=dstcol: e.activation(
                    col(d, tt), col("tiny2", tt), AF.Exp, scale=-0.5),
                    inc=True, waits=[("act", lv)])
            trk("stats").write([("act", rv)])

        ffn_x2a_cell = [None]
        abadds_cell = [None]
        def emit_dense_attn(kind, tile_deps=None):
            is_aa = kind == "aa"
            src = tgt = "x" if is_aa else "x2"
            CA = R_A if is_aa else R_X
            O_K, O_Q, O_V = CA, CA + 16384, CA + 32768
            O_HF, O_AT, O_HT = CA + 49408, CA + 57600, CA + 61696
            O_W3, O_YF = R_B, R_B + 24576
            O_STG, O_LN, O_W1 = R_B + 57344, R_B + 61440, GR
            kq = reg("K", O_K, 8192, BF16)
            qq = reg("Q", O_Q, 8192, BF16)
            vh = reg("V", O_V, 8320, BF16)
            hfm = reg("hfm", O_HF, 4096, BF16)
            att = [reg(f"att{i}", O_AT + i * 1024, 512, BF16)
                   for i in range(4)]
            htm = reg("htm", O_HT, 1024, BF16)
            htm2 = reg("htm2", O_AT, 1024, BF16)
            w3 = [reg(n, O_W3 + i * 8192, 4096, BF16)
                  for i, n in enumerate(("wq3", "wk3", "wv3"))]
            wof = reg("wof", O_W3, 8192, BF16)
            yf = reg("yfull", O_YF, 16384, BF16)
            stg = reg("stg", O_STG, 1024, BF16)
            stg2 = reg("stg2", O_STG + 2048, 1024, BF16)
            qsq = reg("qsq", O_AT, 2048, BF16)
            lnscr = reg("lnscr", O_LN, 1024, F32)
            w1b = reg("w1b", O_W1, 1024, F32)
            yfT = reg("yfT", O_AT, 1024, BF16)
            trk("w1b").free = merge(trk("w1b").free, trk("gg0").free,
                                    trk("gg1").free)
            if not is_aa and FSUB <= 0.2:
                return
            if is_aa:
                emit_ln_stats("x", lnscr, tile_deps=tile_deps)
                rd = rdy("x")
                for tt in range(NT):
                    a = emit("act", lambda e, tt=tt: e.activation(
                        col("tiny", tt), V["x"][:, tt * C: tt * C + 1],
                        AF.Sign), inc=True, waits=rd)
                    emit("act", lambda e, tt=tt: e.activation(
                        col("mask01", tt), col("tiny", tt), AF.Square),
                        inc=True, waits=[("act", a)])
                trk("mask01").write([("act", P.cnt["act"])])
                for bn_, dn_ in (("qbc", "aa_qb"), ("kbc", "aa_kb")):
                    t_ = trk(bn_)
                    v_ = emit("sync", lambda e, bn_=bn_, dn_=dn_:
                              e.dma_start(
                                  out=V[bn_],
                                  in_=dram[dn_].rearrange(
                                      "(c p) -> p c", p=128),
                                  allow_slow_non_contiguous=True),
                              inc=True, waits=t_.free,
                              inc_sem=f"q_{bn_}", step=16)
                    t_.write([(f"q_{bn_}", v_)])
            else:
                emit_rms_stats("x2", lnscr, EPS_RMS,
                               tile_deps=ffn_x2a_cell[0])
                dma("sync", "w1b", w1b,
                    dram["f_n1"].rearrange("(a c) -> a c", a=1)
                    .broadcast_to([128, C]))

            wnm = (("aa_wq", "aa_wk", "aa_wv", "aa_wo") if is_aa else
                   ("f_wq", "f_wk", "f_wv", "f_wo"))

            def stage_transpose(wsrc_rows, dst_name, dst3, gc0):
                dt_ = trk(dst_name)
                done = []
                for ci in range(4):
                    sname = "stg" if ci % 2 == 0 else "stg2"
                    sb = V[sname]
                    dma("gp", sname, sb,
                        dram[wsrc_rows][gc0 + ci * 128:
                                        gc0 + (ci + 1) * 128, :])
                    bt = trk("pb0")
                    v = None
                    for j in range(NCH):
                        w = merge(bt.free, rdy(sname), rdy("iden")) \
                            if j == 0 else ()
                        v = emit("pe", lambda e, j=j, sb=sb: e.transpose(
                            bk(0, j * 128, 128, BF16),
                            sb[:, j * 128:(j + 1) * 128], V["iden"]),
                            inc=(j == NCH - 1), waits=w)
                    bt.write([("pe", v)])
                    trk(sname).read([("pe", v)])
                    ev = emit("act", lambda e, ci=ci, dst3=dst3:
                              e.activation(dst3(ci),
                                           bk(0, 0, 1024, BF16), AF.Copy),
                              inc=True, waits=merge(bt.ready, dt_.free))
                    bt.read([("act", ev)])
                    done.append(("act", ev))
                dt_.write(done)

            def stage_grp(g):
                gc0 = g * 512
                if is_aa:
                    for wi, dn in enumerate(("wq3", "wk3", "wv3")):
                        wv3 = w3[wi]

                        def mk(wv3=wv3):
                            return lambda ci: wv3.rearrange(
                                "p (j n) -> p j n", n=512)[
                                    :, :, ci * 128:(ci + 1) * 128]
                        stage_transpose(wnm[wi], dn, mk(), gc0)
                    dma("gp", "rowscr", row(896, 512),
                        dram["aa_vb"].rearrange("(a c) -> a c", a=1)[
                            0:1, gc0:gc0 + 512])
                else:
                    for wi, dn in enumerate(("wq3", "wk3", "wv3")):
                        dma("gp", dn,
                            w3[wi].rearrange("p (c n) -> p c n", n=512),
                            dram[wnm[wi]][:, gc0:gc0 + 512].rearrange(
                                "(c p) n -> p c n", p=128))

            def build_wof():
                trk("wof").free = merge(trk("wof").free, trk("wq3").free,
                                        trk("wk3").free, trk("wv3").free)
                if is_aa:
                    for hlf in range(2):
                        def mk(hlf=hlf):
                            return lambda ci: wof.rearrange(
                                "p (c n) -> p c n", n=1024)[
                                    :, :, hlf * 512 + ci * 128:
                                    hlf * 512 + (ci + 1) * 128]
                        stage_transpose(wnm[3], "wof", mk(), hlf * 512)
                    dma("gp", "rowscr", row(896, 1024),
                        dram["aa_ob"].rearrange("(a c) -> a c", a=1))
                else:
                    dma("gp", "wof",
                        wof.rearrange("p (c n) -> p c n", n=C),
                        dram[wnm[3]].rearrange("(c p) n -> p c n", p=128))

            def phase_a(g):
                gc0 = g * 512
                stats_w = rdy("stats")
                trk("htm2").read(merge(trk("att0").free, trk("att1").free,
                                       trk("qsq").free, trk("yfT").free))
                for tg in range(4):
                    for j in range(4):
                        tt = tg * 4 + j
                        hb = (htm, htm2)[j % 2]
                        hn = ("htm", "htm2")[j % 2]
                        ht = trk(hn)
                        if is_aa:
                            hv = emit("dve", lambda e, tt=tt, hb=hb:
                                      e.tensor_scalar(
                                          hb, xt("x", tt), col("m_c", tt),
                                          col("rstd_c", tt), ALU.subtract,
                                          ALU.mult), inc=True,
                                      waits=merge(stats_w, rdy("x"),
                                                  ht.free))
                        else:
                            hv = emit("dve", lambda e, tt=tt, hb=hb,
                                      w1b=w1b: e.scalar_tensor_tensor(
                                          hb, xt("x2", tt),
                                          col("sig_c", tt), w1b, ALU.mult,
                                          ALU.mult), inc=True,
                                      waits=merge(stats_w,
                                                  rdy("x2", "w1b"),
                                                  ht.free))
                            trk("w1b").read([("dve", hv)])
                        ht.write([("dve", hv)])
                        tb = (1, 0)[j % 2]
                        bt = trk(f"pb{tb}")
                        v = None
                        for ch in range(NCH):
                            w = merge(bt.free, [("dve", hv)], rdy("iden")) \
                                if ch == 0 else ()
                            v = emit("pe", lambda e, ch=ch, hb=hb, tb=tb:
                                     e.transpose(
                                         bk(tb, ch * 128, 128, BF16),
                                         hb[:, ch * 128:(ch + 1) * 128],
                                         V["iden"]),
                                     inc=(ch == NCH - 1), waits=w)
                        bt.write([("pe", v)])
                        ht.read([("pe", v)])
                        hf3 = hfm.rearrange("p (c n) -> p c n", n=512)
                        ev = emit("dve", lambda e, j=j, hf3=hf3, tb=tb:
                                  e.tensor_copy(
                                      hf3[:, :, j * 128:(j + 1) * 128],
                                      bk(tb, 0, 1024, BF16).rearrange(
                                          "p (c n) -> p c n", n=128)),
                                  inc=True,
                                  waits=merge(bt.ready, trk("hfm").free
                                              if j == 0 else ()))
                        bt.read([("dve", ev)])
                    hrd = [("dve", P.cnt["dve"])]
                    trk("hfm").write(hrd)
                    for which, dst, wv3, wname, bco in (
                            ("Q", qq, w3[0], "wq3", "qbc"),
                            ("K", kq, w3[1], "wk3", "kbc")):
                        for dch in range(4):
                            bn = 2 + (dch % 2)
                            parts = [(wv3[:, ci * 512 + dch * 128:
                                          ci * 512 + (dch + 1) * 128],
                                      hfm[:, ci * 512:(ci + 1) * 512])
                                     for ci in range(NCH)]
                            mm_group(bn, bk(bn), parts,
                                     extra_waits=merge(hrd, rdy(wname)),
                                     reads=("hfm", wname))
                            bt2 = trk(f"pb{bn}")
                            dsl = dst[:, dch * 2048 + tg * 512:
                                      dch * 2048 + (tg + 1) * 512]
                            if is_aa:
                                bc = col(bco, g * 4 + dch)
                                ev = emit("dve", lambda e, dsl=dsl, bn=bn,
                                          bc=bc: e.tensor_scalar(
                                              dsl, bk(bn), bc, None,
                                              ALU.add), inc=True,
                                          waits=merge(bt2.ready, rdy(bco),
                                                      trk(which).free))
                                bt2.read([("dve", ev)])
                                trk(which).write([("dve", ev)])
                            else:
                                ev = emit("dve", lambda e, dsl=dsl, bn=bn:
                                          e.tensor_copy(dsl, bk(bn)),
                                          inc=True,
                                          waits=merge(bt2.ready,
                                                      trk(which).free))
                                bt2.read([("dve", ev)])
                                trk(which).write([("dve", ev)])
                    for j in range(4):
                        tt = tg * 4 + j
                        bn = 4 + (j % 2)
                        parts = [(hfm[:, ci * 512 + j * 128:
                                      ci * 512 + (j + 1) * 128],
                                  w3[2][:, ci * 512:(ci + 1) * 512])
                                 for ci in range(NCH)]
                        if is_aa:
                            parts.append((row(0, 128), row(896, 512)))
                        mm_group(bn, bk(bn), parts,
                                 extra_waits=merge(
                                     hrd, rdy("wv3"), rdy("rows"),
                                     rdy("rowscr") if is_aa else ()),
                                 reads=(("hfm", "wv3", "rowscr") if is_aa
                                        else ("hfm", "wv3")))
                        bt2 = trk(f"pb{bn}")
                        vdst = vh[:, tt * 520: tt * 520 + 520].rearrange(
                            "p (h g) -> p h g", g=65)
                        bsrc = bk(bn).rearrange("p (h d) -> p h d", d=64)
                        if is_aa:
                            ev = emit("dve", lambda e, vdst=vdst,
                                      bsrc=bsrc, tt=tt: e.tensor_scalar(
                                          vdst[:, :, 0:64], bsrc,
                                          col("mask01", tt), None,
                                          ALU.mult), inc=True,
                                      waits=merge(bt2.ready,
                                                  rdy("mask01"),
                                                  trk("V").free))
                        else:
                            ev = emit("dve", lambda e, vdst=vdst,
                                      bsrc=bsrc: e.tensor_copy(
                                          vdst[:, :, 0:64], bsrc),
                                      inc=True,
                                      waits=merge(bt2.ready,
                                                  trk("V").free))
                        bt2.read([("dve", ev)])
                        scol = (col("mask01", tt) if is_aa
                                else V["onescol"])
                        dv = emit("dve", lambda e, vdst=vdst, s=scol:
                                  e.tensor_copy(
                                      vdst[:, :, 64:65].rearrange(
                                          "p h o -> p (h o)"),
                                      s.broadcast_to([128, 8])), inc=True,
                                  waits=merge([("dve", ev)],
                                              rdy("mask01") if is_aa
                                              else rdy("onescol")))
                        trk("V").write([("dve", dv)])
                trk("qsq").read(merge(trk("att0").free, trk("att1").free,
                                      trk("htm2").free, trk("yfT").free))
                vlast = None
                for side, srcb, sname in (("q", qq, "Q"), ("k", kq, "K")):
                    for dch in range(4):
                        a = emit("act", lambda e, srcb=srcb, dch=dch,
                                 qsq=qsq: e.activation(
                                     qsq, srcb[:, dch * 2048:
                                               (dch + 1) * 2048],
                                     AF.Square), inc=True,
                                 waits=merge(rdy(sname),
                                             trk("qsq").free))
                        trk("qsq").write([("act", a)])
                        for tt in range(NT):
                            ci = (0 if side == "q" else 128) + \
                                (dch * 16 + tt) * 2
                            w = merge(trk("pb6").free, [("act", a)],
                                      rdy("ind2")) if tt == 0 else ()
                            vlast = emit("pe", lambda e, tt=tt, ci=ci,
                                         qsq=qsq: e.matmul(
                                             bk(6, ci, 2),
                                             qsq[:, tt * 128:
                                                 (tt + 1) * 128],
                                             V["ind2"], start=True,
                                             stop=True),
                                         inc=(tt == NT - 1), waits=w)
                        trk("qsq").read([("pe", vlast)])
                trk("pb6").write([("pe", vlast)])
                _emit_bound_finalize(256, g)

            def phase_b(g):
                for i in range(4):
                    trk(f"att{i}").read(merge(trk("qsq").free,
                                              trk("htm2").free,
                                              trk("yfT").free))
                tcol_used = {}
                STA = (0, 1, 2)
                STB = (3, 4, 7)
                for qc in range(4):
                    for hp in range(4):
                        heads = (2 * hp, 2 * hp + 1)
                        attv_last = []

                        def emit_st(kt):
                            out = []
                            for hi, hh in enumerate(heads):
                                base = (hh % 2) * 64
                                dch = hh // 2
                                bn = (STA, STB)[hi][kt % 3]
                                bt = trk(f"pb{bn}")
                                stv = emit("pe", lambda e, dch=dch, bn=bn,
                                           base=base, kt=kt, qc=qc:
                                           e.matmul(
                                               bk(bn),
                                               kq[base:base + 64,
                                                  dch * 2048 + kt * 128:
                                                  dch * 2048 + (kt + 1)
                                                  * 128],
                                               qq[base:base + 64,
                                                  dch * 2048 + qc * 512:
                                                  dch * 2048 + (qc + 1)
                                                  * 512],
                                               start=True, stop=True),
                                           inc=True,
                                           waits=merge(bt.free,
                                                       rdy("K", "Q")))
                                bt.write([("pe", stv)])
                                out.append(bn)
                            return out

                        st_bn = emit_st(0)
                        for kt in range(NT):
                            evs = []
                            for hi, hh in enumerate(heads):
                                bn = st_bn[hi]
                                bt = trk(f"pb{bn}")
                                an = 2 * hi + (kt % 2)
                                abt = trk(f"att{an}")
                                ev = emit("act", lambda e, an=an, bn=bn,
                                          g=g: e.activation(
                                              att[an], bk(bn), AF.Exp,
                                              bias=col("mneg", g),
                                              scale=SCALE), inc=True,
                                          waits=merge(bt.ready, abt.free,
                                                      rdy("mneg")))
                                bt.read([("act", ev)])
                                abt.write([("act", ev)])
                                evs.append(an)
                            if kt + 1 < NT:
                                st_bn = emit_st(kt + 1)
                            for hi, hh in enumerate(heads):
                                an = evs[hi]
                                abt = trk(f"att{an}")
                                for qt in range(4):
                                    yb = trk(f"pb{5 + hi}")
                                    w = merge(abt.ready, rdy("V"),
                                              yb.free if (kt == 0 and
                                                          qt == 0) else ())
                                    av = emit("pe", lambda e, an=an, kt=kt,
                                              qt=qt, hh=hh, hi=hi:
                                              e.matmul(
                                                  bk(5 + hi, qt * 65, 65),
                                                  att[an][:, qt * 128:
                                                          (qt + 1) * 128],
                                                  vh[:, kt * 520 + hh * 65:
                                                     kt * 520 + (hh + 1)
                                                     * 65],
                                                  start=(kt == 0 and
                                                         qt == 0),
                                                  stop=(kt == NT - 1),
                                                  skip_group_check=True),
                                              inc=(kt == NT - 1), waits=w)
                                    if kt == NT - 1:
                                        attv_last.append(av)
                                    abt.read([("pe", av)])
                        for hi in range(2):
                            trk(f"pb{5 + hi}").write(
                                [("pe", attv_last[-1])])
                        for hi, hh in enumerate(heads):
                            yb = trk(f"pb{5 + hi}")
                            for qt in range(4):
                                tc = 8 + hi * 4 + qt
                                wt = merge(yb.ready,
                                           [("dve", tcol_used[tc])]
                                           if tc in tcol_used else ())
                                rv_ = emit("dve", lambda e, qt=qt, hi=hi,
                                           tc=tc: e.reciprocal(
                                               V["tcol"][:, tc:tc + 1],
                                               bk(5 + hi, qt * 65 + 64,
                                                  1)), inc=True, waits=wt)
                                yb.read([("dve", rv_)])
                                hgl = g * 8 + hh
                                gt = qc * 4 + qt
                                ev = emit("dve", lambda e, qt=qt, hi=hi,
                                          hgl=hgl, gt=gt, tc=tc:
                                          e.tensor_scalar(
                                              yf[:, gt * 1024 + hgl * 64:
                                                 gt * 1024 + (hgl + 1)
                                                 * 64],
                                              bk(5 + hi, qt * 65, 64),
                                              V["tcol"][:, tc:tc + 1],
                                              None, ALU.mult), inc=True,
                                          waits=merge([("dve", rv_)],
                                                      trk("yfull").free))
                                yb.read([("dve", ev)])
                                tcol_used[tc] = ev
                        trk("yfull").ready = [("dve", P.cnt["dve"])]
                pev = [("pe", P.cnt["pe"])]
                for nm in ("Q", "K", "V", "wq3", "wk3", "wv3"):
                    trk(nm).read(pev)

            stage_grp(0)
            phase_a(0)
            stage_grp(1)
            if not is_aa and FSUB <= 1:
                return
            phase_b(0)
            phase_a(1)
            build_wof()
            if not is_aa and FSUB <= 2:
                return
            phase_b(1)
            if not is_aa and FSUB <= 3:
                return
            trk("yfT").read(merge(trk("att0").free, trk("att1").free,
                                  trk("att2").free, trk("att3").free,
                                  trk("qsq").free, trk("htm2").free))
            xadds = []
            for tt in range(NT):
                tb = (1, 0)[tt % 2]
                bt = trk(f"pb{tb}")
                v = None
                for ch in range(NCH):
                    w = merge(bt.free, trk("yfull").ready, rdy("iden")) \
                        if ch == 0 else ()
                    v = emit("pe", lambda e, tt=tt, ch=ch, tb=tb:
                             e.transpose(
                                 bk(tb, ch * 128, 128, BF16),
                                 yf[:, tt * 1024 + ch * 128:
                                    tt * 1024 + (ch + 1) * 128],
                                 V["iden"]), inc=(ch == NCH - 1), waits=w)
                bt.write([("pe", v)])
                trk("yfull").read([("pe", v)])
                ev = emit("act", lambda e, yfT=yfT, tb=tb: e.activation(
                    yfT, bk(tb, 0, 1024, BF16), AF.Copy), inc=True,
                    waits=merge(bt.ready, trk("yfT").free))
                bt.read([("act", ev)])
                trk("yfT").write([("act", ev)])
                for chh in range(2):
                    bn = 2 + chh
                    parts = [(yfT[:, cch * 128:(cch + 1) * 128],
                              wof[:, cch * 1024 + chh * 512:
                                  cch * 1024 + (chh + 1) * 512])
                             for cch in range(NCH)]
                    if is_aa:
                        parts.append((row(0, 128),
                                      row(896 + chh * 512, 512)))
                    mm_group(bn, bk(bn), parts,
                             extra_waits=merge(trk("yfT").ready,
                                               rdy("wof"), rdy("rows"),
                                               rdy("rowscr") if is_aa
                                               else ()),
                             reads=(("yfT", "wof", "rowscr") if is_aa
                                    else ("yfT", "wof")))
                    bt2 = trk(f"pb{bn}")
                    xv = emit("dve", lambda e, tt=tt, chh=chh, bn=bn:
                              e.tensor_tensor(
                                  V[tgt][:, tt * C + chh * 512:
                                         tt * C + (chh + 1) * 512],
                                  bk(bn),
                                  V[tgt][:, tt * C + chh * 512:
                                         tt * C + (chh + 1) * 512],
                                  ALU.add), inc=True,
                              waits=merge(bt2.ready, rdy(tgt)))
                    bt2.read([("dve", xv)])
                    xadds.append(xv)
                trk("yfT").read([("pe", P.cnt["pe"])])
            trk(tgt).write([("dve", xadds[-1])])

        def emit_expert(wg, wu, wd, e, cw_col):
            def esl(nm, df=None):
                a = dram[nm]
                if e is not None:
                    a = a[e]
                return a

            dwr = esl(wd).rearrange("(c p) n -> p c n", p=128)
            dview = V["down"].rearrange("p (c n) -> p c n", n=C)
            down_prev_free = list(trk("down").free)
            for df in range(NDF):
                gN, uN = f"gg{df % 2}", f"ug{df % 2}"
                for nm, wsrc in ((gN, wg), (uN, wu)):
                    src = esl(wsrc)[:, df * 128:(df + 1) * 128].rearrange(
                        "(c p) n -> p c n", p=128)
                    dma("gp", nm, V[nm].rearrange("p (c n) -> p c n", n=128),
                        src)
                if 2 <= df < 6:
                    q = df - 2
                    dma("gp", "down", dview[:, q * 4:(q + 1) * 4, :],
                        dwr[:, q * 4:(q + 1) * 4, :],
                        sem="q_down", waits=down_prev_free)
                for tc in range(4):
                    gb, ub = (tc % 2), 2 + (tc % 2)
                    parts = [(V[gN][:, ci * 128:(ci + 1) * 128],
                              V["h2"][:, ci * 2048 + tc * 512:
                                      ci * 2048 + (tc + 1) * 512])
                             for ci in range(NCH)]
                    mm_group(gb, bk(gb), parts,
                             extra_waits=merge(rdy(gN), rdy("h2")),
                             reads=(gN,))
                    parts = [(V[uN][:, ci * 128:(ci + 1) * 128],
                              V["h2"][:, ci * 2048 + tc * 512:
                                      ci * 2048 + (tc + 1) * 512])
                             for ci in range(NCH)]
                    mm_group(ub, bk(ub), parts,
                             extra_waits=merge(rdy(uN), rdy("h2")),
                             reads=(uN,))
                    ssl = V["su"][:, df * 2048 + tc * 512:
                                  df * 2048 + (tc + 1) * 512]
                    gt, ut = trk(f"pb{gb}"), trk(f"pb{ub}")
                    sv = emit("act", lambda e_, ssl=ssl, gb=gb: e_.activation(
                        ssl, bk(gb), AF.Silu), inc=True,
                        waits=merge(gt.ready, trk("su").free))
                    gt.read([("act", sv)])
                    mv = emit("dve", lambda e_, ssl=ssl, ub=ub:
                              e_.tensor_tensor(ssl, bk(ub), ssl, ALU.mult),
                              inc=True, waits=merge(ut.ready, [("act", sv)]))
                    ut.read([("dve", mv)])
                    trk("su").write([("dve", mv)])
            surd = [("dve", P.cnt["dve"])]
            for tt in range(NT):
                for ch in range(2):
                    bn = 4 + (2 * tt + ch) % 4
                    parts = [(V["su"][:, df * 2048 + tt * 128:
                                      df * 2048 + (tt + 1) * 128],
                              V["down"][:, df * 1024 + ch * 512:
                                        df * 1024 + (ch + 1) * 512])
                             for df in range(NDF)]
                    mm_group(bn, bk(bn), parts,
                             extra_waits=merge(surd, rdy("down")),
                             reads=("su", "down"))
                    bt2 = trk(f"pb{bn}")
                    xsl = V["x"][:, tt * C + ch * 512: tt * C + (ch + 1)
                                 * 512]
                    cw = cw_col(tt)
                    if cw is None:
                        xv = emit("dve", lambda e_, xsl=xsl, bn=bn:
                                  e_.tensor_tensor(xsl, bk(bn), xsl,
                                                   ALU.add), inc=True,
                                  waits=merge(bt2.ready, rdy("x")))
                    else:
                        xv = emit("dve", lambda e_, xsl=xsl, bn=bn, cw=cw:
                                  e_.scalar_tensor_tensor(
                                      xsl, bk(bn), cw, xsl, ALU.mult,
                                      ALU.add), inc=True,
                                  waits=merge(bt2.ready, rdy("x"),
                                              rdy("combine")))
                    bt2.read([("dve", xv)])
            trk("x").write([("dve", P.cnt["dve"])])

        def emit_ffn(entry_deps=None):
            lnscr2 = reg("lnscr2", R_B + 61440, 1024, F32)
            emit_ln_stats("x", lnscr2, tile_deps=entry_deps)
            x2a = {}
            for tt in range(NT):
                x2a[tt] = emit("dve", lambda e, tt=tt: e.tensor_scalar(
                    xt("x2", tt), xt("x", tt), col("m_c", tt),
                    col("rstd_c", tt), ALU.subtract, ALU.mult), inc=True,
                    waits=merge(rdy("stats"), rdy("x"), trk("x2").free))
            trk("x2").write([("dve", x2a[NT - 1])])
            ffn_x2a_cell[0] = {t_: [("dve", v_)] for t_, v_ in x2a.items()}
            if SUB <= 1:
                return
            if os.environ.get("MK_NOSPILL", "0") != "1":
                spv = emit("sync", lambda e: e.dma_start(
                    out=xsp_d.rearrange("(t p) c -> p t c", p=128),
                    in_=V["x"].rearrange("p (t c) -> p t c", c=C)),
                    inc=True, waits=rdy("x"), inc_sem="q_xsp", step=16)
                trk("xsp").write([("q_xsp", spv)])
            else:
                spv = None
            xfree = merge([("q_xsp", spv)] if spv else [],
                          [("dve", x2a[NT - 1])])
            for n in ("K", "Q", "V", "hfm", "htm", "htm2", "att0", "att1",
                      "att2", "att3", "qsq", "yfT"):
                trk(n).read(xfree)
            emit_dense_attn("fa")
            if SUB <= 2:
                return
            trk("h2").read(merge(trk("wq3").free, trk("wk3").free,
                                 trk("wv3").free, trk("wof").free,
                                 trk("yfull").free, trk("stg").free,
                                 trk("lnscr").free))
            emit_rms_stats("x2", lnscr2, EPS_RMS)
            w2b_ld = view(GR, 1024, F32)
            dma("sync", "w1b", w2b_ld,
                dram["f_n2"].rearrange("(a c) -> a c", a=1)
                .broadcast_to([128, C]))
            h2ev = None
            htm2 = view(R_X + 61696, 1024, BF16)
            w2b = view(GR, 1024, F32)
            for tt in range(NT):
                ht = trk("htm")
                hv = emit("dve", lambda e, tt=tt: e.scalar_tensor_tensor(
                    htm2, xt("x2", tt), col("sig_c", tt), w2b,
                    ALU.mult, ALU.mult), inc=True,
                    waits=merge(rdy("stats"), rdy("x2"), rdy("w1b"),
                                ht.free))
                trk("w1b").read([("dve", hv)])
                ht.write([("dve", hv)])
                bt = trk("pb1")
                v = None
                for ch in range(NCH):
                    w = merge(bt.free, [("dve", hv)], rdy("iden")) \
                        if ch == 0 else ()
                    v = emit("pe", lambda e, ch=ch: e.transpose(
                        bk(1, ch * 128, 128, BF16),
                        htm2[:, ch * 128:(ch + 1) * 128],
                        V["iden"]), inc=(ch == NCH - 1), waits=w)
                bt.write([("pe", v)])
                ht.read([("pe", v)])
                h23 = V["h2"].rearrange("p (c n) -> p c n", n=2048)
                h2ev = emit("dve", lambda e, tt=tt, h23=h23:
                            e.tensor_copy(
                                h23[:, :, tt * 128:(tt + 1) * 128],
                                bk(1, 0, 1024, BF16).rearrange(
                                    "p (c n) -> p c n", n=128)), inc=True,
                            waits=merge(bt.ready, trk("h2").free if tt == 0
                                        else ()))
                bt.read([("dve", h2ev)])
            trk("h2").write([("dve", h2ev)])
            rscr = reg("rscr", R_X, 512, F32)
            trk("rscr").read(merge(trk("K").free, trk("Q").free,
                                   trk("hfm").free))
            dma("gp", "rw", reg("rw", R_X + 2048, 64, BF16)
                .rearrange("p (c n) -> p c n", n=8),
                dram["f_rw"].rearrange("(c p) n -> p c n", p=128))
            trk("rw").read([])
            bt = trk("pb0")
            lg = None
            for tt in range(NT):
                for ci in range(NCH):
                    w = merge(bt.free, rdy("h2"), rdy("rw")) \
                        if (tt == 0 and ci == 0) else ()
                    lg = emit("pe", lambda e, tt=tt, ci=ci: e.matmul(
                        bk(0, tt * 8, 8),
                        V["h2"][:, ci * 2048 + tt * 128:
                                ci * 2048 + (tt + 1) * 128],
                        V["rw"][:, ci * 8:(ci + 1) * 8],
                        start=(tt == 0 and ci == 0), stop=(ci == NCH - 1),
                        skip_group_check=True),
                        inc=(ci == NCH - 1), waits=w)
            bt.write([("pe", lg)])
            PR, PM, M1B, M2B = 0, 128, 256, 384
            def rsl(base, tt, n=8):
                return V["rscr"][:, base + tt * 8: base + tt * 8 + n]
            lmv, prv, smv, rcv, nmv = {}, {}, {}, {}, {}
            for tt in range(NT):
                lmv[tt] = emit("dve", lambda e, tt=tt: e.reduce_max(
                    col("tcol", tt), bk(0, tt * 8, 8), axis=AX.X),
                    inc=True, waits=bt.ready)
            bt.read([("dve", lmv[NT - 1])])
            for tt in range(NT):
                nmv[tt] = emit("dve", lambda e, tt=tt: e.tensor_scalar(
                    col("var_c", tt), col("tcol", tt), -1.0, None,
                    ALU.mult), inc=True, waits=[("dve", lmv[tt])])
            exv = {}
            for tt in range(NT):
                exv[tt] = emit("act", lambda e, tt=tt: e.activation(
                    rsl(PR, tt), bk(0, tt * 8, 8), AF.Exp,
                    bias=col("var_c", tt)), inc=True,
                    waits=merge([("dve", nmv[tt])], trk("rscr").free,
                                [("dve", lmv[NT - 1])]))
            bt.read([("act", exv[NT - 1])])
            trk("rscr").write([("act", exv[NT - 1])])
            for tt in range(NT):
                smv[tt] = emit("dve", lambda e, tt=tt: e.reduce_sum(
                    col("s_c", tt), rsl(PR, tt), axis=AX.X), inc=True,
                    waits=[("act", exv[tt])])
            for tt in range(NT):
                rcv[tt] = emit("dve", lambda e, tt=tt: e.reciprocal(
                    col("q_c", tt), col("s_c", tt)), inc=True,
                    waits=[("dve", smv[tt])])
            pnv = {}
            for tt in range(NT):
                pnv[tt] = emit("dve", lambda e, tt=tt: e.tensor_scalar(
                    rsl(PR, tt), rsl(PR, tt), col("q_c", tt), None,
                    ALU.mult), inc=True, waits=[("dve", rcv[tt])])
            m1v, k1v, pmv, m2v, k2v = {}, {}, {}, {}, {}
            for tt in range(NT):
                m1v[tt] = emit("dve", lambda e, tt=tt: e.reduce_max(
                    col("m_c", tt), rsl(PR, tt), axis=AX.X), inc=True,
                    waits=[("dve", pnv[tt])])
            for tt in range(NT):
                k1v[tt] = emit("dve", lambda e, tt=tt: e.tensor_scalar(
                    rsl(M1B, tt), rsl(PR, tt), col("m_c", tt), None,
                    ALU.is_equal), inc=True, waits=[("dve", m1v[tt])])
            for tt in range(NT):
                pmv[tt] = emit("dve", lambda e, tt=tt:
                               e.scalar_tensor_tensor(
                                   rsl(PM, tt), rsl(M1B, tt), -10.0,
                                   rsl(PR, tt), ALU.mult, ALU.add),
                               inc=True, waits=[("dve", k1v[tt])])
            for tt in range(NT):
                m2v[tt] = emit("dve", lambda e, tt=tt: e.reduce_max(
                    col("rstd_c", tt), rsl(PM, tt), axis=AX.X), inc=True,
                    waits=[("dve", pmv[tt])])
            for tt in range(NT):
                k2v[tt] = emit("dve", lambda e, tt=tt: e.tensor_scalar(
                    rsl(M2B, tt), rsl(PM, tt), col("rstd_c", tt), None,
                    ALU.is_equal), inc=True, waits=[("dve", m2v[tt])])
            dv_, ev_, o1v, w1v, w2v = {}, {}, {}, {}, {}
            for tt in range(NT):
                dv_[tt] = emit("dve", lambda e, tt=tt: e.tensor_sub(
                    col("sig_c", tt), col("rstd_c", tt), col("m_c", tt)),
                    inc=True, waits=[("dve", m2v[tt])])
            for tt in range(NT):
                ev_[tt] = emit("act", lambda e, tt=tt: e.activation(
                    col("tiny", tt), col("sig_c", tt), AF.Exp),
                    inc=True, waits=[("dve", dv_[tt])])
            for tt in range(NT):
                o1v[tt] = emit("dve", lambda e, tt=tt: e.tensor_scalar(
                    col("tiny2", tt), col("tiny", tt), 1.0, None, ALU.add),
                    inc=True, waits=[("act", ev_[tt])])
            for tt in range(NT):
                w1v[tt] = emit("dve", lambda e, tt=tt: e.reciprocal(
                    col("m_c", tt), col("tiny2", tt)), inc=True,
                    waits=[("dve", o1v[tt])])
            for tt in range(NT):
                w2v[tt] = emit("dve", lambda e, tt=tt: e.tensor_scalar(
                    col("rstd_c", tt), col("m_c", tt), -1.0, 1.0,
                    ALU.mult, ALU.add), inc=True, waits=[("dve", w1v[tt])])
            c1v, c2v = {}, {}
            for tt in range(NT):
                c1v[tt] = emit("dve", lambda e, tt=tt: e.tensor_scalar(
                    V["combine"][:, tt * 8:(tt + 1) * 8], rsl(M1B, tt),
                    col("m_c", tt), None, ALU.mult), inc=True,
                    waits=merge([("dve", w1v[tt]), ("dve", k1v[tt])],
                                trk("combine").free))
            for tt in range(NT):
                c2v[tt] = emit("dve", lambda e, tt=tt:
                               e.scalar_tensor_tensor(
                                   V["combine"][:, tt * 8:(tt + 1) * 8],
                                   rsl(M2B, tt), col("rstd_c", tt),
                                   V["combine"][:, tt * 8:(tt + 1) * 8],
                                   ALU.mult, ALU.add), inc=True,
                               waits=[("dve", w2v[tt]), ("dve", c1v[tt]),
                                      ("dve", k2v[tt])])
            trk("combine").write([("dve", c2v[NT - 1])])
            trk("rscr").read([("dve", c2v[NT - 1])])
            if SUB <= 3:
                return
            for n in ("rscr", "rw", "V", "K", "Q", "hfm", "htm", "htm2",
                      "att0", "att1", "att2", "att3", "qsq", "yfT",
                      "yfull"):
                trk("x").free = merge(trk("x").free, trk(n).free)
            dma("sync", "x", V["x"].rearrange("p (t c) -> p t c", c=C),
                xsp_d.rearrange("(t p) c -> p t c", p=128), waits=rdy("xsp"))
            xadds = []
            for tt in range(NT):
                xv = emit("dve", lambda e, tt=tt: e.tensor_tensor(
                    xt("x", tt), xt("x2", tt), xt("x", tt), ALU.add),
                    inc=True, waits=merge(rdy("x"), rdy("x2"), rdy("h2"),
                                          rdy("combine")))
                xadds.append(xv)
            trk("x").write([("dve", xadds[-1])])
            trk("x2").read([("dve", xadds[-1])])
            trk("su").read(merge(trk("x2").free))
            trk("down").read(merge(trk("yfull").free, trk("stg").free,
                                   trk("stg2").free, trk("lnscr").free))
            for gname in ("gg0", "gg1", "ug0", "ug1"):
                trk(gname).read(trk("w1b").free)
            if SUB <= 4:
                return
            emit_expert("f_sg", "f_su", "f_sd", None, lambda tt: None)
            if SUB <= 5:
                return
            for e in range(NEXP):
                emit_expert("f_eg", "f_eu", "f_ed", e,
                            lambda tt, e=e: V["combine"][:, tt * 8 + e:
                                                         tt * 8 + e + 1])

        def _emit_bound_finalize(ncols, g):
            r1 = emit("dve", lambda e, ncols=ncols: e.reduce_max(
                col("tiny", 0), bk(6, 0, ncols), axis=AX.X), inc=True,
                waits=trk("pb6").ready)
            trk("pb6").read([("dve", r1)])
            r2 = emit("dve", lambda e: e.tensor_copy(
                V["nm_b"][:, 0:1], col("tiny", 0)), inc=True,
                waits=[("dve", r1)])
            tv = emit("pe", lambda e: e.transpose(
                bk(7, 0, 256, BF16)[0:1, 0:128], V["nm_b"][:, 0:1],
                V["iden"]), inc=True,
                waits=merge([("dve", r2)], trk("pb7").free, rdy("iden")))
            trk("pb7").write([("pe", tv)])
            r3 = emit("dve", lambda e: e.reduce_max(
                col("tiny", 1)[0:1, :], bk(7, 0, 128, BF16)[0:1, :],
                axis=AX.X), inc=True, waits=trk("pb7").ready)
            trk("pb7").read([("dve", r3)])
            r4 = emit("dve", lambda e: e.tensor_scalar(
                col("tiny2", 1)[0:1, :], col("tiny", 1)[0:1, :],
                -SCALE, None, ALU.mult), inc=True, waits=[("dve", r3)])
            r5 = emit("dve", lambda e: e.tensor_copy(
                V["nm_b"][0:1, 1:2], col("tiny2", 1)[0:1, :]), inc=True,
                waits=[("dve", r4)])
            bv_ = emit("pe", lambda e: e.matmul(
                bk(7, 128, 1), row(0, 128), V["nm_b"][0:1, 1:2],
                start=True, stop=True), inc=True,
                waits=merge([("dve", r5)], trk("pb7").free, rdy("rows")))
            trk("pb7").write([("pe", bv_)])
            mv_ = emit("dve", lambda e, g=g: e.tensor_copy(
                col("mneg", g), bk(7, 128, 1)), inc=True,
                waits=trk("pb7").ready)
            trk("pb7").read([("dve", mv_)])
            trk("mneg").write([("dve", mv_)])

        def emit_attn_b():
            blat = reg("blat", R_A, 2048, BF16)
            bV = reg("bV", R_A + 4096, 16384, BF16)
            bWv = reg("bWv", R_A + 36864, 8192, BF16)
            bWbT = reg("bWbT", R_A + 53248, 1024, BF16)
            blatsq = reg("blatsq", R_A + 55296, 2048, BF16)
            bhtm = reg("bhtm", R_A + 59392, 1024, BF16)
            bytm = reg("bytm", R_A + 61440, 1024, BF16)
            byfT = reg("byfT", R_A + 63488, 1024, BF16)
            bWo = reg("bWo", R_B, 8192, BF16)
            batts = reg("batts", R_B + 16384, 8192, BF16)
            bhfm = reg("bhfm", R_B + 32768, 4096, BF16)
            bstg = reg("stg", R_B + 40960, 1024, BF16)
            bstg2 = reg("stg2", R_B + 43008, 1024, BF16)
            blnscr = reg("lnscr", R_B + 45056, 1024, F32)

            emit_ln_stats("x", blnscr)

            def stage_T(wsrc, dst_name, dst_slice, ci0):
                dt_ = trk(dst_name)
                done = []
                for cl in range(4):
                    ci = ci0 + cl
                    sname = "stg" if cl % 2 == 0 else "stg2"
                    sb = V[sname]
                    dma("gp", sname, sb,
                        dram[wsrc][ci * 128:(ci + 1) * 128, :])
                    bt = trk("pb0")
                    v = None
                    for j in range(NCH):
                        w = merge(bt.free, rdy(sname), rdy("iden")) \
                            if j == 0 else ()
                        v = emit("pe", lambda e, j=j, sb=sb: e.transpose(
                            bk(0, j * 128, 128, BF16),
                            sb[:, j * 128:(j + 1) * 128], V["iden"]),
                            inc=(j == NCH - 1), waits=w)
                    bt.write([("pe", v)])
                    trk(sname).read([("pe", v)])
                    ev = emit("act", lambda e, ci=ci, dst_slice=dst_slice:
                              e.activation(
                                  dst_slice(ci),
                                  bk(0, 0, 1024, BF16).rearrange(
                                      "p (j n) -> p j n", n=128),
                                  AF.Copy), inc=True,
                              waits=merge(bt.ready, dt_.free))
                    bt.read([("act", ev)])
                    done.append(("act", ev))
                dt_.write(merge(dt_.ready, done))

            dma("gp", "stg", bstg[0:64, :], dram["ab_wb"][:, :])
            bt = trk("pb0")
            v = None
            for ci in range(NCH):
                w = merge(bt.free, rdy("stg"), rdy("iden")) if ci == 0 \
                    else ()
                v = emit("pe", lambda e, ci=ci: e.transpose(
                    bk(0, ci * 64, 64, BF16),
                    bstg[0:64, ci * 128:(ci + 1) * 128],
                    V["iden"][0:64, 0:64]), inc=(ci == NCH - 1), waits=w)
            bt.write([("pe", v)])
            trk("stg").read([("pe", v)])
            dv = None
            for half in range(2):
                dv = emit("act", lambda e, half=half, bWbT=bWbT:
                          e.activation(
                              bWbT.rearrange("p (c n) -> p c n", n=128)[
                                  :, :, half * 64:(half + 1) * 64],
                              bk(0, 0, 512, BF16).rearrange(
                                  "p (c n) -> p c n", n=64),
                              AF.Copy), inc=True,
                          waits=merge(bt.ready, trk("bWbT").free))
                bt.read([("act", dv)])
            trk("bWbT").write([("act", dv)])

            def wvsl(ci, bWv=bWv):
                return bWv.rearrange("p (j n) -> p j n", n=1024)[
                    :, :, ci * 128:(ci + 1) * 128]
            stage_T("ab_wv", "bWv", wvsl, 0)
            stage_T("ab_wv", "bWv", wvsl, 4)
            dma("gp", "rowscr", row(896, 1024),
                dram["ab_vb"].rearrange("(a c) -> a c", a=1))

            stats_w = rdy("stats")
            for tg in range(4):
                for j in range(4):
                    tt = tg * 4 + j
                    ht = trk("bhtm")
                    hv = emit("dve", lambda e, tt=tt, bhtm=bhtm:
                              e.tensor_scalar(
                                  bhtm, xt("x", tt), col("m_c", tt),
                                  col("rstd_c", tt), ALU.subtract,
                                  ALU.mult), inc=True,
                              waits=merge(stats_w, rdy("x"), ht.free))
                    ht.write([("dve", hv)])
                    bt = trk("pb1")
                    v = None
                    for ch in range(NCH):
                        w = merge(bt.free, [("dve", hv)], rdy("iden")) \
                            if ch == 0 else ()
                        v = emit("pe", lambda e, ch=ch, bhtm=bhtm:
                                 e.transpose(
                                     bk(1, ch * 128, 128, BF16),
                                     bhtm[:, ch * 128:(ch + 1) * 128],
                                     V["iden"]), inc=(ch == NCH - 1),
                                 waits=w)
                    bt.write([("pe", v)])
                    ht.read([("pe", v)])
                    hf3 = bhfm.rearrange("p (c n) -> p c n", n=512)
                    ev = emit("act", lambda e, j=j, hf3=hf3:
                              e.activation(
                                  hf3[:, :, j * 128:(j + 1) * 128],
                                  bk(1, 0, 1024, BF16).rearrange(
                                      "p (c n) -> p c n", n=128),
                                  AF.Copy), inc=True,
                              waits=merge(bt.ready, trk("bhfm").free
                                          if j == 0 else ()))
                    bt.read([("act", ev)])
                hrd = [("act", P.cnt["act"])]
                trk("bhfm").write(hrd)
                parts = [(bWbT[:, ci * 128:(ci + 1) * 128],
                          bhfm[:, ci * 512:(ci + 1) * 512])
                         for ci in range(NCH)]
                mm_group(2, bk(2), parts,
                         extra_waits=merge(hrd, rdy("bWbT")),
                         reads=("bhfm", "bWbT"))
                bt2 = trk("pb2")
                lev = emit("act", lambda e, tg=tg, blat=blat: e.activation(
                    blat[:, tg * 512:(tg + 1) * 512], bk(2), AF.Copy),
                    inc=True, waits=merge(bt2.ready, trk("blat").free))
                bt2.read([("act", lev)])
                trk("blat").ready = [("act", lev)]
                for j in range(4):
                    tt = tg * 4 + j
                    for ch in range(2):
                        bn = 4 + ((2 * j + ch) % 2)
                        parts = [(bhfm[:, ci * 512 + j * 128:
                                       ci * 512 + (j + 1) * 128],
                                  bWv[:, ci * 1024 + ch * 512:
                                      ci * 1024 + (ch + 1) * 512])
                                 for ci in range(NCH)]
                        parts.append((row(0, 128),
                                      row(896 + ch * 512, 512)))
                        mm_group(bn, bk(bn), parts,
                                 extra_waits=merge(hrd, rdy("bWv"),
                                                   rdy("rows"),
                                                   rdy("rowscr")),
                                 reads=("bhfm", "bWv", "rowscr"))
                        bt3 = trk(f"pb{bn}")
                        ev = emit("act", lambda e, tt=tt, ch=ch, bn=bn,
                                  bV=bV: e.activation(
                                      bV[:, tt * 1024 + ch * 512:
                                         tt * 1024 + (ch + 1) * 512],
                                      bk(bn), AF.Copy), inc=True,
                                  waits=merge(bt3.ready, trk("bV").free))
                        bt3.read([("act", ev)])
                        trk("bV").ready = [("act", ev)]
            aq = emit("act", lambda e, blatsq=blatsq, blat=blat:
                      e.activation(blatsq[0:64, :], blat[0:64, :],
                                   AF.Square), inc=True,
                      waits=merge(trk("blat").ready, trk("blatsq").free))
            trk("blatsq").write([("act", aq)])
            vlast = None
            for tt in range(NT):
                w = merge(trk("pb6").free, [("act", aq)],
                          rdy("onescol")) if tt == 0 else ()
                vlast = emit("pe", lambda e, tt=tt, blatsq=blatsq:
                             e.matmul(
                                 bk(6, tt, 1),
                                 blatsq[0:64, tt * 128:(tt + 1) * 128],
                                 V["onescol"][0:64, 0:1], start=True,
                                 stop=True),
                             inc=(tt == NT - 1), waits=w)
            trk("blatsq").read([("pe", vlast)])
            trk("pb6").write([("pe", vlast)])
            _emit_bound_finalize(16, 0)

            def wosl(ci, bWo=bWo):
                return bWo.rearrange("p (j n) -> p j n", n=1024)[
                    :, :, ci * 128:(ci + 1) * 128]
            stage_T("ab_wo", "bWo", wosl, 0)
            stage_T("ab_wo", "bWo", wosl, 4)
            dma("gp", "rowscr2", ROWS[0:1, 640:768] if False else
                row(896, 1024), dram["ab_ob"].rearrange("(a c) -> a c",
                                                        a=1),
                waits=trk("rowscr").free)

            xadds = []
            for qc in range(4):
                nkt = 4 * qc + 4
                den_first = True
                for kt in range(nkt):
                    base = 64 * (kt % 2)
                    bn = 1 + (kt % 2)
                    bt = trk(f"pb{bn}")
                    stv = emit("pe", lambda e, kt=kt, qc=qc, base=base,
                               bn=bn, blat=blat: e.matmul(
                                   bk(bn),
                                   blat[base:base + 64,
                                        kt * 128:(kt + 1) * 128],
                                   blat[base:base + 64,
                                        qc * 512:(qc + 1) * 512],
                                   start=True, stop=True), inc=True,
                               waits=merge(bt.free, trk("blat").ready))
                    bt.write([("pe", stv)])
                    if qc * 4 <= kt:
                        ql = kt - qc * 4
                        dvv = emit("dve", lambda e, ql=ql, bn=bn:
                                   e.tensor_tensor(
                                       bk(bn, ql * 128, 128),
                                       bk(bn, ql * 128, 128),
                                       V["diag"], ALU.add), inc=True,
                                   waits=merge(bt.ready, rdy("diag")))
                        bt.write([("pe", stv), ("dve", dvv)])
                    ev = emit("act", lambda e, kt=kt, bn=bn, batts=batts:
                              e.activation(
                                  batts[:, kt * 512:(kt + 1) * 512],
                                  bk(bn), AF.Exp, bias=col("mneg", 0),
                                  scale=SCALE), inc=True,
                              waits=merge(bt.ready, rdy("mneg"),
                                          trk("batts").free
                                          if kt == 0 else ()))
                    bt.read([("act", ev)])
                trk("batts").write([("act", P.cnt["act"])])
                for qlo in range(4):
                    qt = qc * 4 + qlo
                    dlast = None
                    for kt in range(qt + 1):
                        w = merge(trk("pb7").free, trk("batts").ready,
                                  rdy("onescol")) if kt == 0 else ()
                        dlast = emit("pe", lambda e, kt=kt, qlo=qlo,
                                     qt=qt, batts=batts: e.matmul(
                                         bk(7, qt, 1),
                                         batts[:, kt * 512 + qlo * 128:
                                               kt * 512 + (qlo + 1)
                                               * 128],
                                         V["onescol"], start=(kt == 0 and
                                                              den_first),
                                         stop=(kt == qt),
                                         skip_group_check=True),
                                     inc=(kt == qt), waits=w)
                        den_first = False
                    trk("pb7").write([("pe", dlast)])
                    for half in range(2):
                        bn = 3 + half
                        parts = [(batts[:, kt * 512 + qlo * 128:
                                        kt * 512 + (qlo + 1) * 128],
                                  bV[:, kt * 1024 + half * 512:
                                     kt * 1024 + (half + 1) * 512])
                                 for kt in range(qt + 1)]
                        mm_group(bn, bk(bn), parts,
                                 extra_waits=merge(trk("batts").ready,
                                                   rdy("bV")),
                                 reads=("batts", "bV"))
                    rv_ = emit("dve", lambda e, qt=qt: e.reciprocal(
                        col("tiny", 2), bk(7, qt, 1)), inc=True,
                        waits=merge(trk("pb7").ready,
                                    [("act", P.cnt["act"])]))
                    trk("pb7").read([("dve", rv_)])
                    evs = []
                    for half in range(2):
                        bn = 3 + half
                        bt3 = trk(f"pb{bn}")
                        ev = emit("act", lambda e, half=half, bn=bn,
                                  bytm=bytm: e.activation(
                                      bytm[:, half * 512:(half + 1)
                                           * 512],
                                      bk(bn), AF.Copy,
                                      scale=col("tiny", 2)), inc=True,
                                  waits=merge(bt3.ready, [("dve", rv_)],
                                              trk("bytm").free))
                        bt3.read([("act", ev)])
                        evs.append(("act", ev))
                    trk("bytm").write(evs)
                    bt = trk("pb1")
                    v = None
                    for ch in range(NCH):
                        w = merge(bt.free, trk("bytm").ready,
                                  rdy("iden")) if ch == 0 else ()
                        v = emit("pe", lambda e, ch=ch, bytm=bytm:
                                 e.transpose(
                                     bk(1, ch * 128, 128, BF16),
                                     bytm[:, ch * 128:(ch + 1) * 128],
                                     V["iden"]), inc=(ch == NCH - 1),
                                 waits=w)
                    bt.write([("pe", v)])
                    trk("bytm").read([("pe", v)])
                    ev = emit("act", lambda e, byfT=byfT: e.activation(
                        byfT, bk(1, 0, 1024, BF16), AF.Copy), inc=True,
                        waits=merge(bt.ready, trk("byfT").free))
                    bt.read([("act", ev)])
                    trk("byfT").write([("act", ev)])
                    for chh in range(2):
                        bn = 5
                        parts = [(byfT[:, cch * 128:(cch + 1) * 128],
                                  bWo[:, cch * 1024 + chh * 512:
                                      cch * 1024 + (chh + 1) * 512])
                                 for cch in range(NCH)]
                        parts.append((row(0, 128),
                                      row(896 + chh * 512, 512)))
                        mm_group(bn, bk(bn), parts,
                                 extra_waits=merge(trk("byfT").ready,
                                                   rdy("bWo"), rdy("rows"),
                                                   rdy("rowscr2")),
                                 reads=("byfT", "bWo", "rowscr2"))
                        bt2 = trk(f"pb{bn}")
                        xv = emit("dve", lambda e, qt=qt, chh=chh, bn=bn:
                                  e.tensor_tensor(
                                      V["x"][:, qt * C + chh * 512:
                                             qt * C + (chh + 1) * 512],
                                      bk(bn),
                                      V["x"][:, qt * C + chh * 512:
                                             qt * C + (chh + 1) * 512],
                                      ALU.add), inc=True,
                                  waits=merge(bt2.ready, rdy("x")))
                        bt2.read([("dve", xv)])
                        xadds.append(xv)
            trk("x").write([("dve", xadds[-1])])
            abadds_cell[0] = {tt: [("dve", xadds[2 * tt + 1])]
                              for tt in range(NT)}

        aadds = None
        if STAGES >= 1:
            aadds = emit_dense_attn("aa", tile_deps=x_chunk_deps)
            barrier(exclude=("lnscr", "stats", "x"))
        if STAGES >= 2:
            emit_ffn(entry_deps=aadds)
            barrier()
        if STAGES >= 3:
            emit_attn_b()
            barrier(exclude=("lnscr", "stats", "x"))
        if STAGES >= 4:
            emit_ffn(entry_deps=abadds_cell[0])
            barrier()

        if STAGES >= 2 and SUB < 9:
            vdbg = emit("sync", lambda e: e.dma_start(
                out=dbg_d.rearrange("(t p) c -> p t c", p=128),
                in_=V["x2"].rearrange("p (t c) -> p t c", c=C)),
                inc=True, waits=rdy("x2"), inc_sem="q_dbg", step=16)
            P.wait_only("sync", [("q_dbg", vdbg)])
        vst = emit("sync", lambda e: e.dma_start(
            out=out_d.rearrange("(t p) c -> p t c", p=128),
            in_=V["x"].rearrange("p (t c) -> p t c", c=C)),
            inc=True, waits=rdy("x"), inc_sem="q_out", step=16)
        P.wait_only("sync", [(s, c) for s, c in P.cnt.items()
                             if s.startswith("q_") and c > 0])

        with nc.Block() as block:
            @block.tensor
            def _(t):
                for f in P.ops["pe"]:
                    f(t)

            @block.scalar
            def _(s):
                for f in P.ops["act"]:
                    f(s)

            @block.vector
            def _(v):
                for f in P.ops["dve"]:
                    f(v)

            @block.gpsimd
            def _(g):
                for f in P.ops["gp"]:
                    f(g)

            @block.sync
            def _(s):
                for f in P.ops["sync"]:
                    f(s)
    return nc


_NC_CACHE = {}


def kernel(**inputs):
    if "prog" not in _NC_CACHE:
        _NC_CACHE["prog"] = build_program()
    nc = _NC_CACHE["prog"]
    cst = host_consts()
    x = np.ascontiguousarray(inputs["L0_attn_a_x"])
    names = {
        "aa_wq": "L0_attn_a_wq_w", "aa_qb": "L0_attn_a_wq_b",
        "aa_wk": "L0_attn_a_wk_w", "aa_kb": "L0_attn_a_wk_b",
        "aa_wv": "L0_attn_a_wv_w", "aa_vb": "L0_attn_a_wv_b",
        "aa_wo": "L0_attn_a_wo_w", "aa_ob": "L0_attn_a_wo_b",
        "ab_wb": "L0_attn_b_W_basis", "ab_wv": "L0_attn_b_W_v",
        "ab_vb": "L0_attn_b_b_v", "ab_wo": "L0_attn_b_W_o",
        "ab_ob": "L0_attn_b_b_o", "f_n1": "L0_ffn_norm1_w",
        "f_n2": "L0_ffn_norm2_w", "f_wq": "L0_ffn_wq", "f_wk": "L0_ffn_wk",
        "f_wv": "L0_ffn_wv", "f_wo": "L0_ffn_wo",
        "f_rw": "L0_ffn_router_w", "f_sg": "L0_ffn_sh_gate",
        "f_su": "L0_ffn_sh_up", "f_sd": "L0_ffn_sh_down",
        "f_eg": "L0_ffn_e_gate", "f_eu": "L0_ffn_e_up",
        "f_ed": "L0_ffn_e_down",
    }
    base = {k: np.ascontiguousarray(inputs[v]) for k, v in names.items()}
    base.update(cst)
    in_maps = []
    for i in range(8):
        m = dict(base)
        m["x"] = np.ascontiguousarray(x[i])
        in_maps.append(m)
    res = run_bass_kernel_spmd(nc, in_maps, core_ids=list(range(8)))
    return np.stack([np.asarray(r["out"], np.float32)
                     for r in res.results], axis=0)


# version 07
# speedup vs baseline: 1.1865x; 1.1856x over previous
import os
import numpy as np
import ml_dtypes
import concourse.bass as bass
import concourse.mybir as mybir
from concourse.bass_utils import run_bass_kernel_spmd
from contextlib import ExitStack

F32 = mybir.dt.float32
BF16 = mybir.dt.bfloat16
AF = mybir.ActivationFunctionType
ALU = mybir.AluOpType
AX = mybir.AxisListType

T, C, H, DH = 2048, 1024, 16, 64
NT, NCH = 16, 8
DFF, NDF, NEXP = 2048, 16, 8
SCALE = 0.125
EPS_LN, EPS_RMS = 1e-5, 1e-6
NEG_BIG = -1.0e30
STAGES = int(os.environ.get("MK_STAGES", "4"))
SUB = int(os.environ.get("MK_SUB", "9"))
FSUB = float(os.environ.get("MK_FSUB", "9"))
BF = ml_dtypes.bfloat16


def host_consts():
    iden = np.eye(128, dtype=BF)
    ones128 = np.ones((1, 128), dtype=BF)
    ones512 = np.ones((1, 512), dtype=BF)
    sel0 = np.zeros((1, 128), dtype=BF); sel0[0, :64] = 1
    sel1 = np.zeros((1, 128), dtype=BF); sel1[0, 64:] = 1
    ind2 = np.zeros((128, 2), dtype=BF); ind2[:64, 0] = 1; ind2[64:, 1] = 1
    kk, qq = np.meshgrid(np.arange(128), np.arange(128), indexing="ij")
    diag = np.where(qq >= kk, 0.0, NEG_BIG).astype(np.float32)
    onescol = np.ones((128, 1), dtype=BF)
    return {"c_iden": iden, "c_ones128": ones128, "c_ones512": ones512,
            "c_sel0": sel0, "c_sel1": sel1, "c_ind2": ind2, "c_diag": diag,
            "c_onescol": onescol}


class Planner:

    QUEUES = ("sync", "gp")
    ENGINES = ("pe", "act", "dve", "pool") + QUEUES

    def __init__(self, mksem):
        self.ops = {e: [] for e in self.ENGINES}
        self.cnt = {e: 0 for e in self.ENGINES}
        self.last_wait = {e: {} for e in self.ENGINES}
        self.sems = {}
        self.mksem = mksem

    def sem(self, name):
        if name not in self.sems:
            self.sems[name] = self.mksem(name)
            self.cnt.setdefault(name, 0)
        return self.sems[name]

    def emit(self, eng, fn, inc=False, waits=(), inc_sem=None, step=1):
        lw = self.last_wait[eng]
        eff = []
        for s, v in waits:
            if v <= 0 or lw.get(s, -1) >= v:
                continue
            lw[s] = v
            eff.append((s, v))
            self.sem(s)
        tgt = None
        if inc:
            sname = inc_sem or eng
            self.sem(sname)
            self.cnt[sname] = self.cnt.get(sname, 0) + step
            tgt = self.cnt[sname]
        sems = self.sems

        def run(e, eff=tuple(eff), fn=fn, inc=inc, step=step,
                sname=(inc_sem or eng)):
            for s, v in eff:
                e.wait_ge(sems[s], v)
            ins = fn(e)
            if inc:
                ins.then_inc(sems[sname], step)
        self.ops[eng].append(run)
        return tgt if inc else self.cnt.get(eng, 0)

    def wait_only(self, eng, waits):
        self.emit(eng, lambda e: None, waits=waits)

    def now(self):
        return [(s, c) for s, c in self.cnt.items() if c > 0]


class Trk:
    def __init__(self):
        self.ready = []
        self.free = []

    def write(self, ready):
        self.ready = list(ready)
        self.free = []

    def read(self, free):
        for w in free:
            if w not in self.free:
                self.free.append(w)


def merge(*wls):
    out = []
    for wl in wls:
        for w in wl:
            if w not in out:
                out.append(w)
    return out


def build_program():
    nc = bass.Bass("TRN2", target_bir_lowering=False,
                   detect_race_conditions=(
                       os.environ.get("MK_NORACE", "0") != "1"))
    P = None
    dram = {}

    def din(name, shape, dt=F32):
        dram[name] = nc.dram_tensor(name, shape, dt, kind="ExternalInput").ap()

    din("x", [T, C])
    for nm, shp in [("aa_wq", [C, C]), ("aa_wk", [C, C]), ("aa_wv", [C, C]),
                    ("aa_wo", [C, C]), ("ab_wb", [64, C]), ("ab_wv", [C, C]),
                    ("ab_wo", [C, C]), ("f_wq", [C, C]), ("f_wk", [C, C]),
                    ("f_wv", [C, C]), ("f_wo", [C, C]), ("f_rw", [C, NEXP]),
                    ("f_sg", [C, DFF]), ("f_su", [C, DFF]),
                    ("f_sd", [DFF, C]), ("f_eg", [NEXP, C, DFF]),
                    ("f_eu", [NEXP, C, DFF]), ("f_ed", [NEXP, DFF, C])]:
        din(nm, shp)
    for nm in ["aa_qb", "aa_kb", "aa_vb", "aa_ob", "ab_vb", "ab_ob",
               "f_n1", "f_n2"]:
        din(nm, [C])
    din("c_iden", [128, 128], BF16)
    din("c_ones128", [1, 128], BF16)
    din("c_ones512", [1, 512], BF16)
    din("c_sel0", [1, 128], BF16)
    din("c_sel1", [1, 128], BF16)
    din("c_ind2", [128, 2], BF16)
    din("c_diag", [128, 128], F32)
    din("c_onescol", [128, 1], BF16)
    out_d = nc.dram_tensor("out", [T, C], F32, kind="ExternalOutput").ap()
    dbg_d = nc.dram_tensor("dbg", [T, C], F32, kind="ExternalOutput").ap()
    xsp_d = nc.dram_tensor("xspill", [T, C], F32, kind="Internal").ap()

    bufs = {}

    def trk(name):
        if name not in bufs:
            bufs[name] = Trk()
        return bufs[name]

    def barrier(exclude=()):
        g = P.now()
        for nm, t in bufs.items():
            if nm in exclude:
                continue
            t.read(g)

    ctx = ExitStack()
    with ctx:
        AR_BYTES = 208128
        AR = ctx.enter_context(nc.sbuf_tensor("AR", [128, AR_BYTES // 4],
                                              F32))
        R_X, R_A, R_B, R_Z = 0, 65536, 131072, 206080

        def view(off_b, elems, dt=F32):
            assert off_b % 4 == 0, off_b
            n32 = elems if dt == F32 else (elems + 1) // 2
            assert off_b // 4 + n32 <= AR_BYTES // 4, (off_b, elems)
            a = AR[:, off_b // 4: off_b // 4 + n32]
            if dt != F32:
                a = a.bitcast(dt)[:, 0:elems]
            return a

        V = {}

        def reg(name, off_b, elems, dt=F32):
            V[name] = view(off_b, elems, dt)
            trk(name)
            return V[name]

        reg("x", R_X, 16384, F32)
        reg("x2", R_A, 16384, F32)
        reg("su", R_A, 32768, BF16)
        reg("h2", R_B, 16384, BF16)
        reg("down", R_B + 32768, 16384, BF16)
        GR = R_B + 65536
        for i in range(2):
            reg(f"gg{i}", GR + i * 2048, 1024, BF16)
            reg(f"ug{i}", GR + 4096 + i * 2048, 1024, BF16)
        reg("m_c", R_Z, 16)
        reg("rstd_c", R_Z + 64, 16)
        reg("sig_c", R_Z + 128, 16)
        reg("s_c", R_Z + 192, 16)
        reg("q_c", R_Z + 256, 16)
        reg("var_c", R_Z + 320, 16)
        reg("tiny", R_Z + 384, 16)
        reg("tiny2", R_Z + 448, 16)
        reg("combine", R_Z + 512, 128)
        reg("mneg", R_Z + 1024, 4)
        reg("nm_b", R_Z + 1040, 8, BF16)
        reg("mask01", R_Z + 1056, 16)
        reg("iden", R_Z + 1120, 128, BF16)
        reg("ind2", R_Z + 1376, 2, BF16)
        reg("onescol", R_Z + 1380, 1, BF16)
        reg("diag", R_Z + 1384, 128, F32)
        reg("qbc", R_Z + 1896, 8, F32)
        reg("kbc", R_Z + 1928, 8, F32)
        reg("tcol", R_Z + 1960, 16, F32)
        reg("eps_c", R_Z + 2024, 2, F32)
        assert R_Z + 2032 <= AR_BYTES

        ROWS = ctx.enter_context(nc.sbuf_tensor("ROWS", [1, 2048], BF16))
        trk("rows")
        trk("rowscr")

        def row(off, n):
            return ROWS[0:1, off:off + n]

        pb = [ctx.enter_context(nc.psum_tensor(f"pb{i}", [128, 512], F32))
              for i in range(8)]
        for i in range(8):
            trk(f"pb{i}")

        def bk(i, lo=0, n=512, dt=F32):
            if dt == F32:
                return pb[i][:, lo:lo + n]
            return pb[i][:].bitcast(dt)[:, lo:lo + n]

        P = Planner(lambda name: ctx.enter_context(
            nc.semaphore(f"sm_{name}")))
        for s in ("pe", "act", "dve", "pool"):
            P.sem(s)
        emit = P.emit

        def dma(queue, name, dst, src, waits=(), sem=None):
            t = trk(name)
            sname = sem or f"q_{name}"
            v = emit(queue, lambda e, d=dst, s=src: e.dma_start(out=d, in_=s),
                     inc=True, waits=merge(t.free, waits),
                     inc_sem=sname, step=16)
            t.write([(sname, v)])
            return v

        def rdy(*names):
            return merge(*[trk(n).ready for n in names])

        def mm_group(bnum, bank_ap, parts, extra_waits=(), inc=True,
                     reads=()):
            t = trk(f"pb{bnum}")
            n = len(parts)
            v = None
            for i, (l, r) in enumerate(parts):
                w = merge(t.free, extra_waits) if i == 0 else ()
                v = emit("pe", lambda e, l=l, r=r, i=i, n=n, b=bank_ap:
                         e.matmul(b, l, r, start=(i == 0), stop=(i == n - 1)),
                         inc=(inc and i == n - 1), waits=w)
            t.write([("pe", v)])
            for nm in reads:
                trk(nm).read([("pe", v)])
            return v

        xsb4 = V["x"].rearrange("p (t c) -> p t c", c=C)
        xhb4 = dram["x"].rearrange("(t p) c -> p t c", p=128)
        for ch_ in range(4):
            xv_ = emit("sync", lambda e, c=ch_: e.dma_start(
                out=xsb4[:, c * 4:(c + 1) * 4], in_=xhb4[:, c * 4:(c + 1) * 4]),
                inc=True, inc_sem="q_x", step=16)
        trk("x").write([("q_x", xv_)])
        x_chunk_deps = {tt: [("q_x", 16 * (tt // 4 + 1))] for tt in range(16)}
        dma("sync", "iden", V["iden"], dram["c_iden"])
        dma("sync", "ind2", V["ind2"], dram["c_ind2"])
        dma("sync", "diag", V["diag"], dram["c_diag"])
        dma("sync", "onescol", V["onescol"], dram["c_onescol"])
        for dst_, srcn_ in ((row(0, 128), "c_ones128"),
                            (row(128, 512), "c_ones512"),
                            (row(640, 128), "c_sel0"),
                            (row(768, 128), "c_sel1")):
            v0 = emit("sync", lambda e, d=dst_, s=srcn_: e.dma_start(
                out=d, in_=dram[s]), inc=True, inc_sem="q_rows", step=16)
        trk("rows").write([("q_rows", v0)])
        ev_ = emit("dve", lambda e: e.memset(V["eps_c"][:, 0:1], EPS_LN),
                   inc=True)
        ev_ = emit("dve", lambda e: e.memset(V["eps_c"][:, 1:2], EPS_RMS),
                   inc=True)
        trk("eps_c").write([("dve", ev_)])

        def col(name, tt):
            return V[name][:, tt:tt + 1]

        def xt(src, tt):
            return V[src][:, tt * C:(tt + 1) * C]

        def emit_ln_stats(src, scr, tile_deps=None):
            rd = rdy(src) if tile_deps is None else ()
            st = trk("lnscr")
            sv, qv, mv, vv, rv = {}, {}, {}, {}, {}

            def rstd_pair(u):
                lv = emit("act", lambda e, u=u: e.activation(
                    col("tiny2", u), col("var_c", u), AF.Ln,
                    bias=V["eps_c"][:, 0:1]), inc=True,
                    waits=merge([("dve", vv[u])], rdy("eps_c")))
                rv[u] = emit("act", lambda e, u=u: e.activation(
                    col("rstd_c", u), col("tiny2", u), AF.Exp, scale=-0.5),
                    inc=True, waits=[("act", lv)])

            pvd = None
            for tt in range(NT):
                dep = merge(rd, tile_deps[tt] if tile_deps else (),
                            st.free if tt == 0 else ())
                sv[tt] = emit("act", lambda e, tt=tt, s=scr: e.activation(
                    s, xt(src, tt), AF.Identity, accum_out=col("s_c", tt)),
                    inc=True, waits=dep)
                qv[tt] = emit("act", lambda e, tt=tt, s=scr: e.activation(
                    s, xt(src, tt), AF.Square, accum_out=col("q_c", tt)),
                    inc=True, waits=dep)
                mv[tt] = emit("dve", lambda e, tt=tt: e.tensor_scalar(
                    col("m_c", tt), col("s_c", tt), 1.0 / C, None, ALU.mult),
                    inc=True, waits=merge(
                        [("act", sv[tt])],
                        [("dve", pvd)] if pvd else ()))
                tv = emit("dve", lambda e, tt=tt: e.tensor_tensor(
                    col("tiny", tt), col("m_c", tt), col("m_c", tt),
                    ALU.mult), inc=True, waits=[("dve", mv[tt])])
                vv[tt] = emit("dve", lambda e, tt=tt: e.scalar_tensor_tensor(
                    col("var_c", tt), col("q_c", tt), 1.0 / C,
                    col("tiny", tt), ALU.mult, ALU.subtract), inc=True,
                    waits=[("dve", tv), ("act", qv[tt])])
                pvd = vv[tt]
                if tt >= 1:
                    rstd_pair(tt - 1)
            rstd_pair(NT - 1)
            st.write([("act", qv[NT - 1])])
            st.read([("act", qv[NT - 1])])
            trk("stats").write([("act", rv[NT - 1]), ("dve", vv[NT - 1])])
            return {u: [("act", rv[u]), ("dve", mv[u])] for u in range(NT)}

        def emit_rms_stats(src, scr, eps, dstcol="sig_c", tile_deps=None):
            rd = rdy(src) if tile_deps is None else ()
            st = trk("lnscr")
            qv, tv, rv = {}, {}, {}

            def rms_pair(u):
                lv = emit("act", lambda e, u=u: e.activation(
                    col("tiny2", u), col("tiny", u), AF.Ln,
                    bias=V["eps_c"][:, 1:2]), inc=True,
                    waits=merge([("dve", tv[u])], rdy("eps_c")))
                rv[u] = emit("act", lambda e, u=u, d=dstcol: e.activation(
                    col(d, u), col("tiny2", u), AF.Exp, scale=-0.5),
                    inc=True, waits=[("act", lv)])

            for tt in range(NT):
                w = merge(rd, tile_deps[tt] if tile_deps else (),
                          st.free if tt == 0 else ())
                qv[tt] = emit("act", lambda e, tt=tt, s=scr, d=dstcol:
                              e.activation(s, xt(src, tt), AF.Square,
                                           accum_out=col(d, tt)),
                              inc=True, waits=w)
                tv[tt] = emit("dve", lambda e, tt=tt, d=dstcol:
                              e.tensor_scalar(
                                  col("tiny", tt), col(d, tt), 1.0 / C,
                                  None, ALU.mult),
                              inc=True, waits=[("act", qv[tt])])
                if tt >= 1:
                    rms_pair(tt - 1)
            rms_pair(NT - 1)
            st.write([("act", rv[NT - 1])])
            st.read([("act", rv[NT - 1])])
            trk("stats").write([("act", rv[NT - 1])])
            return {u: [("act", rv[u])] for u in range(NT)}

        ffn_x2a_cell = [None]
        abadds_cell = [None]
        def emit_dense_attn(kind, tile_deps=None):
            is_aa = kind == "aa"
            src = tgt = "x" if is_aa else "x2"
            CA = R_A if is_aa else R_X
            O_K, O_Q, O_V = CA, CA + 16384, CA + 32768
            O_HF, O_AT, O_HT = CA + 49408, CA + 57600, CA + 61696
            O_W3, O_YF = R_B, R_B + 24576
            O_STG, O_LN, O_W1 = R_B + 57344, R_B + 61440, GR
            kq = reg("K", O_K, 8192, BF16)
            qq = reg("Q", O_Q, 8192, BF16)
            vh = reg("V", O_V, 8320, BF16)
            hfm = reg("hfm", O_HF, 4096, BF16)
            att = [reg(f"att{i}", O_AT + i * 1024, 512, BF16)
                   for i in range(4)]
            htm = reg("htm", O_HT, 1024, BF16)
            htm2 = reg("htm2", O_AT, 1024, BF16)
            w3 = [reg(n, O_W3 + i * 8192, 4096, BF16)
                  for i, n in enumerate(("wq3", "wk3", "wv3"))]
            wof = reg("wof", O_W3, 8192, BF16)
            yf = reg("yfull", O_YF, 16384, BF16)
            stg = reg("stg", O_STG, 1024, BF16)
            stg2 = reg("stg2", O_STG + 2048, 1024, BF16)
            qsq = reg("qsq", O_AT, 2048, BF16)
            lnscr = reg("lnscr", O_LN, 1024, F32)
            w1b = reg("w1b", O_W1, 1024, F32)
            yfT = reg("yfT", O_AT, 1024, BF16)
            trk("w1b").free = merge(trk("w1b").free, trk("gg0").free,
                                    trk("gg1").free)
            if not is_aa and FSUB <= 0.2:
                return
            if is_aa:
                emit_ln_stats("x", lnscr, tile_deps=tile_deps)
                rd = rdy("x")
                for tt in range(NT):
                    a = emit("act", lambda e, tt=tt: e.activation(
                        col("tiny", tt), V["x"][:, tt * C: tt * C + 1],
                        AF.Sign), inc=True, waits=rd)
                    emit("act", lambda e, tt=tt: e.activation(
                        col("mask01", tt), col("tiny", tt), AF.Square),
                        inc=True, waits=[("act", a)])
                trk("mask01").write([("act", P.cnt["act"])])
                for bn_, dn_ in (("qbc", "aa_qb"), ("kbc", "aa_kb")):
                    t_ = trk(bn_)
                    v_ = emit("sync", lambda e, bn_=bn_, dn_=dn_:
                              e.dma_start(
                                  out=V[bn_],
                                  in_=dram[dn_].rearrange(
                                      "(c p) -> p c", p=128),
                                  allow_slow_non_contiguous=True),
                              inc=True, waits=t_.free,
                              inc_sem=f"q_{bn_}", step=16)
                    t_.write([(f"q_{bn_}", v_)])
            else:
                emit_rms_stats("x2", lnscr, EPS_RMS,
                               tile_deps=ffn_x2a_cell[0])
                dma("sync", "w1b", w1b,
                    dram["f_n1"].rearrange("(a c) -> a c", a=1)
                    .broadcast_to([128, C]))

            wnm = (("aa_wq", "aa_wk", "aa_wv", "aa_wo") if is_aa else
                   ("f_wq", "f_wk", "f_wv", "f_wo"))

            def stage_transpose(wsrc_rows, dst_name, dst3, gc0):
                dt_ = trk(dst_name)
                done = []
                for ci in range(4):
                    sname = "stg" if ci % 2 == 0 else "stg2"
                    sb = V[sname]
                    dma("gp", sname, sb,
                        dram[wsrc_rows][gc0 + ci * 128:
                                        gc0 + (ci + 1) * 128, :])
                    bt = trk("pb0")
                    v = None
                    for j in range(NCH):
                        w = merge(bt.free, rdy(sname), rdy("iden")) \
                            if j == 0 else ()
                        v = emit("pe", lambda e, j=j, sb=sb: e.transpose(
                            bk(0, j * 128, 128, BF16),
                            sb[:, j * 128:(j + 1) * 128], V["iden"]),
                            inc=(j == NCH - 1), waits=w)
                    bt.write([("pe", v)])
                    trk(sname).read([("pe", v)])
                    ev = emit("act", lambda e, ci=ci, dst3=dst3:
                              e.activation(dst3(ci),
                                           bk(0, 0, 1024, BF16), AF.Copy),
                              inc=True, waits=merge(bt.ready, dt_.free))
                    bt.read([("act", ev)])
                    done.append(("act", ev))
                dt_.write(done)

            def stage_grp(g):
                gc0 = g * 512
                if is_aa:
                    for wi, dn in enumerate(("wq3", "wk3", "wv3")):
                        wv3 = w3[wi]

                        def mk(wv3=wv3):
                            return lambda ci: wv3.rearrange(
                                "p (j n) -> p j n", n=512)[
                                    :, :, ci * 128:(ci + 1) * 128]
                        stage_transpose(wnm[wi], dn, mk(), gc0)
                    dma("gp", "rowscr", row(896, 512),
                        dram["aa_vb"].rearrange("(a c) -> a c", a=1)[
                            0:1, gc0:gc0 + 512])
                else:
                    for wi, dn in enumerate(("wq3", "wk3", "wv3")):
                        dma("gp", dn,
                            w3[wi].rearrange("p (c n) -> p c n", n=512),
                            dram[wnm[wi]][:, gc0:gc0 + 512].rearrange(
                                "(c p) n -> p c n", p=128))

            def build_wof():
                trk("wof").free = merge(trk("wof").free, trk("wq3").free,
                                        trk("wk3").free, trk("wv3").free)
                if is_aa:
                    for hlf in range(2):
                        def mk(hlf=hlf):
                            return lambda ci: wof.rearrange(
                                "p (c n) -> p c n", n=1024)[
                                    :, :, hlf * 512 + ci * 128:
                                    hlf * 512 + (ci + 1) * 128]
                        stage_transpose(wnm[3], "wof", mk(), hlf * 512)
                    dma("gp", "rowscr", row(896, 1024),
                        dram["aa_ob"].rearrange("(a c) -> a c", a=1))
                else:
                    dma("gp", "wof",
                        wof.rearrange("p (c n) -> p c n", n=C),
                        dram[wnm[3]].rearrange("(c p) n -> p c n", p=128))

            def phase_a(g):
                gc0 = g * 512
                stats_w = rdy("stats")
                trk("htm2").read(merge(trk("att0").free, trk("att1").free,
                                       trk("qsq").free, trk("yfT").free))
                for tg in range(4):
                    for j in range(4):
                        tt = tg * 4 + j
                        hb = (htm, htm2)[j % 2]
                        hn = ("htm", "htm2")[j % 2]
                        ht = trk(hn)
                        if is_aa:
                            hv = emit("dve", lambda e, tt=tt, hb=hb:
                                      e.tensor_scalar(
                                          hb, xt("x", tt), col("m_c", tt),
                                          col("rstd_c", tt), ALU.subtract,
                                          ALU.mult), inc=True,
                                      waits=merge(stats_w, rdy("x"),
                                                  ht.free))
                        else:
                            hv = emit("dve", lambda e, tt=tt, hb=hb,
                                      w1b=w1b: e.scalar_tensor_tensor(
                                          hb, xt("x2", tt),
                                          col("sig_c", tt), w1b, ALU.mult,
                                          ALU.mult), inc=True,
                                      waits=merge(stats_w,
                                                  rdy("x2", "w1b"),
                                                  ht.free))
                            trk("w1b").read([("dve", hv)])
                        ht.write([("dve", hv)])
                        tb = (1, 0)[j % 2]
                        bt = trk(f"pb{tb}")
                        v = None
                        for ch in range(NCH):
                            w = merge(bt.free, [("dve", hv)], rdy("iden")) \
                                if ch == 0 else ()
                            v = emit("pe", lambda e, ch=ch, hb=hb, tb=tb:
                                     e.transpose(
                                         bk(tb, ch * 128, 128, BF16),
                                         hb[:, ch * 128:(ch + 1) * 128],
                                         V["iden"]),
                                     inc=(ch == NCH - 1), waits=w)
                        bt.write([("pe", v)])
                        ht.read([("pe", v)])
                        hf3 = hfm.rearrange("p (c n) -> p c n", n=512)
                        ev = emit("dve", lambda e, j=j, hf3=hf3, tb=tb:
                                  e.tensor_copy(
                                      hf3[:, :, j * 128:(j + 1) * 128],
                                      bk(tb, 0, 1024, BF16).rearrange(
                                          "p (c n) -> p c n", n=128)),
                                  inc=True,
                                  waits=merge(bt.ready, trk("hfm").free
                                              if j == 0 else ()))
                        bt.read([("dve", ev)])
                    hrd = [("dve", P.cnt["dve"])]
                    trk("hfm").write(hrd)
                    for which, dst, wv3, wname, bco in (
                            ("Q", qq, w3[0], "wq3", "qbc"),
                            ("K", kq, w3[1], "wk3", "kbc")):
                        for dch in range(4):
                            bn = 2 + (dch % 2)
                            parts = [(wv3[:, ci * 512 + dch * 128:
                                          ci * 512 + (dch + 1) * 128],
                                      hfm[:, ci * 512:(ci + 1) * 512])
                                     for ci in range(NCH)]
                            mm_group(bn, bk(bn), parts,
                                     extra_waits=merge(hrd, rdy(wname)),
                                     reads=("hfm", wname))
                            bt2 = trk(f"pb{bn}")
                            dsl = dst[:, dch * 2048 + tg * 512:
                                      dch * 2048 + (tg + 1) * 512]
                            if is_aa:
                                bc = col(bco, g * 4 + dch)
                                ev = emit("dve", lambda e, dsl=dsl, bn=bn,
                                          bc=bc: e.tensor_scalar(
                                              dsl, bk(bn), bc, None,
                                              ALU.add), inc=True,
                                          waits=merge(bt2.ready, rdy(bco),
                                                      trk(which).free))
                                bt2.read([("dve", ev)])
                                trk(which).write([("dve", ev)])
                            else:
                                ev = emit("dve", lambda e, dsl=dsl, bn=bn:
                                          e.tensor_copy(dsl, bk(bn)),
                                          inc=True,
                                          waits=merge(bt2.ready,
                                                      trk(which).free))
                                bt2.read([("dve", ev)])
                                trk(which).write([("dve", ev)])
                    for j in range(4):
                        tt = tg * 4 + j
                        bn = 4 + (j % 2)
                        parts = [(hfm[:, ci * 512 + j * 128:
                                      ci * 512 + (j + 1) * 128],
                                  w3[2][:, ci * 512:(ci + 1) * 512])
                                 for ci in range(NCH)]
                        if is_aa:
                            parts.append((row(0, 128), row(896, 512)))
                        mm_group(bn, bk(bn), parts,
                                 extra_waits=merge(
                                     hrd, rdy("wv3"), rdy("rows"),
                                     rdy("rowscr") if is_aa else ()),
                                 reads=(("hfm", "wv3", "rowscr") if is_aa
                                        else ("hfm", "wv3")))
                        bt2 = trk(f"pb{bn}")
                        vdst = vh[:, tt * 520: tt * 520 + 520].rearrange(
                            "p (h g) -> p h g", g=65)
                        bsrc = bk(bn).rearrange("p (h d) -> p h d", d=64)
                        if is_aa:
                            ev = emit("dve", lambda e, vdst=vdst,
                                      bsrc=bsrc, tt=tt: e.tensor_scalar(
                                          vdst[:, :, 0:64], bsrc,
                                          col("mask01", tt), None,
                                          ALU.mult), inc=True,
                                      waits=merge(bt2.ready,
                                                  rdy("mask01"),
                                                  trk("V").free))
                        else:
                            ev = emit("dve", lambda e, vdst=vdst,
                                      bsrc=bsrc: e.tensor_copy(
                                          vdst[:, :, 0:64], bsrc),
                                      inc=True,
                                      waits=merge(bt2.ready,
                                                  trk("V").free))
                        bt2.read([("dve", ev)])
                        scol = (col("mask01", tt) if is_aa
                                else V["onescol"])
                        dv = emit("dve", lambda e, vdst=vdst, s=scol:
                                  e.tensor_copy(
                                      vdst[:, :, 64:65].rearrange(
                                          "p h o -> p (h o)"),
                                      s.broadcast_to([128, 8])), inc=True,
                                  waits=merge([("dve", ev)],
                                              rdy("mask01") if is_aa
                                              else rdy("onescol")))
                        trk("V").write([("dve", dv)])
                trk("qsq").read(merge(trk("att0").free, trk("att1").free,
                                      trk("htm2").free, trk("yfT").free))
                vlast = None
                for side, srcb, sname in (("q", qq, "Q"), ("k", kq, "K")):
                    for dch in range(4):
                        a = emit("act", lambda e, srcb=srcb, dch=dch,
                                 qsq=qsq: e.activation(
                                     qsq, srcb[:, dch * 2048:
                                               (dch + 1) * 2048],
                                     AF.Square), inc=True,
                                 waits=merge(rdy(sname),
                                             trk("qsq").free))
                        trk("qsq").write([("act", a)])
                        for tt in range(NT):
                            ci = (0 if side == "q" else 128) + \
                                (dch * 16 + tt) * 2
                            w = merge(trk("pb6").free, [("act", a)],
                                      rdy("ind2")) if tt == 0 else ()
                            vlast = emit("pe", lambda e, tt=tt, ci=ci,
                                         qsq=qsq: e.matmul(
                                             bk(6, ci, 2),
                                             qsq[:, tt * 128:
                                                 (tt + 1) * 128],
                                             V["ind2"], start=True,
                                             stop=True),
                                         inc=(tt == NT - 1), waits=w)
                        trk("qsq").read([("pe", vlast)])
                trk("pb6").write([("pe", vlast)])
                _emit_bound_finalize(256, g)

            def phase_b(g):
                for i in range(4):
                    trk(f"att{i}").read(merge(trk("qsq").free,
                                              trk("htm2").free,
                                              trk("yfT").free))
                tcol_used = {}
                STA = (0, 1, 2)
                STB = (3, 4, 7)
                for qc in range(4):
                    for hp in range(4):
                        heads = (2 * hp, 2 * hp + 1)
                        attv_last = []

                        def emit_st(kt):
                            out = []
                            for hi, hh in enumerate(heads):
                                base = (hh % 2) * 64
                                dch = hh // 2
                                bn = (STA, STB)[hi][kt % 3]
                                bt = trk(f"pb{bn}")
                                stv = emit("pe", lambda e, dch=dch, bn=bn,
                                           base=base, kt=kt, qc=qc:
                                           e.matmul(
                                               bk(bn),
                                               kq[base:base + 64,
                                                  dch * 2048 + kt * 128:
                                                  dch * 2048 + (kt + 1)
                                                  * 128],
                                               qq[base:base + 64,
                                                  dch * 2048 + qc * 512:
                                                  dch * 2048 + (qc + 1)
                                                  * 512],
                                               start=True, stop=True),
                                           inc=True,
                                           waits=merge(bt.free,
                                                       rdy("K", "Q")))
                                bt.write([("pe", stv)])
                                out.append(bn)
                            return out

                        st_bn = emit_st(0)
                        for kt in range(NT):
                            evs = []
                            for hi, hh in enumerate(heads):
                                bn = st_bn[hi]
                                bt = trk(f"pb{bn}")
                                an = 2 * hi + (kt % 2)
                                abt = trk(f"att{an}")
                                ev = emit("act", lambda e, an=an, bn=bn,
                                          g=g: e.activation(
                                              att[an], bk(bn), AF.Exp,
                                              bias=col("mneg", g),
                                              scale=SCALE), inc=True,
                                          waits=merge(bt.ready, abt.free,
                                                      rdy("mneg")))
                                bt.read([("act", ev)])
                                abt.write([("act", ev)])
                                evs.append(an)
                            if kt + 1 < NT:
                                st_bn = emit_st(kt + 1)
                            for hi, hh in enumerate(heads):
                                an = evs[hi]
                                abt = trk(f"att{an}")
                                for qt in range(4):
                                    yb = trk(f"pb{5 + hi}")
                                    w = merge(abt.ready, rdy("V"),
                                              yb.free if (kt == 0 and
                                                          qt == 0) else ())
                                    av = emit("pe", lambda e, an=an, kt=kt,
                                              qt=qt, hh=hh, hi=hi:
                                              e.matmul(
                                                  bk(5 + hi, qt * 65, 65),
                                                  att[an][:, qt * 128:
                                                          (qt + 1) * 128],
                                                  vh[:, kt * 520 + hh * 65:
                                                     kt * 520 + (hh + 1)
                                                     * 65],
                                                  start=(kt == 0 and
                                                         qt == 0),
                                                  stop=(kt == NT - 1),
                                                  skip_group_check=True),
                                              inc=(kt == NT - 1), waits=w)
                                    if kt == NT - 1:
                                        attv_last.append(av)
                                    abt.read([("pe", av)])
                        for hi in range(2):
                            trk(f"pb{5 + hi}").write(
                                [("pe", attv_last[-1])])
                        for hi, hh in enumerate(heads):
                            yb = trk(f"pb{5 + hi}")
                            for qt in range(4):
                                tc = 8 + hi * 4 + qt
                                wt = merge(yb.ready,
                                           [("dve", tcol_used[tc])]
                                           if tc in tcol_used else ())
                                rv_ = emit("dve", lambda e, qt=qt, hi=hi,
                                           tc=tc: e.reciprocal(
                                               V["tcol"][:, tc:tc + 1],
                                               bk(5 + hi, qt * 65 + 64,
                                                  1)), inc=True, waits=wt)
                                yb.read([("dve", rv_)])
                                hgl = g * 8 + hh
                                gt = qc * 4 + qt
                                ev = emit("dve", lambda e, qt=qt, hi=hi,
                                          hgl=hgl, gt=gt, tc=tc:
                                          e.tensor_scalar(
                                              yf[:, gt * 1024 + hgl * 64:
                                                 gt * 1024 + (hgl + 1)
                                                 * 64],
                                              bk(5 + hi, qt * 65, 64),
                                              V["tcol"][:, tc:tc + 1],
                                              None, ALU.mult), inc=True,
                                          waits=merge([("dve", rv_)],
                                                      trk("yfull").free))
                                yb.read([("dve", ev)])
                                tcol_used[tc] = ev
                        trk("yfull").ready = [("dve", P.cnt["dve"])]
                pev = [("pe", P.cnt["pe"])]
                for nm in ("Q", "K", "V", "wq3", "wk3", "wv3"):
                    trk(nm).read(pev)

            stage_grp(0)
            phase_a(0)
            stage_grp(1)
            if not is_aa and FSUB <= 1:
                return
            phase_b(0)
            phase_a(1)
            build_wof()
            if not is_aa and FSUB <= 2:
                return
            phase_b(1)
            if not is_aa and FSUB <= 3:
                return
            trk("yfT").read(merge(trk("att0").free, trk("att1").free,
                                  trk("att2").free, trk("att3").free,
                                  trk("qsq").free, trk("htm2").free))
            xadds = []
            for tt in range(NT):
                tb = (1, 0)[tt % 2]
                bt = trk(f"pb{tb}")
                v = None
                for ch in range(NCH):
                    w = merge(bt.free, trk("yfull").ready, rdy("iden")) \
                        if ch == 0 else ()
                    v = emit("pe", lambda e, tt=tt, ch=ch, tb=tb:
                             e.transpose(
                                 bk(tb, ch * 128, 128, BF16),
                                 yf[:, tt * 1024 + ch * 128:
                                    tt * 1024 + (ch + 1) * 128],
                                 V["iden"]), inc=(ch == NCH - 1), waits=w)
                bt.write([("pe", v)])
                trk("yfull").read([("pe", v)])
                ev = emit("act", lambda e, yfT=yfT, tb=tb: e.activation(
                    yfT, bk(tb, 0, 1024, BF16), AF.Copy), inc=True,
                    waits=merge(bt.ready, trk("yfT").free))
                bt.read([("act", ev)])
                trk("yfT").write([("act", ev)])
                for chh in range(2):
                    bn = 2 + chh
                    parts = [(yfT[:, cch * 128:(cch + 1) * 128],
                              wof[:, cch * 1024 + chh * 512:
                                  cch * 1024 + (chh + 1) * 512])
                             for cch in range(NCH)]
                    if is_aa:
                        parts.append((row(0, 128),
                                      row(896 + chh * 512, 512)))
                    mm_group(bn, bk(bn), parts,
                             extra_waits=merge(trk("yfT").ready,
                                               rdy("wof"), rdy("rows"),
                                               rdy("rowscr") if is_aa
                                               else ()),
                             reads=(("yfT", "wof", "rowscr") if is_aa
                                    else ("yfT", "wof")))
                    bt2 = trk(f"pb{bn}")
                    xv = emit("dve", lambda e, tt=tt, chh=chh, bn=bn:
                              e.tensor_tensor(
                                  V[tgt][:, tt * C + chh * 512:
                                         tt * C + (chh + 1) * 512],
                                  bk(bn),
                                  V[tgt][:, tt * C + chh * 512:
                                         tt * C + (chh + 1) * 512],
                                  ALU.add), inc=True,
                              waits=merge(bt2.ready, rdy(tgt)))
                    bt2.read([("dve", xv)])
                    xadds.append(xv)
                trk("yfT").read([("pe", P.cnt["pe"])])
            trk(tgt).write([("dve", xadds[-1])])

        def emit_expert(wg, wu, wd, e, cw_col):
            def esl(nm, df=None):
                a = dram[nm]
                if e is not None:
                    a = a[e]
                return a

            dwr = esl(wd).rearrange("(c p) n -> p c n", p=128)
            dview = V["down"].rearrange("p (c n) -> p c n", n=C)
            down_prev_free = list(trk("down").free)
            for df in range(NDF):
                gN, uN = f"gg{df % 2}", f"ug{df % 2}"
                for nm, wsrc in ((gN, wg), (uN, wu)):
                    src = esl(wsrc)[:, df * 128:(df + 1) * 128].rearrange(
                        "(c p) n -> p c n", p=128)
                    dma("gp", nm, V[nm].rearrange("p (c n) -> p c n", n=128),
                        src)
                if 2 <= df < 6:
                    q = df - 2
                    dma("gp", "down", dview[:, q * 4:(q + 1) * 4, :],
                        dwr[:, q * 4:(q + 1) * 4, :],
                        sem="q_down", waits=down_prev_free)
                for tc in range(4):
                    gb, ub = (tc % 2), 2 + (tc % 2)
                    parts = [(V[gN][:, ci * 128:(ci + 1) * 128],
                              V["h2"][:, ci * 2048 + tc * 512:
                                      ci * 2048 + (tc + 1) * 512])
                             for ci in range(NCH)]
                    mm_group(gb, bk(gb), parts,
                             extra_waits=merge(rdy(gN), rdy("h2")),
                             reads=(gN,))
                    parts = [(V[uN][:, ci * 128:(ci + 1) * 128],
                              V["h2"][:, ci * 2048 + tc * 512:
                                      ci * 2048 + (tc + 1) * 512])
                             for ci in range(NCH)]
                    mm_group(ub, bk(ub), parts,
                             extra_waits=merge(rdy(uN), rdy("h2")),
                             reads=(uN,))
                    ssl = V["su"][:, df * 2048 + tc * 512:
                                  df * 2048 + (tc + 1) * 512]
                    gt, ut = trk(f"pb{gb}"), trk(f"pb{ub}")
                    sv = emit("act", lambda e_, ssl=ssl, gb=gb: e_.activation(
                        ssl, bk(gb), AF.Silu), inc=True,
                        waits=merge(gt.ready, trk("su").free))
                    gt.read([("act", sv)])
                    mv = emit("dve", lambda e_, ssl=ssl, ub=ub:
                              e_.tensor_tensor(ssl, bk(ub), ssl, ALU.mult),
                              inc=True, waits=merge(ut.ready, [("act", sv)]))
                    ut.read([("dve", mv)])
                    trk("su").write([("dve", mv)])
            surd = [("dve", P.cnt["dve"])]
            exadds = {}
            for tt in range(NT):
                for ch in range(2):
                    bn = 4 + (2 * tt + ch) % 4
                    parts = [(V["su"][:, df * 2048 + tt * 128:
                                      df * 2048 + (tt + 1) * 128],
                              V["down"][:, df * 1024 + ch * 512:
                                        df * 1024 + (ch + 1) * 512])
                             for df in range(NDF)]
                    mm_group(bn, bk(bn), parts,
                             extra_waits=merge(surd, rdy("down")),
                             reads=("su", "down"))
                    bt2 = trk(f"pb{bn}")
                    xsl = V["x"][:, tt * C + ch * 512: tt * C + (ch + 1)
                                 * 512]
                    cw = cw_col(tt)
                    if cw is None:
                        xv = emit("dve", lambda e_, xsl=xsl, bn=bn:
                                  e_.tensor_tensor(xsl, bk(bn), xsl,
                                                   ALU.add), inc=True,
                                  waits=merge(bt2.ready, rdy("x")))
                    else:
                        xv = emit("dve", lambda e_, xsl=xsl, bn=bn, cw=cw:
                                  e_.scalar_tensor_tensor(
                                      xsl, bk(bn), cw, xsl, ALU.mult,
                                      ALU.add), inc=True,
                                  waits=merge(bt2.ready, rdy("x"),
                                              rdy("combine")))
                    bt2.read([("dve", xv)])
                    if ch == 1:
                        exadds[tt] = xv
            trk("x").write([("dve", P.cnt["dve"])])
            return exadds

        def emit_ffn(entry_deps=None):
            lnscr2 = reg("lnscr2", R_B + 61440, 1024, F32)
            ln_ret = emit_ln_stats("x", lnscr2, tile_deps=entry_deps)
            x2a = {}
            for tt in range(NT):
                x2a[tt] = emit("dve", lambda e, tt=tt: e.tensor_scalar(
                    xt("x2", tt), xt("x", tt), col("m_c", tt),
                    col("rstd_c", tt), ALU.subtract, ALU.mult), inc=True,
                    waits=merge(ln_ret[tt], rdy("x"), trk("x2").free))
            trk("x2").write([("dve", x2a[NT - 1])])
            ffn_x2a_cell[0] = {t_: [("dve", v_)] for t_, v_ in x2a.items()}
            if SUB <= 1:
                return
            if os.environ.get("MK_NOSPILL", "0") != "1":
                spv = emit("sync", lambda e: e.dma_start(
                    out=xsp_d.rearrange("(t p) c -> p t c", p=128),
                    in_=V["x"].rearrange("p (t c) -> p t c", c=C)),
                    inc=True, waits=rdy("x"), inc_sem="q_xsp", step=16)
                trk("xsp").write([("q_xsp", spv)])
            else:
                spv = None
            xfree = merge([("q_xsp", spv)] if spv else [],
                          [("dve", x2a[NT - 1])])
            for n in ("K", "Q", "V", "hfm", "htm", "htm2", "att0", "att1",
                      "att2", "att3", "qsq", "yfT"):
                trk(n).read(xfree)
            emit_dense_attn("fa")
            if SUB <= 2:
                return
            trk("h2").read(merge(trk("wq3").free, trk("wk3").free,
                                 trk("wv3").free, trk("wof").free,
                                 trk("yfull").free, trk("stg").free,
                                 trk("lnscr").free))
            emit_rms_stats("x2", lnscr2, EPS_RMS)
            w2b_ld = view(GR, 1024, F32)
            dma("sync", "w1b", w2b_ld,
                dram["f_n2"].rearrange("(a c) -> a c", a=1)
                .broadcast_to([128, C]))
            h2ev = None
            htm2 = view(R_X + 61696, 1024, BF16)
            w2b = view(GR, 1024, F32)
            for tt in range(NT):
                ht = trk("htm")
                hv = emit("dve", lambda e, tt=tt: e.scalar_tensor_tensor(
                    htm2, xt("x2", tt), col("sig_c", tt), w2b,
                    ALU.mult, ALU.mult), inc=True,
                    waits=merge(rdy("stats"), rdy("x2"), rdy("w1b"),
                                ht.free))
                trk("w1b").read([("dve", hv)])
                ht.write([("dve", hv)])
                bt = trk("pb1")
                v = None
                for ch in range(NCH):
                    w = merge(bt.free, [("dve", hv)], rdy("iden")) \
                        if ch == 0 else ()
                    v = emit("pe", lambda e, ch=ch: e.transpose(
                        bk(1, ch * 128, 128, BF16),
                        htm2[:, ch * 128:(ch + 1) * 128],
                        V["iden"]), inc=(ch == NCH - 1), waits=w)
                bt.write([("pe", v)])
                ht.read([("pe", v)])
                h23 = V["h2"].rearrange("p (c n) -> p c n", n=2048)
                h2ev = emit("dve", lambda e, tt=tt, h23=h23:
                            e.tensor_copy(
                                h23[:, :, tt * 128:(tt + 1) * 128],
                                bk(1, 0, 1024, BF16).rearrange(
                                    "p (c n) -> p c n", n=128)), inc=True,
                            waits=merge(bt.ready, trk("h2").free if tt == 0
                                        else ()))
                bt.read([("dve", h2ev)])
            trk("h2").write([("dve", h2ev)])
            rscr = reg("rscr", R_X, 512, F32)
            trk("rscr").read(merge(trk("K").free, trk("Q").free,
                                   trk("hfm").free))
            dma("gp", "rw", reg("rw", R_X + 2048, 64, BF16)
                .rearrange("p (c n) -> p c n", n=8),
                dram["f_rw"].rearrange("(c p) n -> p c n", p=128))
            trk("rw").read([])
            bt = trk("pb0")
            lg = None
            for tt in range(NT):
                for ci in range(NCH):
                    w = merge(bt.free, rdy("h2"), rdy("rw")) \
                        if (tt == 0 and ci == 0) else ()
                    lg = emit("pe", lambda e, tt=tt, ci=ci: e.matmul(
                        bk(0, tt * 8, 8),
                        V["h2"][:, ci * 2048 + tt * 128:
                                ci * 2048 + (tt + 1) * 128],
                        V["rw"][:, ci * 8:(ci + 1) * 8],
                        start=(tt == 0 and ci == 0), stop=(ci == NCH - 1),
                        skip_group_check=True),
                        inc=(ci == NCH - 1), waits=w)
            bt.write([("pe", lg)])
            PR, PM, M1B, M2B = 0, 128, 256, 384
            def rsl(base, tt, n=8):
                return V["rscr"][:, base + tt * 8: base + tt * 8 + n]
            lmv, prv, smv, rcv, nmv = {}, {}, {}, {}, {}
            for tt in range(NT):
                lmv[tt] = emit("dve", lambda e, tt=tt: e.reduce_max(
                    col("tcol", tt), bk(0, tt * 8, 8), axis=AX.X),
                    inc=True, waits=bt.ready)
            bt.read([("dve", lmv[NT - 1])])
            for tt in range(NT):
                nmv[tt] = emit("dve", lambda e, tt=tt: e.tensor_scalar(
                    col("var_c", tt), col("tcol", tt), -1.0, None,
                    ALU.mult), inc=True, waits=[("dve", lmv[tt])])
            exv = {}
            for tt in range(NT):
                exv[tt] = emit("act", lambda e, tt=tt: e.activation(
                    rsl(PR, tt), bk(0, tt * 8, 8), AF.Exp,
                    bias=col("var_c", tt)), inc=True,
                    waits=merge([("dve", nmv[tt])], trk("rscr").free,
                                [("dve", lmv[NT - 1])]))
            bt.read([("act", exv[NT - 1])])
            trk("rscr").write([("act", exv[NT - 1])])
            for tt in range(NT):
                smv[tt] = emit("dve", lambda e, tt=tt: e.reduce_sum(
                    col("s_c", tt), rsl(PR, tt), axis=AX.X), inc=True,
                    waits=[("act", exv[tt])])
            for tt in range(NT):
                rcv[tt] = emit("dve", lambda e, tt=tt: e.reciprocal(
                    col("q_c", tt), col("s_c", tt)), inc=True,
                    waits=[("dve", smv[tt])])
            pnv = {}
            for tt in range(NT):
                pnv[tt] = emit("dve", lambda e, tt=tt: e.tensor_scalar(
                    rsl(PR, tt), rsl(PR, tt), col("q_c", tt), None,
                    ALU.mult), inc=True, waits=[("dve", rcv[tt])])
            m1v, k1v, pmv, m2v, k2v = {}, {}, {}, {}, {}
            for tt in range(NT):
                m1v[tt] = emit("dve", lambda e, tt=tt: e.reduce_max(
                    col("m_c", tt), rsl(PR, tt), axis=AX.X), inc=True,
                    waits=[("dve", pnv[tt])])
            for tt in range(NT):
                k1v[tt] = emit("dve", lambda e, tt=tt: e.tensor_scalar(
                    rsl(M1B, tt), rsl(PR, tt), col("m_c", tt), None,
                    ALU.is_equal), inc=True, waits=[("dve", m1v[tt])])
            for tt in range(NT):
                pmv[tt] = emit("dve", lambda e, tt=tt:
                               e.scalar_tensor_tensor(
                                   rsl(PM, tt), rsl(M1B, tt), -10.0,
                                   rsl(PR, tt), ALU.mult, ALU.add),
                               inc=True, waits=[("dve", k1v[tt])])
            for tt in range(NT):
                m2v[tt] = emit("dve", lambda e, tt=tt: e.reduce_max(
                    col("rstd_c", tt), rsl(PM, tt), axis=AX.X), inc=True,
                    waits=[("dve", pmv[tt])])
            for tt in range(NT):
                k2v[tt] = emit("dve", lambda e, tt=tt: e.tensor_scalar(
                    rsl(M2B, tt), rsl(PM, tt), col("rstd_c", tt), None,
                    ALU.is_equal), inc=True, waits=[("dve", m2v[tt])])
            dv_, ev_, o1v, w1v, w2v = {}, {}, {}, {}, {}
            for tt in range(NT):
                dv_[tt] = emit("dve", lambda e, tt=tt: e.tensor_sub(
                    col("sig_c", tt), col("rstd_c", tt), col("m_c", tt)),
                    inc=True, waits=[("dve", m2v[tt])])
            for tt in range(NT):
                ev_[tt] = emit("act", lambda e, tt=tt: e.activation(
                    col("tiny", tt), col("sig_c", tt), AF.Exp),
                    inc=True, waits=[("dve", dv_[tt])])
            for tt in range(NT):
                o1v[tt] = emit("dve", lambda e, tt=tt: e.tensor_scalar(
                    col("tiny2", tt), col("tiny", tt), 1.0, None, ALU.add),
                    inc=True, waits=[("act", ev_[tt])])
            for tt in range(NT):
                w1v[tt] = emit("dve", lambda e, tt=tt: e.reciprocal(
                    col("m_c", tt), col("tiny2", tt)), inc=True,
                    waits=[("dve", o1v[tt])])
            for tt in range(NT):
                w2v[tt] = emit("dve", lambda e, tt=tt: e.tensor_scalar(
                    col("rstd_c", tt), col("m_c", tt), -1.0, 1.0,
                    ALU.mult, ALU.add), inc=True, waits=[("dve", w1v[tt])])
            c1v, c2v = {}, {}
            for tt in range(NT):
                c1v[tt] = emit("dve", lambda e, tt=tt: e.tensor_scalar(
                    V["combine"][:, tt * 8:(tt + 1) * 8], rsl(M1B, tt),
                    col("m_c", tt), None, ALU.mult), inc=True,
                    waits=merge([("dve", w1v[tt]), ("dve", k1v[tt])],
                                trk("combine").free))
            for tt in range(NT):
                c2v[tt] = emit("dve", lambda e, tt=tt:
                               e.scalar_tensor_tensor(
                                   V["combine"][:, tt * 8:(tt + 1) * 8],
                                   rsl(M2B, tt), col("rstd_c", tt),
                                   V["combine"][:, tt * 8:(tt + 1) * 8],
                                   ALU.mult, ALU.add), inc=True,
                               waits=[("dve", w2v[tt]), ("dve", c1v[tt]),
                                      ("dve", k2v[tt])])
            trk("combine").write([("dve", c2v[NT - 1])])
            trk("rscr").read([("dve", c2v[NT - 1])])
            if SUB <= 3:
                return
            for n in ("rscr", "rw", "V", "K", "Q", "hfm", "htm", "htm2",
                      "att0", "att1", "att2", "att3", "qsq", "yfT",
                      "yfull"):
                trk("x").free = merge(trk("x").free, trk(n).free)
            dma("sync", "x", V["x"].rearrange("p (t c) -> p t c", c=C),
                xsp_d.rearrange("(t p) c -> p t c", p=128), waits=rdy("xsp"))
            xadds = []
            for tt in range(NT):
                xv = emit("dve", lambda e, tt=tt: e.tensor_tensor(
                    xt("x", tt), xt("x2", tt), xt("x", tt), ALU.add),
                    inc=True, waits=merge(rdy("x"), rdy("x2"), rdy("h2"),
                                          rdy("combine")))
                xadds.append(xv)
            trk("x").write([("dve", xadds[-1])])
            trk("x2").read([("dve", xadds[-1])])
            trk("su").read(merge(trk("x2").free))
            trk("down").read(merge(trk("yfull").free, trk("stg").free,
                                   trk("stg2").free, trk("lnscr").free))
            for gname in ("gg0", "gg1", "ug0", "ug1"):
                trk(gname).read(trk("w1b").free)
            if SUB <= 4:
                return
            emit_expert("f_sg", "f_su", "f_sd", None, lambda tt: None)
            if SUB <= 5:
                return
            for e in range(NEXP):
                emit_expert("f_eg", "f_eu", "f_ed", e,
                            lambda tt, e=e: V["combine"][:, tt * 8 + e:
                                                         tt * 8 + e + 1])

        def _emit_bound_finalize(ncols, g):
            r1 = emit("dve", lambda e, ncols=ncols: e.reduce_max(
                col("tiny", 0), bk(6, 0, ncols), axis=AX.X), inc=True,
                waits=trk("pb6").ready)
            trk("pb6").read([("dve", r1)])
            r2 = emit("dve", lambda e: e.tensor_copy(
                V["nm_b"][:, 0:1], col("tiny", 0)), inc=True,
                waits=[("dve", r1)])
            tv = emit("pe", lambda e: e.transpose(
                bk(7, 0, 256, BF16)[0:1, 0:128], V["nm_b"][:, 0:1],
                V["iden"]), inc=True,
                waits=merge([("dve", r2)], trk("pb7").free, rdy("iden")))
            trk("pb7").write([("pe", tv)])
            r3 = emit("dve", lambda e: e.reduce_max(
                col("tiny", 1)[0:1, :], bk(7, 0, 128, BF16)[0:1, :],
                axis=AX.X), inc=True, waits=trk("pb7").ready)
            trk("pb7").read([("dve", r3)])
            r4 = emit("dve", lambda e: e.tensor_scalar(
                col("tiny2", 1)[0:1, :], col("tiny", 1)[0:1, :],
                -SCALE, None, ALU.mult), inc=True, waits=[("dve", r3)])
            r5 = emit("dve", lambda e: e.tensor_copy(
                V["nm_b"][0:1, 1:2], col("tiny2", 1)[0:1, :]), inc=True,
                waits=[("dve", r4)])
            bv_ = emit("pe", lambda e: e.matmul(
                bk(7, 128, 1), row(0, 128), V["nm_b"][0:1, 1:2],
                start=True, stop=True), inc=True,
                waits=merge([("dve", r5)], trk("pb7").free, rdy("rows")))
            trk("pb7").write([("pe", bv_)])
            mv_ = emit("dve", lambda e, g=g: e.tensor_copy(
                col("mneg", g), bk(7, 128, 1)), inc=True,
                waits=trk("pb7").ready)
            trk("pb7").read([("dve", mv_)])
            trk("mneg").write([("dve", mv_)])

        def emit_attn_b():
            blat = reg("blat", R_A, 2048, BF16)
            bV = reg("bV", R_A + 4096, 16384, BF16)
            bWv = reg("bWv", R_A + 36864, 8192, BF16)
            bWbT = reg("bWbT", R_A + 53248, 1024, BF16)
            blatsq = reg("blatsq", R_A + 55296, 2048, BF16)
            bhtm = reg("bhtm", R_A + 59392, 1024, BF16)
            bytm = reg("bytm", R_A + 61440, 1024, BF16)
            byfT = reg("byfT", R_A + 63488, 1024, BF16)
            bWo = reg("bWo", R_B, 8192, BF16)
            batts = reg("batts", R_B + 16384, 8192, BF16)
            bhfm = reg("bhfm", R_B + 32768, 4096, BF16)
            bstg = reg("stg", R_B + 40960, 1024, BF16)
            bstg2 = reg("stg2", R_B + 43008, 1024, BF16)
            blnscr = reg("lnscr", R_B + 45056, 1024, F32)

            emit_ln_stats("x", blnscr)

            def stage_T(wsrc, dst_name, dst_slice, ci0):
                dt_ = trk(dst_name)
                done = []
                for cl in range(4):
                    ci = ci0 + cl
                    sname = "stg" if cl % 2 == 0 else "stg2"
                    sb = V[sname]
                    dma("gp", sname, sb,
                        dram[wsrc][ci * 128:(ci + 1) * 128, :])
                    bt = trk("pb0")
                    v = None
                    for j in range(NCH):
                        w = merge(bt.free, rdy(sname), rdy("iden")) \
                            if j == 0 else ()
                        v = emit("pe", lambda e, j=j, sb=sb: e.transpose(
                            bk(0, j * 128, 128, BF16),
                            sb[:, j * 128:(j + 1) * 128], V["iden"]),
                            inc=(j == NCH - 1), waits=w)
                    bt.write([("pe", v)])
                    trk(sname).read([("pe", v)])
                    ev = emit("act", lambda e, ci=ci, dst_slice=dst_slice:
                              e.activation(
                                  dst_slice(ci),
                                  bk(0, 0, 1024, BF16).rearrange(
                                      "p (j n) -> p j n", n=128),
                                  AF.Copy), inc=True,
                              waits=merge(bt.ready, dt_.free))
                    bt.read([("act", ev)])
                    done.append(("act", ev))
                dt_.write(merge(dt_.ready, done))

            dma("gp", "stg", bstg[0:64, :], dram["ab_wb"][:, :])
            bt = trk("pb0")
            v = None
            for ci in range(NCH):
                w = merge(bt.free, rdy("stg"), rdy("iden")) if ci == 0 \
                    else ()
                v = emit("pe", lambda e, ci=ci: e.transpose(
                    bk(0, ci * 64, 64, BF16),
                    bstg[0:64, ci * 128:(ci + 1) * 128],
                    V["iden"][0:64, 0:64]), inc=(ci == NCH - 1), waits=w)
            bt.write([("pe", v)])
            trk("stg").read([("pe", v)])
            dv = None
            for half in range(2):
                dv = emit("act", lambda e, half=half, bWbT=bWbT:
                          e.activation(
                              bWbT.rearrange("p (c n) -> p c n", n=128)[
                                  :, :, half * 64:(half + 1) * 64],
                              bk(0, 0, 512, BF16).rearrange(
                                  "p (c n) -> p c n", n=64),
                              AF.Copy), inc=True,
                          waits=merge(bt.ready, trk("bWbT").free))
                bt.read([("act", dv)])
            trk("bWbT").write([("act", dv)])

            def wvsl(ci, bWv=bWv):
                return bWv.rearrange("p (j n) -> p j n", n=1024)[
                    :, :, ci * 128:(ci + 1) * 128]
            stage_T("ab_wv", "bWv", wvsl, 0)
            stage_T("ab_wv", "bWv", wvsl, 4)
            dma("gp", "rowscr", row(896, 1024),
                dram["ab_vb"].rearrange("(a c) -> a c", a=1))

            stats_w = rdy("stats")
            for tg in range(4):
                for j in range(4):
                    tt = tg * 4 + j
                    ht = trk("bhtm")
                    hv = emit("dve", lambda e, tt=tt, bhtm=bhtm:
                              e.tensor_scalar(
                                  bhtm, xt("x", tt), col("m_c", tt),
                                  col("rstd_c", tt), ALU.subtract,
                                  ALU.mult), inc=True,
                              waits=merge(stats_w, rdy("x"), ht.free))
                    ht.write([("dve", hv)])
                    bt = trk("pb1")
                    v = None
                    for ch in range(NCH):
                        w = merge(bt.free, [("dve", hv)], rdy("iden")) \
                            if ch == 0 else ()
                        v = emit("pe", lambda e, ch=ch, bhtm=bhtm:
                                 e.transpose(
                                     bk(1, ch * 128, 128, BF16),
                                     bhtm[:, ch * 128:(ch + 1) * 128],
                                     V["iden"]), inc=(ch == NCH - 1),
                                 waits=w)
                    bt.write([("pe", v)])
                    ht.read([("pe", v)])
                    hf3 = bhfm.rearrange("p (c n) -> p c n", n=512)
                    ev = emit("act", lambda e, j=j, hf3=hf3:
                              e.activation(
                                  hf3[:, :, j * 128:(j + 1) * 128],
                                  bk(1, 0, 1024, BF16).rearrange(
                                      "p (c n) -> p c n", n=128),
                                  AF.Copy), inc=True,
                              waits=merge(bt.ready, trk("bhfm").free
                                          if j == 0 else ()))
                    bt.read([("act", ev)])
                hrd = [("act", P.cnt["act"])]
                trk("bhfm").write(hrd)
                parts = [(bWbT[:, ci * 128:(ci + 1) * 128],
                          bhfm[:, ci * 512:(ci + 1) * 512])
                         for ci in range(NCH)]
                mm_group(2, bk(2), parts,
                         extra_waits=merge(hrd, rdy("bWbT")),
                         reads=("bhfm", "bWbT"))
                bt2 = trk("pb2")
                lev = emit("act", lambda e, tg=tg, blat=blat: e.activation(
                    blat[:, tg * 512:(tg + 1) * 512], bk(2), AF.Copy),
                    inc=True, waits=merge(bt2.ready, trk("blat").free))
                bt2.read([("act", lev)])
                trk("blat").ready = [("act", lev)]
                for j in range(4):
                    tt = tg * 4 + j
                    for ch in range(2):
                        bn = 4 + ((2 * j + ch) % 2)
                        parts = [(bhfm[:, ci * 512 + j * 128:
                                       ci * 512 + (j + 1) * 128],
                                  bWv[:, ci * 1024 + ch * 512:
                                      ci * 1024 + (ch + 1) * 512])
                                 for ci in range(NCH)]
                        parts.append((row(0, 128),
                                      row(896 + ch * 512, 512)))
                        mm_group(bn, bk(bn), parts,
                                 extra_waits=merge(hrd, rdy("bWv"),
                                                   rdy("rows"),
                                                   rdy("rowscr")),
                                 reads=("bhfm", "bWv", "rowscr"))
                        bt3 = trk(f"pb{bn}")
                        ev = emit("act", lambda e, tt=tt, ch=ch, bn=bn,
                                  bV=bV: e.activation(
                                      bV[:, tt * 1024 + ch * 512:
                                         tt * 1024 + (ch + 1) * 512],
                                      bk(bn), AF.Copy), inc=True,
                                  waits=merge(bt3.ready, trk("bV").free))
                        bt3.read([("act", ev)])
                        trk("bV").ready = [("act", ev)]
            aq = emit("act", lambda e, blatsq=blatsq, blat=blat:
                      e.activation(blatsq[0:64, :], blat[0:64, :],
                                   AF.Square), inc=True,
                      waits=merge(trk("blat").ready, trk("blatsq").free))
            trk("blatsq").write([("act", aq)])
            vlast = None
            for tt in range(NT):
                w = merge(trk("pb6").free, [("act", aq)],
                          rdy("onescol")) if tt == 0 else ()
                vlast = emit("pe", lambda e, tt=tt, blatsq=blatsq:
                             e.matmul(
                                 bk(6, tt, 1),
                                 blatsq[0:64, tt * 128:(tt + 1) * 128],
                                 V["onescol"][0:64, 0:1], start=True,
                                 stop=True),
                             inc=(tt == NT - 1), waits=w)
            trk("blatsq").read([("pe", vlast)])
            trk("pb6").write([("pe", vlast)])
            _emit_bound_finalize(16, 0)

            def wosl(ci, bWo=bWo):
                return bWo.rearrange("p (j n) -> p j n", n=1024)[
                    :, :, ci * 128:(ci + 1) * 128]
            stage_T("ab_wo", "bWo", wosl, 0)
            stage_T("ab_wo", "bWo", wosl, 4)
            dma("gp", "rowscr2", ROWS[0:1, 640:768] if False else
                row(896, 1024), dram["ab_ob"].rearrange("(a c) -> a c",
                                                        a=1),
                waits=trk("rowscr").free)

            xadds = []
            for qc in range(4):
                nkt = 4 * qc + 4
                den_first = True
                for kt in range(nkt):
                    base = 64 * (kt % 2)
                    bn = 1 + (kt % 2)
                    bt = trk(f"pb{bn}")
                    stv = emit("pe", lambda e, kt=kt, qc=qc, base=base,
                               bn=bn, blat=blat: e.matmul(
                                   bk(bn),
                                   blat[base:base + 64,
                                        kt * 128:(kt + 1) * 128],
                                   blat[base:base + 64,
                                        qc * 512:(qc + 1) * 512],
                                   start=True, stop=True), inc=True,
                               waits=merge(bt.free, trk("blat").ready))
                    bt.write([("pe", stv)])
                    if qc * 4 <= kt:
                        ql = kt - qc * 4
                        dvv = emit("dve", lambda e, ql=ql, bn=bn:
                                   e.tensor_tensor(
                                       bk(bn, ql * 128, 128),
                                       bk(bn, ql * 128, 128),
                                       V["diag"], ALU.add), inc=True,
                                   waits=merge(bt.ready, rdy("diag")))
                        bt.write([("pe", stv), ("dve", dvv)])
                    ev = emit("act", lambda e, kt=kt, bn=bn, batts=batts:
                              e.activation(
                                  batts[:, kt * 512:(kt + 1) * 512],
                                  bk(bn), AF.Exp, bias=col("mneg", 0),
                                  scale=SCALE), inc=True,
                              waits=merge(bt.ready, rdy("mneg"),
                                          trk("batts").free
                                          if kt == 0 else ()))
                    bt.read([("act", ev)])
                trk("batts").write([("act", P.cnt["act"])])
                for qlo in range(4):
                    qt = qc * 4 + qlo
                    dlast = None
                    for kt in range(qt + 1):
                        w = merge(trk("pb7").free, trk("batts").ready,
                                  rdy("onescol")) if kt == 0 else ()
                        dlast = emit("pe", lambda e, kt=kt, qlo=qlo,
                                     qt=qt, batts=batts: e.matmul(
                                         bk(7, qt, 1),
                                         batts[:, kt * 512 + qlo * 128:
                                               kt * 512 + (qlo + 1)
                                               * 128],
                                         V["onescol"], start=(kt == 0 and
                                                              den_first),
                                         stop=(kt == qt),
                                         skip_group_check=True),
                                     inc=(kt == qt), waits=w)
                        den_first = False
                    trk("pb7").write([("pe", dlast)])
                    for half in range(2):
                        bn = 3 + half
                        parts = [(batts[:, kt * 512 + qlo * 128:
                                        kt * 512 + (qlo + 1) * 128],
                                  bV[:, kt * 1024 + half * 512:
                                     kt * 1024 + (half + 1) * 512])
                                 for kt in range(qt + 1)]
                        mm_group(bn, bk(bn), parts,
                                 extra_waits=merge(trk("batts").ready,
                                                   rdy("bV")),
                                 reads=("batts", "bV"))
                    rv_ = emit("dve", lambda e, qt=qt: e.reciprocal(
                        col("tiny", 2), bk(7, qt, 1)), inc=True,
                        waits=merge(trk("pb7").ready,
                                    [("act", P.cnt["act"])]))
                    trk("pb7").read([("dve", rv_)])
                    evs = []
                    for half in range(2):
                        bn = 3 + half
                        bt3 = trk(f"pb{bn}")
                        ev = emit("act", lambda e, half=half, bn=bn,
                                  bytm=bytm: e.activation(
                                      bytm[:, half * 512:(half + 1)
                                           * 512],
                                      bk(bn), AF.Copy,
                                      scale=col("tiny", 2)), inc=True,
                                  waits=merge(bt3.ready, [("dve", rv_)],
                                              trk("bytm").free))
                        bt3.read([("act", ev)])
                        evs.append(("act", ev))
                    trk("bytm").write(evs)
                    bt = trk("pb1")
                    v = None
                    for ch in range(NCH):
                        w = merge(bt.free, trk("bytm").ready,
                                  rdy("iden")) if ch == 0 else ()
                        v = emit("pe", lambda e, ch=ch, bytm=bytm:
                                 e.transpose(
                                     bk(1, ch * 128, 128, BF16),
                                     bytm[:, ch * 128:(ch + 1) * 128],
                                     V["iden"]), inc=(ch == NCH - 1),
                                 waits=w)
                    bt.write([("pe", v)])
                    trk("bytm").read([("pe", v)])
                    ev = emit("act", lambda e, byfT=byfT: e.activation(
                        byfT, bk(1, 0, 1024, BF16), AF.Copy), inc=True,
                        waits=merge(bt.ready, trk("byfT").free))
                    bt.read([("act", ev)])
                    trk("byfT").write([("act", ev)])
                    for chh in range(2):
                        bn = 5
                        parts = [(byfT[:, cch * 128:(cch + 1) * 128],
                                  bWo[:, cch * 1024 + chh * 512:
                                      cch * 1024 + (chh + 1) * 512])
                                 for cch in range(NCH)]
                        parts.append((row(0, 128),
                                      row(896 + chh * 512, 512)))
                        mm_group(bn, bk(bn), parts,
                                 extra_waits=merge(trk("byfT").ready,
                                                   rdy("bWo"), rdy("rows"),
                                                   rdy("rowscr2")),
                                 reads=("byfT", "bWo", "rowscr2"))
                        bt2 = trk(f"pb{bn}")
                        xv = emit("dve", lambda e, qt=qt, chh=chh, bn=bn:
                                  e.tensor_tensor(
                                      V["x"][:, qt * C + chh * 512:
                                             qt * C + (chh + 1) * 512],
                                      bk(bn),
                                      V["x"][:, qt * C + chh * 512:
                                             qt * C + (chh + 1) * 512],
                                      ALU.add), inc=True,
                                  waits=merge(bt2.ready, rdy("x")))
                        bt2.read([("dve", xv)])
                        xadds.append(xv)
            trk("x").write([("dve", xadds[-1])])
            abadds_cell[0] = {tt: [("dve", xadds[2 * tt + 1])]
                              for tt in range(NT)}

        aadds = None
        if STAGES >= 1:
            aadds = emit_dense_attn("aa", tile_deps=x_chunk_deps)
            barrier(exclude=("lnscr", "stats", "x"))
        if STAGES >= 2:
            emit_ffn(entry_deps=aadds)
            barrier()
        if STAGES >= 3:
            emit_attn_b()
            barrier(exclude=("lnscr", "stats", "x"))
        if STAGES >= 4:
            emit_ffn(entry_deps=abadds_cell[0])
            barrier()

        if STAGES >= 2 and SUB < 9:
            vdbg = emit("sync", lambda e: e.dma_start(
                out=dbg_d.rearrange("(t p) c -> p t c", p=128),
                in_=V["x2"].rearrange("p (t c) -> p t c", c=C)),
                inc=True, waits=rdy("x2"), inc_sem="q_dbg", step=16)
            P.wait_only("sync", [("q_dbg", vdbg)])
        vst = emit("sync", lambda e: e.dma_start(
            out=out_d.rearrange("(t p) c -> p t c", p=128),
            in_=V["x"].rearrange("p (t c) -> p t c", c=C)),
            inc=True, waits=rdy("x"), inc_sem="q_out", step=16)
        P.wait_only("sync", [(s, c) for s, c in P.cnt.items()
                             if s.startswith("q_") and c > 0])

        with nc.Block() as block:
            @block.tensor
            def _(t):
                for f in P.ops["pe"]:
                    f(t)

            @block.scalar
            def _(s):
                for f in P.ops["act"]:
                    f(s)

            @block.vector
            def _(v):
                for f in P.ops["dve"]:
                    f(v)

            @block.gpsimd
            def _(g):
                for f in P.ops["gp"]:
                    f(g)

            @block.sync
            def _(s):
                for f in P.ops["sync"]:
                    f(s)
    return nc


_NC_CACHE = {}


def kernel(**inputs):
    if "prog" not in _NC_CACHE:
        _NC_CACHE["prog"] = build_program()
    nc = _NC_CACHE["prog"]
    cst = host_consts()
    x = np.ascontiguousarray(inputs["L0_attn_a_x"])
    names = {
        "aa_wq": "L0_attn_a_wq_w", "aa_qb": "L0_attn_a_wq_b",
        "aa_wk": "L0_attn_a_wk_w", "aa_kb": "L0_attn_a_wk_b",
        "aa_wv": "L0_attn_a_wv_w", "aa_vb": "L0_attn_a_wv_b",
        "aa_wo": "L0_attn_a_wo_w", "aa_ob": "L0_attn_a_wo_b",
        "ab_wb": "L0_attn_b_W_basis", "ab_wv": "L0_attn_b_W_v",
        "ab_vb": "L0_attn_b_b_v", "ab_wo": "L0_attn_b_W_o",
        "ab_ob": "L0_attn_b_b_o", "f_n1": "L0_ffn_norm1_w",
        "f_n2": "L0_ffn_norm2_w", "f_wq": "L0_ffn_wq", "f_wk": "L0_ffn_wk",
        "f_wv": "L0_ffn_wv", "f_wo": "L0_ffn_wo",
        "f_rw": "L0_ffn_router_w", "f_sg": "L0_ffn_sh_gate",
        "f_su": "L0_ffn_sh_up", "f_sd": "L0_ffn_sh_down",
        "f_eg": "L0_ffn_e_gate", "f_eu": "L0_ffn_e_up",
        "f_ed": "L0_ffn_e_down",
    }
    base = {k: np.ascontiguousarray(inputs[v]) for k, v in names.items()}
    base.update(cst)
    in_maps = []
    for i in range(8):
        m = dict(base)
        m["x"] = np.ascontiguousarray(x[i])
        in_maps.append(m)
    res = run_bass_kernel_spmd(nc, in_maps, core_ids=list(range(8)))
    return np.stack([np.asarray(r["out"], np.float32)
                     for r in res.results], axis=0)
